# version 15
# speedup vs baseline: 1.2496x; 1.0228x over previous
import numpy as np
import concourse.bass as bass
import concourse.mybir as mybir
from concourse.bass_utils import run_bass_kernel_spmd

F32 = mybir.dt.float32
BF16 = mybir.dt.bfloat16
I32 = mybir.dt.int32
AF = mybir.ActivationFunctionType
OP = mybir.AluOpType
AXX = mybir.AxisListType.X

T = 4096
D = 1024
TC = 512
NCHK = T // TC
EPS = 1e-5

M_INTER, M_NST, M_HEADS, M_P = 2048, 128, 32, 64
M_CONVD = M_INTER + 2 * M_NST
M_PROJ = M_INTER + M_CONVD + M_HEADS
CH = 128
NSC = T // CH

L_HEADS, L_DH, L_INNER, L_WIN, L_HID = 8, 64, 512, 512, 2730
SQ8 = float(np.sqrt(8.0))
A_HEADS, A_DH = 16, 64


def host_constants():
    c = {}
    c["ident"] = np.eye(128, dtype=np.float32)
    c["ones2d"] = np.ones((128, 128), dtype=np.float32)
    c["slt"] = np.tril(np.ones((128, 128), dtype=np.float32), -1)
    c["utri"] = np.triu(np.ones((128, 128), dtype=np.float32))
    c["keep_jle"] = np.triu(np.ones((128, 128), dtype=np.float32))
    c["keep_ile"] = np.tril(np.ones((128, 128), dtype=np.float32))
    c["odiv2048"] = np.full((128, 128), 1.0 / 2048.0, dtype=np.float32)
    c["odiv1024"] = np.full((128, 128), 1.0 / 1024.0, dtype=np.float32)
    half = np.arange(0, L_DH, 2, dtype=np.float64)
    inv_freq = 1.0 / 10000.0 ** (half / L_DH)
    t = np.arange(T, dtype=np.float64)
    fr = t[:, None] * inv_freq[None, :]
    fr = np.concatenate([fr, fr], axis=1)
    base = (half + 0.4 * L_DH) / (1.4 * L_DH)
    base2 = np.concatenate([base, base])
    il = (t % L_WIN) / (L_WIN // 2)
    qsc = base2[None, :] ** il[:, None]
    ksc = base2[None, :] ** (-il[:, None])
    sign = np.where(np.arange(L_DH) < 32, -1.0, 1.0)[None, :]
    c["rot_qc"] = (np.cos(fr) * qsc).astype(np.float32)
    c["rot_qs"] = (np.sin(fr) * qsc * sign).astype(np.float32)
    c["rot_kc"] = (np.cos(fr) * ksc).astype(np.float32)
    c["rot_ks"] = (np.sin(fr) * ksc * sign).astype(np.float32)
    c["base2_col"] = np.tile(base2 ** 2.0, 2).reshape(128, 1).astype(np.float32)
    ind = np.zeros((1024, 16), dtype=np.float32)
    for d in range(1024):
        ind[d, d // 64] = 1.0
    c["head_ind"] = ind
    return c


CONSTS = host_constants()


class P:

    def __init__(self, nc, dbg):
        self.nc = nc
        self.ctx = []
        self.scopes = []
        self.sem = {}
        self.dram = {}
        self.dbg = set(dbg)
        for name in ["ld", "gld", "st", "mm", "act", "vec", "evac", "aux", "lx0", "lx1"]:
            h = nc.semaphore(name).__enter__()
            self.sem[name] = [h, 0]

    def sbuf(self, name, shape, dt):
        cm = self.nc.sbuf_tensor(name, list(shape), dt)
        v = cm.__enter__()
        self.ctx.append(cm)
        return v

    def psum(self, name, shape, dt):
        cm = self.nc.psum_tensor(name, list(shape), dt)
        v = cm.__enter__()
        self.ctx.append(cm)
        return v

    def push(self):
        self.scopes.append(len(self.ctx))

    def pop(self):
        base = self.scopes.pop()
        for cm in reversed(self.ctx[base:]):
            cm.__exit__(None, None, None)
        del self.ctx[base:]

    def scratch(self, name, shape, dt):
        if name in getattr(self, "inov", ()):
            kind = "ExternalInput"
        else:
            kind = "ExternalOutput" if name in self.dbg else "Internal"
        h = self.nc.dram_tensor(name, list(shape), dt, kind=kind).ap()
        self.dram[name] = h
        return h

    def inc(self, name, call, n=1):
        s = self.sem[name]
        call.then_inc(s[0], n)
        s[1] += n

    def dma(self, eng, name, out, in_):
        s = self.sem[name]
        eng.dma_start(out=out, in_=in_).then_inc(s[0], 16)
        s[1] += 16

    def n(self, name):
        return self.sem[name][1]

    def wait(self, eng, name, val=None):
        s = self.sem[name]
        eng.wait_ge(s[0], s[1] if val is None else val)

    def seq(self, eng, name, call):
        self.inc(name, call)
        self.wait(eng, name)


def ftiles(C):
    return (C + 127) // 128


def frange(i, C):
    return slice(i * 128, min((i + 1) * 128, C))


def fsz(i, C):
    return min(128, C - i * 128)


def ln_pass(p, tag, src, dst, CIN, dst32=None, src_is_bf=False):
    nc = p.nc
    p.push()
    nt = ftiles(CIN)
    x32 = [p.sbuf(f"{tag}_x32_{i}", (128, TC), F32) for i in range(nt)]
    xbf = [p.sbuf(f"{tag}_xbf_{i}", (128, TC), BF16) for i in range(nt)]
    sq = [p.sbuf(f"{tag}_sq_{i}", (128, TC), BF16) for i in range(nt)]
    onesd = p.sbuf(f"{tag}_onesd", (128, 128), BF16)
    musb = p.sbuf(f"{tag}_mu", (128, TC), F32)
    va = p.sbuf(f"{tag}_va", (128, TC), F32)
    s_sb = p.sbuf(f"{tag}_s", (128, TC), F32)
    outb = [p.sbuf(f"{tag}_o_{i}", (128, TC), BF16) for i in range(nt)]
    o32 = [p.sbuf(f"{tag}_o32_{i}", (128, TC), F32) for i in range(nt)] if dst32 is not None else None
    pmu = p.psum(f"{tag}_pmu", (128, TC), F32)
    pex = p.psum(f"{tag}_pex", (128, TC), F32)

    oc = p.dram["odiv2048"] if CIN == 2048 else p.dram["odiv1024"]
    with nc.Block() as blk:
        @blk.gpsimd
        def _(eng):
            p.wait(eng, "st")
            p.dma(eng, "gld", onesd[:], oc[:, :])

    for ci in range(NCHK):
        cs = slice(ci * TC, (ci + 1) * TC)
        with nc.Block() as blk:
            @blk.sync
            def _(eng):
                for i in range(nt):
                    p.dma(eng, "ld", (xbf if src_is_bf else x32)[i][:], src[frange(i, CIN), cs])
            @blk.gpsimd
            def _(eng):
                for i in range(nt):
                    p.dma(eng, "gld", (x32 if src_is_bf else xbf)[i][:], src[frange(i, CIN), cs])
            @blk.scalar
            def _(eng):
                p.wait(eng, "gld")
                p.wait(eng, "ld")
                for i in range(nt):
                    call = eng.activation(sq[i][:], xbf[i][:], AF.Square)
                p.inc("act", call)
            @blk.tensor
            def _(eng):
                p.wait(eng, "ld")
                p.wait(eng, "gld")
                for i in range(nt):
                    eng.matmul(pmu[:], onesd[:], xbf[i][:], start=(i == 0), stop=(i == nt - 1))
                p.wait(eng, "act")
                for i in range(nt):
                    call = eng.matmul(pex[:], onesd[:], sq[i][:], start=(i == 0), stop=(i == nt - 1))
                p.inc("mm", call)
            @blk.vector
            def _(eng):
                p.wait(eng, "mm")
                p.seq(eng, "vec", eng.tensor_copy(musb[:], pmu[:]))
                p.seq(eng, "vec", eng.tensor_tensor(va[:], musb[:], musb[:], OP.mult))
                p.seq(eng, "vec", eng.tensor_tensor(va[:], pex[:], va[:], OP.subtract))
                p.seq(eng, "vec", eng.tensor_scalar(va[:], va[:], EPS, None, OP.add))
                call = eng.reciprocal(s_sb[:], va[:])
                p.inc("vec", call)
            @blk.scalar
            def _(eng):
                p.wait(eng, "vec")
                call = eng.activation(s_sb[:], s_sb[:], AF.Sqrt)
                p.inc("act", call)
            @blk.vector
            def _(eng):
                p.wait(eng, "act")
                p.wait(eng, "st")
                for i in range(nt):
                    call = eng.tensor_tensor(x32[i][:], x32[i][:], musb[:], OP.subtract)
                p.seq(eng, "vec", call)
                for i in range(nt):
                    if o32 is not None:
                        call = eng.tensor_tensor(o32[i][:], x32[i][:], s_sb[:], OP.mult)
                    else:
                        call = eng.tensor_tensor(outb[i][:], x32[i][:], s_sb[:], OP.mult)
                p.seq(eng, "vec", call)
                if o32 is not None:
                    for i in range(nt):
                        call = eng.tensor_copy(outb[i][:], o32[i][:])
                    p.inc("vec", call)
            @blk.gpsimd
            def _(eng):
                p.wait(eng, "vec")
                for i in range(nt):
                    p.dma(eng, "st", dst[frange(i, CIN), cs], outb[i][:])
                    if dst32 is not None:
                        p.dma(eng, "st", dst32[frange(i, CIN), cs], o32[i][:])
    p.pop()


def dense_pass(p, tag, src, CIN, wdram, FOUT, sinks, src_shift=None, src_f32=False):
    nc = p.nc
    assert not src_f32
    p.push()
    nct = ftiles(CIN)
    nft = ftiles(FOUT)
    has_res = any(spec.get("res") is not None for (_, _, spec) in sinks)
    osz = sum((4 if spec.get("dt", BF16) == F32 else 2) * 128 * TC
              for (flo, fhi, spec) in sinks for _ in range(flo // 128, (fhi + 127) // 128))
    rqsz = nft * 128 * TC * 4 if has_res else 0
    xb1 = nct * 128 * (TC + 4) * 2

    def plan(dbo, dbx):
        return (nct * 128 * FOUT * 2 + (2 if dbx else 1) * xb1
                + (2 if dbo else 1) * (osz + rqsz) + rqsz)
    LIM = 23 * 1024 * 1024
    dbo = False
    dbx = plan(dbo, True) < LIM
    wsb = [p.sbuf(f"{tag}_w{i}", (128, FOUT), BF16) for i in range(nct)]
    xsb = [[p.sbuf(f"{tag}_x{b}_{i}", (128, TC + 4), BF16) for i in range(nct)]
           for b in range(2 if dbx else 1)]
    if not dbx:
        xsb.append(xsb[0])
    NB = 4
    pb = [p.psum(f"{tag}_pb{i}", (128, TC), F32) for i in range(NB)]
    osb = [[], []]
    bias_cols = {}
    for (flo, fhi, spec) in sinks:
        for f in range(flo // 128, (fhi + 127) // 128):
            osb[0].append(p.sbuf(f"{tag}_o0_{f}", (128, TC), spec.get("dt", BF16)))
            osb[1].append(p.sbuf(f"{tag}_o1_{f}", (128, TC), spec.get("dt", BF16)) if dbo
                          else osb[0][-1])
        if spec.get("bias") is not None:
            bias_cols[(flo, fhi)] = p.sbuf(f"{tag}_b{flo}", (128, ftiles(fhi - flo)), F32)
        if spec.get("scale_col") is not None:
            spec["_scol"] = p.sbuf(f"{tag}_s{flo}", (128, 1), F32)
    if has_res:
        r32 = [p.sbuf(f"{tag}_r{f}", (128, TC), F32) for f in range(nft)]
        q32 = [[p.sbuf(f"{tag}_q0_{f}", (128, TC), F32) for f in range(nft)]]
        q32.append([p.sbuf(f"{tag}_q1_{f}", (128, TC), F32) for f in range(nft)] if dbo
                   else q32[0])

    def spec_of(f):
        for (flo, fhi, spec) in sinks:
            if flo <= f * 128 < fhi:
                return flo, fhi, spec
        raise AssertionError(f)

    shifts = src_shift or [0] * nct

    def emit_xload(eng, ci):
        sname = f"lx{ci % 2}"
        buf = xsb[ci % 2]
        for i in range(nct):
            rows = fsz(i, CIN)
            s = shifts[i]
            lo = ci * TC - s
            if lo < 0:
                p.dma(eng, sname, buf[i][0:rows, 4:4 + TC - s], src[frange(i, CIN), 0:TC - s])
            else:
                p.dma(eng, sname, buf[i][0:rows, 4:4 + TC], src[frange(i, CIN), lo:lo + TC])
        return p.n(sname)

    lxmark = [0] * NCHK
    gldmark0 = [0]

    with nc.Block() as blk:
        @blk.gpsimd
        def _(eng):
            p.wait(eng, "st")
            for i in range(nct):
                rows = fsz(i, CIN)
                p.dma(eng, "gld", wsb[i][0:rows, :], wdram[i * 128:i * 128 + rows, :])
            for (flo, fhi, spec) in sinks:
                if spec.get("bias") is not None:
                    bc = bias_cols[(flo, fhi)]
                    for j in range(ftiles(fhi - flo)):
                        rows = fsz(j, fhi - flo)
                        p.dma(eng, "gld", bc[0:rows, j:j + 1], spec["bias"][j * 128:j * 128 + rows, :])
            gldmark0[0] = p.n("gld")
        @blk.sync
        def _(eng):
            p.wait(eng, "st")
            lxmark[0] = emit_xload(eng, 0)
        @blk.vector
        def _(eng):
            p.wait(eng, "gld")
            for b in range(2 if dbx else 1):
                for i in range(nct):
                    eng.memset(xsb[b][i][:, 0:4], 0.0)
            for (flo, fhi, spec) in sinks:
                if spec.get("scale_col") is not None:
                    call = eng.memset(spec["_scol"][:], float(spec["scale_col"]))
                    p.inc("vec", call)
                if spec.get("bias_scale") is not None:
                    bc = bias_cols[(flo, fhi)]
                    call = eng.tensor_scalar(bc[:], bc[:], float(spec["bias_scale"]), None, OP.mult)
                    p.inc("vec", call)

    mm_mark = [0] * NCHK
    vres_mark = [0] * NCHK
    evb0 = [0]
    with nc.Block() as blk:
        for ci in range(NCHK):
            @blk.sync
            def _(eng):
                if not dbx and ci > 0:
                    p.wait(eng, "mm", mm_mark[ci - 1])
                    lxmark[ci] = emit_xload(eng, ci)
                if has_res:
                    if ci >= 1:
                        p.wait(eng, "vec", vres_mark[ci - 1])
                    for f in range(nft):
                        flo2, _, sp = spec_of(f)
                        if sp.get("res") is not None:
                            w = fsz(f, FOUT)
                            r0 = sp.get("dst_f0", 0) + f * 128 - flo2
                            p.dma(eng, "ld", r32[f][0:w, :], sp["res"][r0:r0 + w, ci * TC:(ci + 1) * TC])
                if dbx and ci + 1 < NCHK:
                    if ci >= 1:
                        p.wait(eng, "mm", mm_mark[ci - 1])
                    lxmark[ci + 1] = emit_xload(eng, ci + 1)
            @blk.tensor
            def _(eng):
                p.wait(eng, f"lx{ci % 2}", lxmark[ci])
                if ci == 0:
                    p.wait(eng, "gld", gldmark0[0])
                    evb0[0] = p.n("evac")
                for f in range(nft):
                    gf = ci * nft + f
                    w = fsz(f, FOUT)
                    bank = pb[gf % NB]
                    if gf >= NB:
                        eng.wait_ge(p.sem["evac"][0], evb0[0] + (gf - NB + 1))
                    for c in range(nct):
                        rows = fsz(c, CIN)
                        s = shifts[c]
                        off = 4 if (ci > 0 or s == 0) else 4 - s
                        call = eng.matmul(bank[0:w, :], wsb[c][0:rows, f * 128:f * 128 + w],
                                          xsb[ci % 2][c][0:rows, off:off + TC], start=(c == 0), stop=(c == nct - 1))
                    p.inc("mm", call)
                mm_mark[ci] = p.n("mm")
            @blk.scalar
            def _(eng):
                p.wait(eng, "st")
                mmb = p.n("mm") - nft
                for f in range(nft):
                    gf = ci * nft + f
                    eng.wait_ge(p.sem["mm"][0], mmb + f + 1)
                    w = fsz(f, FOUT)
                    flo, fhi, spec = spec_of(f)
                    j = f - flo // 128
                    kw = {}
                    if spec.get("bias") is not None:
                        kw["bias"] = bias_cols[(flo, fhi)][0:w, j:j + 1]
                    if spec.get("scale_col") is not None:
                        kw["scale"] = spec["_scol"][0:w, 0:1]
                    call = eng.activation(osb[ci % 2][f][0:w, :], pb[gf % NB][0:w, :],
                                          spec.get("func", AF.Identity), **kw)
                    p.inc("evac", call)
            if has_res:
                @blk.vector
                def _(eng):
                    p.wait(eng, "evac")
                    p.wait(eng, "ld")
                    p.wait(eng, "st")
                    for f in range(nft):
                        _, _, sp = spec_of(f)
                        if sp.get("res") is not None:
                            call = eng.tensor_tensor(q32[ci % 2][f][0:fsz(f, FOUT), :], osb[ci % 2][f][0:fsz(f, FOUT), :],
                                                     r32[f][0:fsz(f, FOUT), :], OP.add)
                    p.inc("vec", call)
            vres_mark[ci] = p.n("vec")
            @blk.gpsimd
            def _(eng):
                if has_res:
                    p.wait(eng, "vec")
                else:
                    p.wait(eng, "evac")
                for (flo, fhi, spec) in sinks:
                    d0 = spec.get("dst_f0", 0)
                    for f in range(flo // 128, (fhi + 127) // 128):
                        w = fsz(f, FOUT)
                        r0 = d0 + f * 128 - flo
                        if spec.get("res") is not None:
                            p.dma(eng, "st", spec["dst"][r0:r0 + w, ci * TC:(ci + 1) * TC], q32[ci % 2][f][0:w, :])
                        else:
                            p.dma(eng, "st", spec["dst"][r0:r0 + w, ci * TC:(ci + 1) * TC], osb[ci % 2][f][0:w, :])
    p.pop()


def conv_dt_pass(p, tag="cdt"):
    nc = p.nc
    p.push()
    stage = p.sbuf(f"{tag}_stage", (128, T + 4), BF16)
    acc_a = p.sbuf(f"{tag}_acca", (128, T), BF16)
    acc_b = p.sbuf(f"{tag}_accb", (128, T), BF16)
    outt = p.sbuf(f"{tag}_out", (128, T), BF16)
    wcols = p.sbuf(f"{tag}_w", (128, 4), F32)
    bcol = p.sbuf(f"{tag}_b", (128, 1), F32)
    xbcr = p.dram["xbcr"]
    cw = p.dram["conv_w4"]
    cb = p.dram["conv_b_col"]
    nt = ftiles(M_CONVD)
    with nc.Block() as blk:
        @blk.vector
        def _(eng):
            eng.memset(stage[:, 0:4], 0.0)
        @blk.gpsimd
        def _(eng):
            p.wait(eng, "st")
    for i in range(nt):
        dst = (p.dram["xc"], i) if i < 16 else ((p.dram["bT"], 0) if i == 16 else (p.dram["cT"], 0))
        with nc.Block() as blk:
            @blk.sync
            def _(eng):
                p.dma(eng, "ld", stage[:, 4:4 + T], xbcr[frange(i, M_CONVD), :])
                p.dma(eng, "ld", wcols[:], cw[i * 128:(i + 1) * 128, :])
                p.dma(eng, "ld", bcol[:], cb[i * 128:(i + 1) * 128, :])
            @blk.vector
            def _(eng):
                p.wait(eng, "ld")
                if i > 0:
                    p.wait(eng, "st")
                p.seq(eng, "vec", eng.tensor_scalar(acc_a[:], stage[:, 1:1 + T], wcols[:, 0:1], None, OP.mult))
                p.seq(eng, "vec", eng.scalar_tensor_tensor(acc_b[:], stage[:, 2:2 + T], wcols[:, 1:2], acc_a[:], OP.mult, OP.add))
                p.seq(eng, "vec", eng.scalar_tensor_tensor(acc_a[:], stage[:, 3:3 + T], wcols[:, 2:3], acc_b[:], OP.mult, OP.add))
                call = eng.scalar_tensor_tensor(acc_b[:], stage[:, 4:4 + T], wcols[:, 3:4], acc_a[:], OP.mult, OP.add)
                p.inc("vec", call)
            @blk.scalar
            def _(eng):
                p.wait(eng, "vec")
                p.wait(eng, "st")
                call = eng.activation(outt[:], acc_b[:], AF.Silu, bias=bcol[:, 0:1])
                p.inc("act", call)
            @blk.gpsimd
            def _(eng):
                p.wait(eng, "act")
                d, j = dst
                p.dma(eng, "st", d[j * 128:(j + 1) * 128, :] if i < 16 else d[:, :], outt[:])
    dtr, dts = p.dram["dtr"], p.dram["dts"]
    t0 = p.sbuf(f"{tag}_dt0", (32, T), F32)
    t1 = p.sbuf(f"{tag}_dt1", (32, T), F32)
    t2 = p.sbuf(f"{tag}_dt2", (32, T), F32)
    neg1 = p.sbuf(f"{tag}_neg1", (32, 1), F32)
    one1 = p.sbuf(f"{tag}_one1", (32, 1), F32)
    with nc.Block() as blk:
        @blk.sync
        def _(eng):
            p.dma(eng, "ld", t0[:], dtr[:, :])
        @blk.vector
        def _(eng):
            eng.memset(neg1[:], -1.0)
            call = eng.memset(one1[:], 1.0)
            p.inc("vec", call)
        @blk.scalar
        def _(eng):
            p.wait(eng, "ld")
            p.wait(eng, "vec")
            p.seq(eng, "act", eng.activation(t1[:], t0[:], AF.Abs))
            p.seq(eng, "act", eng.activation(t1[:], t1[:], AF.Exp, scale=neg1[:, 0:1]))
            call = eng.activation(t1[:], t1[:], AF.Ln, bias=one1[:, 0:1])
            p.inc("act", call)
        @blk.vector
        def _(eng):
            p.wait(eng, "act")
            p.seq(eng, "vec", eng.tensor_scalar(t2[:], t0[:], 0.0, None, OP.max))
            call = eng.tensor_tensor(t2[:], t2[:], t1[:], OP.add)
            p.inc("vec", call)
        @blk.gpsimd
        def _(eng):
            p.wait(eng, "vec")
            p.dma(eng, "st", dts[:, :], t2[:])
    p.pop()


def ssd_pass(p, tag="ssd"):
    nc = p.nc
    p.push()
    identf = p.sbuf(f"{tag}_idf", (128, 128), F32)
    slt_bf = p.sbuf(f"{tag}_slt", (128, 128), BF16)
    slt_f = p.sbuf(f"{tag}_sltf", (128, 128), F32)
    ones_bf = p.sbuf(f"{tag}_onesb", (128, 128), BF16)
    utri_f = p.sbuf(f"{tag}_utri", (128, 128), F32)
    utri_b = p.sbuf(f"{tag}_utrib", (128, 128), BF16)
    ones_row = p.sbuf(f"{tag}_or", (1, 128), F32)
    ones_col = p.sbuf(f"{tag}_ocol", (128, 1), F32)
    arow = p.sbuf(f"{tag}_arow", (1, 32), F32)
    drow = p.sbuf(f"{tag}_drow", (1, 32), F32)
    abc = p.sbuf(f"{tag}_abc", (128, 32), F32)
    dbc = p.sbuf(f"{tag}_dbc", (128, 32), F32)
    carry32a = p.sbuf(f"{tag}_ca", (128, 2048), F32)
    carry32b = p.sbuf(f"{tag}_cb", (128, 2048), F32)
    carryb = p.sbuf(f"{tag}_cbf", (128, 2048), BF16)
    x_tok = p.sbuf(f"{tag}_xtok", (128, 2048), BF16)
    b_tok = p.sbuf(f"{tag}_btok", (128, 128), BF16)
    bT_sb = p.sbuf(f"{tag}_bT", (128, 128), BF16)
    cT_sb = p.sbuf(f"{tag}_cT", (128, 128), BF16)
    dt_sb = p.sbuf(f"{tag}_dt", (32, 128), F32)
    dtok = p.sbuf(f"{tag}_dtok", (128, 32), BF16)
    da_tok = p.sbuf(f"{tag}_da", (128, 32), BF16)
    b_all = p.sbuf(f"{tag}_ball", (128, 4096), BF16)
    expm = p.sbuf(f"{tag}_expm", (128, 4096), BF16)
    ea = p.sbuf(f"{tag}_ea", (128, 4096), BF16)
    gm_sb = p.sbuf(f"{tag}_gm", (128, 128), BF16)
    gm_all = p.sbuf(f"{tag}_gma", (128, 4096), BF16)
    dxt = p.sbuf(f"{tag}_dxt", (128, 2048), BF16)
    ch_all = p.sbuf(f"{tag}_cha", (128, 4096), BF16)
    dd_sb = p.sbuf(f"{tag}_dd", (128, 32), BF16)
    cd_sb = p.sbuf(f"{tag}_cd", (1, 32), F32)
    cdb_sb = p.sbuf(f"{tag}_cdb", (128, 32), F32)
    xs_sb = p.sbuf(f"{tag}_xs", (128, 2048), BF16)
    xd_sb = p.sbuf(f"{tag}_xd", (128, 2048), BF16)
    ystage = p.sbuf(f"{tag}_yst", (128, 2048), BF16)
    psA = p.psum(f"{tag}_psA", (128, 2048), F32)
    psB = p.psum(f"{tag}_psB", (128, 2048), F32)

    xc, bT, cT, dts, ytok = (p.dram[k] for k in ["xc", "bT", "cT", "dts", "ytok"])

    with nc.Block() as blk:
        @blk.gpsimd
        def _(eng):
            p.wait(eng, "st")
            p.dma(eng, "gld", slt_bf[:], p.dram["slt"][:, :])
            p.dma(eng, "gld", ones_bf[:], p.dram["ones2d"][:, :])
            p.dma(eng, "gld", utri_b[:], p.dram["utri"][:, :])
            p.wait(eng, "gld")
        @blk.sync
        def _(eng):
            p.dma(eng, "ld", identf[:], p.dram["ident"][:, :])
            p.dma(eng, "ld", slt_f[:], p.dram["slt"][:, :])
            p.dma(eng, "ld", utri_f[:], p.dram["utri"][:, :])
            p.dma(eng, "ld", arow[:], p.dram["A_log_row"][:, :])
            p.dma(eng, "ld", drow[:], p.dram["D_row"][:, :])
            p.wait(eng, "ld")
        @blk.vector
        def _(eng):
            eng.memset(ones_row[:], 1.0)
            eng.memset(ones_col[:], 1.0)
            eng.memset(carry32a[:], 0.0)
            call = eng.memset(carryb[:], 0.0)
            p.inc("vec", call)
    with nc.Block() as blk:
        @blk.scalar
        def _(eng):
            p.wait(eng, "ld")
            call = eng.activation(arow[:], arow[:], AF.Exp)
            p.inc("act", call)
        @blk.vector
        def _(eng):
            p.wait(eng, "act")
            call = eng.tensor_scalar(arow[:], arow[:], -1.0, None, OP.mult)
            p.inc("vec", call)
            p.wait(eng, "vec")
    with nc.Block() as blk:
        @blk.tensor
        def _(eng):
            p.wait(eng, "vec")
            eng.matmul(psA[:, 0:32], ones_row[:], arow[:], start=True, stop=True)
            call = eng.matmul(psA[:, 32:64], ones_row[:], drow[:], start=True, stop=True)
            p.inc("mm", call)
        @blk.vector
        def _(eng):
            p.wait(eng, "mm")
            eng.tensor_copy(abc[:], psA[:, 0:32])
            call = eng.tensor_copy(dbc[:], psA[:, 32:64])
            p.inc("vec", call)

    for c in range(getattr(p, 'ssd_chunks', NSC)):
        cs = slice(c * CH, (c + 1) * CH)
        with nc.Block() as blk:
            @blk.sync
            def _(eng):
                p.dma(eng, "ld", dt_sb[:], dts[:, cs])
                p.dma(eng, "ld", bT_sb[:], bT[:, cs])
                p.dma(eng, "ld", cT_sb[:], cT[:, cs])
                eng.dma_start(out=x_tok[:], in_=xc[:, cs], transpose=True).then_inc(p.sem["ld"][0], 16)
                p.sem["ld"][1] += 16
                eng.dma_start(out=b_tok[:], in_=bT[:, cs], transpose=True).then_inc(p.sem["ld"][0], 16)
                p.sem["ld"][1] += 16
            @blk.tensor
            def _(eng):
                p.wait(eng, "ld")
                call = eng.transpose(psB[:, 0:32], dt_sb[:], identf[0:32, 0:32])
                p.inc("mm", call)
            @blk.vector
            def _(eng):
                p.wait(eng, "mm")
                eng.tensor_copy(dtok[:], psB[:, 0:32])
                call = eng.tensor_tensor(da_tok[:], psB[:, 0:32], abc[:], OP.mult)
                p.inc("vec", call)
            @blk.vector
            def _(eng):
                p.wait(eng, "vec")
                call = eng.tensor_tensor(
                    b_all[:].rearrange("p (h l) -> p h l", h=32),
                    da_tok[:].unsqueeze(-1).broadcast_to((128, 32, 128)),
                    utri_b[:].unsqueeze(1).broadcast_to((128, 32, 128)),
                    OP.mult)
                p.inc("vec", call)
            mm0 = p.n("mm")
            a0 = p.n("act")
            @blk.tensor
            def _(eng):
                p.wait(eng, "vec")
                for j in range(4):
                    call = eng.matmul(psA[:, j * 512:(j + 1) * 512], slt_bf[:],
                                      b_all[:, j * 512:(j + 1) * 512], start=True, stop=True)
                p.inc("mm", call)
                for j in range(4):
                    call = eng.matmul(psB[:, j * 512:(j + 1) * 512], ones_bf[:],
                                      b_all[:, j * 512:(j + 1) * 512], start=True, stop=True)
                p.inc("mm", call)
                eng.wait_ge(p.sem["act"][0], a0 + 1)
                for j in range(4):
                    call = eng.matmul(psA[:, j * 512:(j + 1) * 512], slt_bf[:],
                                      b_all[:, 2048 + j * 512:2048 + (j + 1) * 512], start=True, stop=True)
                p.inc("mm", call)
                eng.wait_ge(p.sem["act"][0], a0 + 2)
                for j in range(4):
                    call = eng.matmul(psB[:, j * 512:(j + 1) * 512], ones_bf[:],
                                      b_all[:, 2048 + j * 512:2048 + (j + 1) * 512], start=True, stop=True)
                p.inc("mm", call)
            @blk.scalar
            def _(eng):
                eng.wait_ge(p.sem["mm"][0], mm0 + 1)
                call = eng.activation(expm[:, 0:2048], psA[:], AF.Exp)
                p.inc("act", call)
                eng.wait_ge(p.sem["mm"][0], mm0 + 2)
                call = eng.activation(ea[:, 0:2048], psB[:], AF.Exp)
                p.inc("act", call)
                eng.wait_ge(p.sem["mm"][0], mm0 + 3)
                call = eng.activation(expm[:, 2048:4096], psA[:], AF.Exp)
                p.inc("act", call)
                eng.wait_ge(p.sem["mm"][0], mm0 + 4)
                call = eng.activation(ea[:, 2048:4096], psB[:], AF.Exp)
                p.inc("act", call)
            mm1 = p.n("mm")
            a1 = p.n("act")
            @blk.tensor
            def _(eng):
                p.wait(eng, "act")
                eng.matmul(psB[:, 0:128], bT_sb[:], cT_sb[:], start=True, stop=True)
                call = eng.matmul(psB[:, 512:544], slt_bf[:], da_tok[:], start=True, stop=True)
                p.inc("mm", call)
            v1 = p.n("vec")
            @blk.vector
            def _(eng):
                eng.wait_ge(p.sem["mm"][0], mm1 + 1)
                call = eng.tensor_tensor(gm_sb[:], psB[:, 0:128], utri_f[:], OP.mult)
                p.inc("vec", call)
            @blk.scalar
            def _(eng):
                eng.wait_ge(p.sem["mm"][0], mm1 + 1)
                call = eng.activation(dd_sb[:], psB[:, 512:544], AF.Exp)
                p.inc("act", call)
            v2 = p.n("vec")
            @blk.vector
            def _(eng):
                p.wait(eng, "vec", v1 + 1)
                p.wait(eng, "act")
                eng.tensor_tensor(
                    gm_all[:].rearrange("p (h l) -> p h l", h=32),
                    gm_sb[:].unsqueeze(1).broadcast_to((128, 32, 128)),
                    expm[:].rearrange("p (h l) -> p h l", h=32), OP.mult)
                call = eng.tensor_tensor(ch_all[:].rearrange("p (h l) -> p h l", h=32),
                                         cT_sb[:].unsqueeze(1).broadcast_to((128, 32, 128)),
                                         ea[:].rearrange("p (h l) -> p h l", h=32), OP.mult)
                p.inc("vec", call)
            @blk.vector
            def _(eng):
                p.wait(eng, "vec")
                p.wait(eng, "act")
                p.seq(eng, "vec", eng.tensor_tensor(
                    xs_sb[:].rearrange("p (h q) -> p h q", h=32),
                    x_tok[:].rearrange("p (h q) -> p h q", h=32),
                    dtok[:].unsqueeze(-1).broadcast_to((128, 32, 64)), OP.mult))
                call = eng.tensor_tensor(xd_sb[:].rearrange("p (h q) -> p h q", h=32),
                                         xs_sb[:].rearrange("p (h q) -> p h q", h=32),
                                         dd_sb[:].unsqueeze(-1).broadcast_to((128, 32, 64)), OP.mult)
                p.inc("vec", call)
            mm2 = p.n("mm")
            @blk.tensor
            def _(eng):
                p.wait(eng, "vec")
                call = eng.matmul(psB[0:1, 1024:1056], ones_bf[:, 0:1], da_tok[:], start=True, stop=True)
                p.inc("mm", call)
            a2 = p.n("act")
            @blk.scalar
            def _(eng):
                p.wait(eng, "mm", mm2 + 1)
                call = eng.activation(cd_sb[:], psB[0:1, 1024:1056], AF.Exp)
                p.inc("act", call)
            mm3 = p.n("mm")
            @blk.tensor
            def _(eng):
                p.wait(eng, "act", a2 + 1)
                call = eng.matmul(psB[:, 1536:1568], ones_row[:], cd_sb[:], start=True, stop=True)
                p.inc("mm", call)
            v3 = p.n("vec")
            @blk.vector
            def _(eng):
                p.wait(eng, "mm", mm3 + 1)
                call = eng.tensor_copy(cdb_sb[:], psB[:, 1536:1568])
                p.inc("vec", call)
            mm4 = p.n("mm")
            @blk.tensor
            def _(eng):
                p.wait(eng, "vec")
                p.wait(eng, "act")
                for h in range(32):
                    hs = slice(h * 64, (h + 1) * 64)
                    hl = slice(h * 128, (h + 1) * 128)
                    eng.matmul(psA[:, hs], gm_all[:, hl], xs_sb[:, hs], start=True, stop=False)
                    call = eng.matmul(psA[:, hs], ch_all[:, hl], carryb[:, hs], start=False, stop=True)
                p.inc("mm", call)
                for h in range(32):
                    call = eng.matmul(psB[:, h * 64:(h + 1) * 64],
                                      b_tok[:], xd_sb[:, h * 64:(h + 1) * 64], start=True, stop=True)
                p.inc("mm", call)
            @blk.vector
            def _(eng):
                p.wait(eng, "mm", mm4 + 1)
                p.wait(eng, "st")
                p.seq(eng, "vec", eng.tensor_tensor(
                    dxt[:].rearrange("p (h q) -> p h q", h=32),
                    x_tok[:].rearrange("p (h q) -> p h q", h=32),
                    dbc[:].unsqueeze(-1).broadcast_to((128, 32, 64)), OP.mult))
                call = eng.tensor_tensor(ystage[:], psA[:], dxt[:], OP.add)
                p.inc("vec", call)
            v9 = p.n("vec")
            @blk.gpsimd
            def _(eng):
                p.wait(eng, "vec", v9)
                p.dma(eng, "st", ytok[cs, :], ystage[:])
            ca, cb_ = (carry32a, carry32b) if c % 2 == 0 else (carry32b, carry32a)
            @blk.vector
            def _(eng):
                p.wait(eng, "mm")
                p.wait(eng, "vec")
                p.seq(eng, "vec", eng.tensor_tensor(
                    cb_[:].rearrange("p (h q) -> p h q", h=32),
                    ca[:].rearrange("p (h q) -> p h q", h=32),
                    cdb_sb[:].unsqueeze(-1).broadcast_to((128, 32, 64)), OP.mult))
                p.seq(eng, "vec", eng.tensor_tensor(cb_[:], cb_[:], psB[:, 0:2048], OP.add))
                call = eng.tensor_copy(carryb[:], cb_[:])
                p.inc("vec", call)
    p.ssd_tiles = dict(b_all=b_all, expm=expm, ea=ea, gm_sb=gm_sb, gm_all=gm_all, ch_all=ch_all, dd_sb=dd_sb, cdb_sb=cdb_sb, xs_sb=xs_sb,
                       xd_sb=xd_sb, da_tok=da_tok, dtok=dtok, x_tok=x_tok,
                       abc=abc, arow=arow, utri_f=utri_f, utri_b=utri_b, dbc=dbc,
                       carryb=carryb, b_tok=b_tok, bT_sb=bT_sb, cT_sb=cT_sb, ystage=ystage)
    p.pop()


def groupln_gate_pass(p, tag="gln"):
    nc = p.nc
    p.push()
    nt = 16
    y32 = [p.sbuf(f"{tag}_y32_{i}", (128, TC), F32) for i in range(nt)]
    ybf = [p.sbuf(f"{tag}_ybf_{i}", (128, TC), BF16) for i in range(nt)]
    sq = [p.sbuf(f"{tag}_sq_{i}", (128, TC), BF16) for i in range(nt)]
    gsb = [p.sbuf(f"{tag}_g_{i}", (128, TC), BF16) for i in range(nt)]
    onesd = p.sbuf(f"{tag}_onesd", (128, 128), BF16)
    musb = p.sbuf(f"{tag}_mu", (128, TC), F32)
    va = p.sbuf(f"{tag}_va", (128, TC), F32)
    s_sb = p.sbuf(f"{tag}_s", (128, TC), F32)
    wcol = p.sbuf(f"{tag}_w", (128, nt), F32)
    bcol = p.sbuf(f"{tag}_bc", (128, nt), F32)
    outb = [p.sbuf(f"{tag}_o_{i}", (128, TC), BF16) for i in range(nt)]
    pmu = p.psum(f"{tag}_pmu", (128, TC), F32)
    pex = p.psum(f"{tag}_pex", (128, TC), F32)
    ytok, gate_s, mzin = p.dram["ytok"], p.dram["gate_s"], p.dram["mzin"]
    with nc.Block() as blk:
        @blk.gpsimd
        def _(eng):
            p.wait(eng, "st")
            p.dma(eng, "gld", onesd[:], p.dram["odiv2048"][:, :])
            p.wait(eng, "gld")
        @blk.sync
        def _(eng):
            for i in range(nt):
                p.dma(eng, "ld", wcol[:, i:i + 1], p.dram["mln_w_col"][i * 128:(i + 1) * 128, :])
                p.dma(eng, "ld", bcol[:, i:i + 1], p.dram["mln_b_col"][i * 128:(i + 1) * 128, :])
            p.wait(eng, "ld")
    for ci in range(NCHK):
        cs = slice(ci * TC, (ci + 1) * TC)
        with nc.Block() as blk:
            @blk.sync
            def _(eng):
                for i in range(nt):
                    eng.dma_start(out=ybf[i][:], in_=ytok[cs, i * 128:(i + 1) * 128],
                                  transpose=True).then_inc(p.sem["ld"][0], 16)
                    p.sem["ld"][1] += 16
                    p.dma(eng, "ld", gsb[i][:], gate_s[frange(i, 2048), cs])
            @blk.scalar
            def _(eng):
                p.wait(eng, "ld")
                for i in range(nt):
                    call = eng.activation(sq[i][:], ybf[i][:], AF.Square)
                p.inc("act", call)
            @blk.tensor
            def _(eng):
                p.wait(eng, "ld")
                p.wait(eng, "gld")
                for i in range(nt):
                    eng.matmul(pmu[:], onesd[:], ybf[i][:], start=(i == 0), stop=(i == nt - 1))
                p.wait(eng, "act")
                for i in range(nt):
                    call = eng.matmul(pex[:], onesd[:], sq[i][:], start=(i == 0), stop=(i == nt - 1))
                p.inc("mm", call)
            @blk.vector
            def _(eng):
                p.wait(eng, "mm")
                p.seq(eng, "vec", eng.tensor_copy(musb[:], pmu[:]))
                p.seq(eng, "vec", eng.tensor_tensor(va[:], musb[:], musb[:], OP.mult))
                p.seq(eng, "vec", eng.tensor_tensor(va[:], pex[:], va[:], OP.subtract))
                p.seq(eng, "vec", eng.tensor_scalar(va[:], va[:], EPS, None, OP.add))
                call = eng.reciprocal(s_sb[:], va[:])
                p.inc("vec", call)
            @blk.scalar
            def _(eng):
                p.wait(eng, "vec")
                call = eng.activation(s_sb[:], s_sb[:], AF.Sqrt)
                p.inc("act", call)
            @blk.vector
            def _(eng):
                p.wait(eng, "act")
                for i in range(nt):
                    call = eng.tensor_tensor(y32[i][:], ybf[i][:], musb[:], OP.subtract)
                p.seq(eng, "vec", call)
                for i in range(nt):
                    call = eng.tensor_tensor(y32[i][:], y32[i][:], s_sb[:], OP.mult)
                p.inc("vec", call)
            @blk.scalar
            def _(eng):
                p.wait(eng, "vec")
                for i in range(nt):
                    call = eng.activation(ybf[i][:], y32[i][:], AF.Identity,
                                          scale=wcol[:, i:i + 1], bias=bcol[:, i:i + 1])
                p.inc("act", call)
            @blk.vector
            def _(eng):
                p.wait(eng, "act")
                p.wait(eng, "st")
                for i in range(nt):
                    call = eng.tensor_tensor(outb[i][:], ybf[i][:], gsb[i][:], OP.mult)
                p.inc("vec", call)
            @blk.gpsimd
            def _(eng):
                p.wait(eng, "vec")
                for i in range(nt):
                    p.dma(eng, "st", mzin[frange(i, 2048), cs], outb[i][:])
    p.pop()


def res_ln_pass(p, tag, o_src, xres_src, xres_dst, y_dst=None, y_dst32=None):
    nc = p.nc
    p.push()
    nt = 8
    o32 = [p.sbuf(f"{tag}_o32_{i}", (128, TC), F32) for i in range(nt)]
    obf = [p.sbuf(f"{tag}_obf_{i}", (128, TC), BF16) for i in range(nt)]
    x32 = [p.sbuf(f"{tag}_x32_{i}", (128, TC), F32) for i in range(nt)]
    sq = [p.sbuf(f"{tag}_sq_{i}", (128, TC), BF16) for i in range(nt)]
    xnb = [p.sbuf(f"{tag}_xnb_{i}", (128, TC), BF16) for i in range(nt)]
    yb = [p.sbuf(f"{tag}_yb_{i}", (128, TC), BF16) for i in range(nt)]
    y32 = [p.sbuf(f"{tag}_y32_{i}", (128, TC), F32) for i in range(nt)] if y_dst32 is not None else None
    onesd = p.sbuf(f"{tag}_onesd", (128, 128), BF16)
    musb = p.sbuf(f"{tag}_mu", (128, TC), F32)
    va = p.sbuf(f"{tag}_va", (128, TC), F32)
    s_sb = p.sbuf(f"{tag}_s", (128, TC), F32)
    pmu = p.psum(f"{tag}_pmu", (128, TC), F32)
    pex = p.psum(f"{tag}_pex", (128, TC), F32)
    with nc.Block() as blk:
        @blk.gpsimd
        def _(eng):
            p.wait(eng, "st")
            p.dma(eng, "gld", onesd[:], p.dram["odiv1024"][:, :])
    for ci in range(NCHK):
        cs = slice(ci * TC, (ci + 1) * TC)
        with nc.Block() as blk:
            @blk.sync
            def _(eng):
                for i in range(nt):
                    p.dma(eng, "ld", o32[i][:], o_src[frange(i, D), cs])
                    p.dma(eng, "ld", x32[i][:], xres_src[frange(i, D), cs])
            @blk.gpsimd
            def _(eng):
                for i in range(nt):
                    p.dma(eng, "gld", obf[i][:], o_src[frange(i, D), cs])

            def stats(src_tiles, sq_dst):
                @blk.scalar
                def _(eng):
                    p.wait(eng, "vec")
                    p.wait(eng, "aux")
                    for i in range(nt):
                        call = eng.activation(sq_dst[i][:], src_tiles[i][:], AF.Square)
                    p.inc("act", call)
                @blk.tensor
                def _(eng):
                    p.wait(eng, "vec")
                    p.wait(eng, "aux")
                    p.wait(eng, "gld")
                    p.wait(eng, "ld")
                    for i in range(nt):
                        eng.matmul(pmu[:], onesd[:], src_tiles[i][:], start=(i == 0), stop=(i == nt - 1))
                    p.wait(eng, "act")
                    for i in range(nt):
                        call = eng.matmul(pex[:], onesd[:], sq_dst[i][:], start=(i == 0), stop=(i == nt - 1))
                    p.inc("mm", call)
                @blk.vector
                def _(eng):
                    p.wait(eng, "mm")
                    p.seq(eng, "vec", eng.tensor_copy(musb[:], pmu[:]))
                    p.seq(eng, "vec", eng.tensor_tensor(va[:], musb[:], musb[:], OP.mult))
                    p.seq(eng, "vec", eng.tensor_tensor(va[:], pex[:], va[:], OP.subtract))
                    p.seq(eng, "vec", eng.tensor_scalar(va[:], va[:], EPS, None, OP.add))
                    call = eng.reciprocal(s_sb[:], va[:])
                    p.inc("vec", call)
                @blk.scalar
                def _(eng):
                    p.wait(eng, "vec")
                    call = eng.activation(s_sb[:], s_sb[:], AF.Sqrt)
                    p.inc("act", call)
                return p.n("act")

            def norm_tiles(eng, sem, tiles, smark, stm, first):
                p.wait(eng, "act", smark)
                p.wait(eng, "st", stm)
                for i in tiles:
                    eng.tensor_tensor(first[i][:], first[i][:], musb[:], OP.subtract)
                for i in tiles:
                    eng.tensor_tensor(first[i][:], first[i][:], s_sb[:], OP.mult)

            @blk.scalar
            def _(eng):
                p.wait(eng, "gld")
                p.wait(eng, "ld")
            smark1 = stats(obf, sq)
            stm = p.n("st")
            VT, GT = list(range(4)), list(range(4, nt))
            @blk.vector
            def _(eng):
                norm_tiles(eng, "vec", VT, smark1, stm, o32)
                for i in VT:
                    eng.tensor_tensor(x32[i][:], x32[i][:], o32[i][:], OP.add)
                for i in VT:
                    call = eng.tensor_copy(xnb[i][:], x32[i][:])
                p.inc("vec", call)
            @blk.gpsimd
            def _(eng):
                norm_tiles(eng, "aux", GT, smark1, stm, o32)
                for i in GT:
                    eng.tensor_tensor(x32[i][:], x32[i][:], o32[i][:], OP.add)
                for i in GT:
                    call = eng.tensor_copy(xnb[i][:], x32[i][:])
                p.inc("aux", call)
            @blk.gpsimd
            def _(eng):
                p.wait(eng, "vec")
                for i in range(nt):
                    p.dma(eng, "st", xres_dst[frange(i, D), cs], x32[i][:])
            if y_dst is None:
                continue
            smark2 = stats(xnb, sq)
            stm2 = p.n("st")
            @blk.vector
            def _(eng):
                norm_tiles(eng, "vec", VT, smark2, stm2, x32)
                for i in VT:
                    call = eng.tensor_copy(yb[i][:], x32[i][:])
                p.inc("vec", call)
            @blk.gpsimd
            def _(eng):
                norm_tiles(eng, "aux", GT, smark2, stm2, x32)
                for i in GT:
                    call = eng.tensor_copy(yb[i][:], x32[i][:])
                p.inc("aux", call)
            @blk.gpsimd
            def _(eng):
                p.wait(eng, "vec")
                for i in range(nt):
                    p.dma(eng, "st", y_dst[frange(i, D), cs], yb[i][:])
                    if y_dst32 is not None:
                        p.dma(eng, "st", y_dst32[frange(i, D), cs], x32[i][:])
    p.pop()


INPUT_DEFS = [
    ("xT", (D, T), F32),
    ("W_inproj", (D, M_PROJ), F32),
    ("conv_w4", (M_CONVD, 4), F32),
    ("conv_b_col", (M_CONVD, 1), F32),
    ("dt_bias_col", (32, 1), F32),
    ("A_log_row", (1, 32), F32),
    ("D_row", (1, 32), F32),
    ("mln_w_col", (M_INTER, 1), F32),
    ("mln_b_col", (M_INTER, 1), F32),
    ("W_outproj", (M_INTER, D), F32),
] + [(k, v.shape, F32) for k, v in CONSTS.items()] + [
    (f"ffn_wqkv{_s}", (D, 3 * L_INNER), F32) for _s in range(2)] + [
    (f"ffn_qs_col{_s}", (128, 1), F32) for _s in range(2)] + [
    (f"ffn_ks_col{_s}", (128, 1), F32) for _s in range(2)] + [
    (f"ffn_qs_row{_s}", (1, 64), F32) for _s in range(2)] + [
    (f"ffn_ks_row{_s}", (1, 64), F32) for _s in range(2)] + [
    (f"ffn_wo{_s}", (L_INNER, D), F32) for _s in range(2)] + [
    (f"ffn_lng_col{_s}", (D, 1), F32) for _s in range(2)] + [
    (f"ffn_lnb_col{_s}", (D, 1), F32) for _s in range(2)] + [
    (f"ffn_w1_{_s}", (D, 2 * L_HID), F32) for _s in range(2)] + [
    (f"ffn_b1_col{_s}", (2 * L_HID, 1), F32) for _s in range(2)] + [
    (f"ffn_w2_{_s}", (L_HID, D), F32) for _s in range(2)] + [
    (f"ffn_b2_col{_s}", (D, 1), F32) for _s in range(2)] + [
    ("attn_qkv_w", (D, 3 * D), F32),
    ("attn_qkv_w_v", (D, D), F32),
    ("attn_qkv_bq_col", (D, 1), F32),
    ("attn_qkv_bk_col", (D, 1), F32),
    ("attn_bv_col", (D, 1), F32),
    ("attn_out_w", (D, D), F32),
    ("attn_out_b_row", (1, D), F32),
    ("l1_lng_col", (D, 1), F32),
    ("l1_lnb_col", (D, 1), F32),
    ("fc1_q", (D, 4096), I32),
    ("fc1_scales", (8, 4096), F32),
    ("fc1_b_col", (4096, 1), F32),
    ("fc2_q", (4096, D), I32),
    ("fc2_scales", (32, D), F32),
    ("fc2_b_col", (D, 1), F32),
]

SCRATCH_DEFS_B2 = []
for _s in range(2):
    SCRATCH_DEFS_B2 += [
        (f"qkvtok{_s}", (T, 1536), BF16),
        (f"oT{_s}", (L_INNER, T), BF16),
        (f"zmid{_s}", (D, T), F32),
        (f"ln2o{_s}", (D, T), BF16),
        (f"h1_{_s}", (2 * L_HID, T), BF16),
        (f"m1_{_s}", (L_HID, T), BF16),
        (f"zout{_s}", (D, T), F32),
    ]
SCRATCH_DEFS_B2 += [("x2", (D, T), F32), ("y3", (D, T), BF16)]

SCRATCH_DEFS_B34 = [
    ("qT3", (D, T), BF16),
    ("kT3", (D, T), BF16),
    ("vtok3", (T, D), BF16),
    ("qnm", (A_HEADS, T), F32),
    ("oT3", (D, T), BF16),
    ("rrow3", (1, D), F32),
    ("mo3", (D, T), F32),
    ("x3", (D, T), F32),
    ("y4", (D, T), BF16),
    ("y4f", (D, T), F32),
    ("w1d", (D, 4096), BF16),
    ("w2d", (4096, D), BF16),
    ("h4", (D, T), BF16),
    ("y4h", (4096, T), BF16),
    ("ffo", (D, T), F32),
]

SCRATCH_DEFS = [
    ("y1", (D, T), BF16),
    ("gate_s", (M_INTER, T), BF16),
    ("xbcr", (M_CONVD, T), BF16),
    ("dtr", (32, T), F32),
    ("xc", (M_INTER, T), BF16),
    ("bT", (128, T), BF16),
    ("cT", (128, T), BF16),
    ("dts", (32, T), F32),
    ("ytok", (T, M_INTER), BF16),
    ("mzin", (M_INTER, T), BF16),
    ("mo", (D, T), F32),
    ("x1", (D, T), F32),
    ("y2", (D, T), BF16),
    ("y2f", (D, T), F32),
]


def emit_b1(p, upto=99):
    d = p.dram
    ln_pass(p, "ln0", d["xT"], d["y1"], D)
    if upto < 1:
        return
    dense_pass(p, "inproj", d["y1"], D, d["W_inproj"], M_PROJ, sinks=[
        (0, 2048, dict(func=AF.Silu, dst=d["gate_s"])),
        (2048, 4352, dict(func=AF.Identity, dst=d["xbcr"])),
        (4352, 4384, dict(func=AF.Identity, bias=d["dt_bias_col"], dt=F32, dst=d["dtr"])),
    ])
    if upto < 2:
        return
    conv_dt_pass(p)
    if upto < 3:
        return
    ssd_pass(p)
    if upto < 4:
        return
    groupln_gate_pass(p)
    if upto < 5:
        return
    dense_pass(p, "outproj", d["mzin"], M_INTER, d["W_outproj"], D, sinks=[
        (0, D, dict(func=AF.Identity, dt=F32, dst=d["mo"])),
    ])
    if upto < 6:
        return
    res_ln_pass(p, "rl1", d["mo"], d["xT"], d["x1"], d["y2"], d["y2f"])


def emit_b2(p, upto=99):
    d = p.dram
    zin = d["y2f"]
    step = 0
    for s in range(2):
        shifts = [0, 0, 0, 0, 1, 1, 1, 1]
        dense_tok_pass(p, f"qkv{s}", zin, D, d[f"ffn_wqkv{s}"], 1536, d[f"qkvtok{s}"],
                       shifts=shifts, src_f32=True)
        step += 1
        if upto < step:
            return
        lffn_attn_pass(p, s)
        step += 1
        if upto < step:
            return
        dense_pass(p, f"wo{s}", d[f"oT{s}"], L_INNER, d[f"ffn_wo{s}"], D, sinks=[
            (0, D, dict(func=AF.Identity, res=zin, dt=BF16, dst=d[f"zmid{s}"], res_dt=F32)),
        ])
        step += 1
        if upto < step:
            return
        ln_affine_pass(p, f"ln2{s}", d[f"zmid{s}"], d[f"ln2o{s}"],
                       d[f"ffn_lng_col{s}"], d[f"ffn_lnb_col{s}"], shifts=shifts)
        step += 1
        if upto < step:
            return
        w1glu_pass(p, f"w1g{s}", d[f"ln2o{s}"], d[f"ffn_w1_{s}"], d[f"ffn_b1_col{s}"],
                   d[f"m1_{s}"])
        step += 2
        if upto < step:
            return
        dense_pass(p, f"w2{s}", d[f"m1_{s}"], L_HID, d[f"ffn_w2_{s}"], D, sinks=[
            (0, D, dict(func=AF.Identity, bias=d[f"ffn_b2_col{s}"], res=d[f"zmid{s}"], dst=d[f"zout{s}"])),
        ])
        step += 1
        if upto < step:
            return
        zin = d[f"zout{s}"]
    res_ln_pass(p, "rl2", d["zout1"], d["x1"], d["x2"], d["y3"])


def emit_b3(p, upto=99):
    d = p.dram
    s8 = float(1.0 / np.sqrt(8.0))
    dense_pass(p, "qk3", d["y3"], D, d["attn_qkv_w"][:, 0:2048], 2048, sinks=[
        (0, D, dict(func=AF.Identity, bias=d["attn_qkv_bq_col"], bias_scale=s8, scale_col=s8, dst=d["qT3"])),
        (D, 2 * D, dict(func=AF.Identity, bias=d["attn_qkv_bk_col"], bias_scale=s8, scale_col=s8, dst=d["kT3"])),
    ])
    if upto < 2:
        return
    dense_tok_pass(p, "v3", d["y3"], D, d["attn_qkv_w_v"], D, d["vtok3"], src_f32=False)
    if upto < 3:
        return
    l1_bounds_pass(p)
    if upto < 4:
        return
    l1_attn_pass(p)
    if upto < 5:
        return
    l1_rvec_pass(p)
    if upto < 6:
        return
    rcol = d["rrow3"].rearrange("one (f c) -> (one f) c", c=1)
    dense_pass(p, "ao3", d["oT3"], D, d["attn_out_w"], D, sinks=[
        (0, D, dict(func=AF.Identity, bias=rcol, dt=F32, dst=d["mo3"])),
    ])
    if upto < 7:
        return
    res_ln_pass(p, "rl3", d["mo3"], d["x2"], d["x3"], d["y4"], d["y4f"])


def emit_b4(p, upto=99):
    d = p.dram
    dequant_pass(p, "dq1", d["fc1_q"], d["fc1_scales"], d["w1d"], D, 4096)
    dequant_pass(p, "dq2", d["fc2_q"], d["fc2_scales"], d["w2d"], 4096, D)
    if upto < 2:
        return
    ln_affine_pass(p, "ln4", d["y4f"], d["h4"], d["l1_lng_col"], d["l1_lnb_col"])
    if upto < 3:
        return
    dense_pass(p, "fc1", d["h4"], D, d["w1d"], 4096, sinks=[
        (0, 4096, dict(func=AF.Silu, bias=d["fc1_b_col"], dst=d["y4h"])),
    ])
    if upto < 4:
        return
    dense_pass(p, "fc2", d["y4h"], 4096, d["w2d"], D, sinks=[
        (0, D, dict(func=AF.Identity, bias=d["fc2_b_col"], bias_scale=0.5, scale_col=0.5,
                    res=d["y4f"], dst=d["ffo"])),
    ])
    if upto < 5:
        return
    res_ln_pass(p, "rl4", d["ffo"], d["x3"], d["xoutT"])


def build(dbg=(), blocks=(1,), upto=99, in_override=(), passes=None, ssd_chunks=NSC):
    nc = bass.Bass("TRN2", target_bir_lowering=False)
    p = P(nc, dbg)
    p.inov = set(in_override)
    p.ssd_chunks = ssd_chunks
    for name, shape, dt in INPUT_DEFS:
        p.dram[name] = nc.dram_tensor(name, list(shape), dt, kind="ExternalInput").ap()
    for name, shape, dt in SCRATCH_DEFS + SCRATCH_DEFS_B2 + SCRATCH_DEFS_B34:
        p.scratch(name, shape, dt)
    p.dram["xoutT"] = nc.dram_tensor("xoutT", [D, T], F32, kind="ExternalOutput").ap()
    for _s in range(2):
        if f"dbg_qT{_s}" in p.dbg:
            p.scratch(f"dbg_qT{_s}", (512, T), BF16)
            p.scratch(f"dbg_kT{_s}", (512, T), BF16)
        if f"dbg_rt{_s}" in p.dbg:
            p.scratch(f"dbg_rt{_s}", (256, 2048), BF16)
    if passes is not None:
        for fn in passes:
            fn(p)
    else:
        if 1 in blocks:
            emit_b1(p, upto if 2 not in blocks else 99)
        if 2 in blocks:
            emit_b2(p, upto if max(blocks) > 2 else upto)
        if 3 in blocks:
            emit_b3(p, upto if 4 in blocks else upto)
        if 4 in blocks:
            emit_b4(p, upto)
    with nc.Block() as blk:
        @blk.sync
        def _(eng):
            p.wait(eng, "st")
            p.wait(eng, "gld")
            p.wait(eng, "ld")
    while p.scopes:
        p.pop()
    return p


def host_inputs(inputs):
    g = {}
    g["W_inproj"] = np.ascontiguousarray(inputs["L0_attn_in_proj_w"].T)
    g["conv_w4"] = np.ascontiguousarray(inputs["L0_attn_conv_w"].reshape(M_CONVD, 4))
    g["conv_b_col"] = np.ascontiguousarray(inputs["L0_attn_conv_b"].reshape(M_CONVD, 1))
    g["dt_bias_col"] = np.ascontiguousarray(inputs["L0_attn_dt_bias"].reshape(32, 1))
    g["A_log_row"] = np.ascontiguousarray(inputs["L0_attn_A_log"].reshape(1, 32))
    g["D_row"] = np.ascontiguousarray(inputs["L0_attn_D"].reshape(1, 32))
    g["mln_w_col"] = np.ascontiguousarray(inputs["L0_attn_ln_w"].reshape(M_INTER, 1))
    g["mln_b_col"] = np.ascontiguousarray(inputs["L0_attn_ln_b"].reshape(M_INTER, 1))
    g["W_outproj"] = np.ascontiguousarray(inputs["L0_attn_out_proj_w"].T)
    for s in range(2):
        g[f"ffn_wqkv{s}"] = np.ascontiguousarray(inputs["L0_ffn_wqkv"][s])
        g[f"ffn_qs_col{s}"] = np.ascontiguousarray(np.tile(inputs["L0_ffn_q_scale"][s], 2).reshape(128, 1))
        g[f"ffn_ks_col{s}"] = np.ascontiguousarray(np.tile(inputs["L0_ffn_k_scale"][s], 2).reshape(128, 1))
        g[f"ffn_qs_row{s}"] = np.ascontiguousarray(inputs["L0_ffn_q_scale"][s].reshape(1, 64))
        g[f"ffn_ks_row{s}"] = np.ascontiguousarray(inputs["L0_ffn_k_scale"][s].reshape(1, 64))
        g[f"ffn_wo{s}"] = np.ascontiguousarray(inputs["L0_ffn_wo"][s])
        g[f"ffn_lng_col{s}"] = np.ascontiguousarray(inputs["L0_ffn_ln_g"][s].reshape(D, 1))
        g[f"ffn_lnb_col{s}"] = np.ascontiguousarray(inputs["L0_ffn_ln_b"][s].reshape(D, 1))
        perm = []
        for t in range(NPAIR):
            w = 128 if t < NPAIR - 1 else L_HID - 128 * (NPAIR - 1)
            perm += list(range(128 * t, 128 * t + w))
            perm += list(range(L_HID + 128 * t, L_HID + 128 * t + w))
        g[f"ffn_w1_{s}"] = np.ascontiguousarray(inputs["L0_ffn_w1"][s][:, perm])
        g[f"ffn_b1_col{s}"] = np.ascontiguousarray(inputs["L0_ffn_b1"][s][perm].reshape(2 * L_HID, 1))
        g[f"ffn_w2_{s}"] = np.ascontiguousarray(inputs["L0_ffn_w2"][s])
        g[f"ffn_b2_col{s}"] = np.ascontiguousarray(inputs["L0_ffn_b2"][s].reshape(D, 1))
    g["attn_qkv_w"] = np.ascontiguousarray(inputs["L1_attn_qkv_w"])
    g["attn_qkv_w_v"] = np.ascontiguousarray(inputs["L1_attn_qkv_w"][:, 2048:3072])
    g["attn_qkv_bq_col"] = np.ascontiguousarray(inputs["L1_attn_qkv_b"][0:1024].reshape(D, 1))
    g["attn_qkv_bk_col"] = np.ascontiguousarray(inputs["L1_attn_qkv_b"][1024:2048].reshape(D, 1))
    g["attn_bv_col"] = np.ascontiguousarray(inputs["L1_attn_qkv_b"][2048:3072].reshape(D, 1))
    g["attn_out_w"] = np.ascontiguousarray(inputs["L1_attn_out_w"])
    g["attn_out_b_row"] = np.ascontiguousarray(inputs["L1_attn_out_b"].reshape(1, D))
    g["l1_lng_col"] = np.ascontiguousarray(inputs["L1_ffn_ln_gamma"].reshape(D, 1))
    g["l1_lnb_col"] = np.ascontiguousarray(inputs["L1_ffn_ln_beta"].reshape(D, 1))
    g["fc1_q"] = np.ascontiguousarray(inputs["L1_ffn_fc1_q"])
    g["fc1_scales"] = np.ascontiguousarray(inputs["L1_ffn_fc1_scales"])
    g["fc1_b_col"] = np.ascontiguousarray(inputs["L1_ffn_fc1_bias"].reshape(4096, 1))
    g["fc2_q"] = np.ascontiguousarray(inputs["L1_ffn_fc2_q"])
    g["fc2_scales"] = np.ascontiguousarray(inputs["L1_ffn_fc2_scales"])
    g["fc2_b_col"] = np.ascontiguousarray(inputs["L1_ffn_fc2_bias"].reshape(D, 1))
    for k, v in CONSTS.items():
        g[k] = v
    return g


def kernel(**inputs):
    p = build(dbg=(), blocks=(1, 2, 3, 4))
    g = host_inputs(inputs)
    x = inputs["L0_attn_hidden_states"]
    in_maps = []
    for b in range(8):
        m = dict(g)
        m["xT"] = np.ascontiguousarray(x[b].T)
        in_maps.append(m)
    res = run_bass_kernel_spmd(p.nc, in_maps, core_ids=list(range(8)))
    out = np.stack([np.ascontiguousarray(r["xoutT"].T) for r in res.results])
    return out


def dense_tok_pass(p, tag, src, CIN, wdram, FOUT, dst, shifts=None, src_f32=True):
    nc = p.nc
    p.push()
    nct = ftiles(CIN)
    nfc = (FOUT + TC - 1) // TC
    ngrp = (TC // 128) * nfc
    wsb = [p.sbuf(f"{tag}_w{i}", (128, FOUT), BF16) for i in range(nct)]
    xsb = [[p.sbuf(f"{tag}_x{b}_{i}", (128, TC + 4), BF16) for i in range(nct)] for b in range(2)]
    NB = 4
    pb = [p.psum(f"{tag}_pb{i}", (128, TC), F32) for i in range(NB)]
    osb = [p.sbuf(f"{tag}_o{i}", (128, TC), BF16) for i in range(NB)]
    shifts = shifts or [0] * nct
    lxmark = [0] * NCHK
    gldmark0 = [0]
    mm_mark = [0] * NCHK
    evb0 = [0]

    def emit_xload(eng, ci):
        sname = f"lx{ci % 2}"
        for i in range(nct):
            rows = fsz(i, CIN)
            s = shifts[i]
            lo = ci * TC - s
            if lo < 0:
                p.dma(eng, sname, xsb[ci % 2][i][0:rows, 4:4 + TC - s], src[frange(i, CIN), 0:TC - s])
            else:
                p.dma(eng, sname, xsb[ci % 2][i][0:rows, 4:4 + TC], src[frange(i, CIN), lo:lo + TC])
        return p.n(sname)

    with nc.Block() as blk:
        @blk.gpsimd
        def _(eng):
            p.wait(eng, "st")
            for i in range(nct):
                rows = fsz(i, CIN)
                p.dma(eng, "gld", wsb[i][0:rows, :], wdram[i * 128:i * 128 + rows, :])
            gldmark0[0] = p.n("gld")
            if src_f32:
                lxmark[0] = emit_xload(eng, 0)
        @blk.sync
        def _(eng):
            if not src_f32:
                p.wait(eng, "st")
                lxmark[0] = emit_xload(eng, 0)
        @blk.vector
        def _(eng):
            p.wait(eng, "gld")
            for b in range(2):
                for i in range(nct):
                    eng.memset(xsb[b][i][:, 0:4], 0.0)
    with nc.Block() as blk:
        for ci in range(NCHK):
            @blk.sync
            def _(eng):
                if not src_f32 and ci + 1 < NCHK:
                    if ci >= 1:
                        p.wait(eng, "mm", mm_mark[ci - 1])
                    lxmark[ci + 1] = emit_xload(eng, ci + 1)
            @blk.gpsimd
            def _(eng):
                if src_f32 and ci + 1 < NCHK:
                    if ci >= 1:
                        p.wait(eng, "mm", mm_mark[ci - 1])
                    lxmark[ci + 1] = emit_xload(eng, ci + 1)
            @blk.tensor
            def _(eng):
                p.wait(eng, f"lx{ci % 2}", lxmark[ci])
                if ci == 0:
                    p.wait(eng, "gld", gldmark0[0])
                    evb0[0] = p.n("evac")
                g = 0
                for it in range(TC // 128):
                    for fc in range(nfc):
                        gg = ci * ngrp + g
                        w = min(TC, FOUT - fc * TC)
                        bank = pb[g % NB]
                        if gg >= NB:
                            eng.wait_ge(p.sem["evac"][0], evb0[0] + (gg - NB + 1))
                        for c in range(nct):
                            rows = fsz(c, CIN)
                            s = shifts[c]
                            off = 4 if (ci > 0 or s == 0) else 4 - s
                            call = eng.matmul(bank[:, 0:w],
                                              xsb[ci % 2][c][0:rows, off + it * 128:off + (it + 1) * 128],
                                              wsb[c][0:rows, fc * TC:fc * TC + w],
                                              start=(c == 0), stop=(c == nct - 1))
                        p.inc("mm", call)
                        g += 1
                mm_mark[ci] = p.n("mm")
            @blk.scalar
            def _(eng):
                p.wait(eng, "st")
                mmb = p.n("mm") - ngrp
                g = 0
                for it in range(TC // 128):
                    for fc in range(nfc):
                        w = min(TC, FOUT - fc * TC)
                        eng.wait_ge(p.sem["mm"][0], mmb + g + 1)
                        call = eng.activation(osb[g % NB][:, 0:w], pb[g % NB][:, 0:w], AF.Identity)
                        p.inc("evac", call)
                        g += 1
            @blk.gpsimd
            def _(eng):
                evb2 = p.n("evac")
                g = 0
                for it in range(TC // 128):
                    for fc in range(nfc):
                        w = min(TC, FOUT - fc * TC)
                        p.wait(eng, "evac", evb2 - ngrp + g + 1)
                        p.dma(eng, "st", dst[ci * TC + it * 128:ci * TC + (it + 1) * 128, fc * TC:fc * TC + w],
                              osb[g % NB][:, 0:w])
                        g += 1
    p.pop()


def ln_affine_pass(p, tag, src, dst, gamma, beta, shifts=None, CIN=D):
    nc = p.nc
    p.push()
    nt = ftiles(CIN)
    x32 = [p.sbuf(f"{tag}_x32_{i}", (128, TC + 4), F32) for i in range(nt)]
    xbf = [p.sbuf(f"{tag}_xbf_{i}", (128, TC + 4), BF16) for i in range(nt)]
    sq = [p.sbuf(f"{tag}_sq_{i}", (128, TC), BF16) for i in range(nt)]
    onesd = p.sbuf(f"{tag}_onesd", (128, 128), BF16)
    musb = p.sbuf(f"{tag}_mu", (128, TC), F32)
    va = p.sbuf(f"{tag}_va", (128, TC), F32)
    s_sb = p.sbuf(f"{tag}_s", (128, TC), F32)
    gcol = p.sbuf(f"{tag}_g", (128, nt), F32)
    bcol = p.sbuf(f"{tag}_bc", (128, nt), F32)
    outb = [p.sbuf(f"{tag}_o_{i}", (128, TC), BF16) for i in range(nt)]
    pmu = p.psum(f"{tag}_pmu", (128, TC), F32)
    pex = p.psum(f"{tag}_pex", (128, TC), F32)
    shifts = shifts or [0] * nt
    oc = p.dram["odiv2048"] if CIN == 2048 else p.dram["odiv1024"]
    with nc.Block() as blk:
        @blk.gpsimd
        def _(eng):
            p.wait(eng, "st")
            p.dma(eng, "gld", onesd[:], oc[:, :])
            p.wait(eng, "gld")
        @blk.sync
        def _(eng):
            for i in range(nt):
                p.dma(eng, "ld", gcol[:, i:i + 1], gamma[i * 128:(i + 1) * 128, :])
                p.dma(eng, "ld", bcol[:, i:i + 1], beta[i * 128:(i + 1) * 128, :])
            p.wait(eng, "ld")
        @blk.vector
        def _(eng):
            for i in range(nt):
                eng.memset(x32[i][:, 0:4], 0.0)
                eng.memset(xbf[i][:, 0:4], 0.0)
    for ci in range(NCHK):
        with nc.Block() as blk:
            @blk.sync
            def _(eng):
                for i in range(nt):
                    s = shifts[i]
                    lo = ci * TC - s
                    if lo < 0:
                        p.dma(eng, "ld", x32[i][:, 4:4 + TC - s], src[frange(i, CIN), 0:TC - s])
                    else:
                        p.dma(eng, "ld", x32[i][:, 4:4 + TC], src[frange(i, CIN), lo:lo + TC])
            @blk.gpsimd
            def _(eng):
                for i in range(nt):
                    s = shifts[i]
                    lo = ci * TC - s
                    if lo < 0:
                        p.dma(eng, "gld", xbf[i][:, 4:4 + TC - s], src[frange(i, CIN), 0:TC - s])
                    else:
                        p.dma(eng, "gld", xbf[i][:, 4:4 + TC], src[frange(i, CIN), lo:lo + TC])

            def off(i):
                return 4 if (ci > 0 or shifts[i] == 0) else 4 - shifts[i]
            @blk.scalar
            def _(eng):
                p.wait(eng, "gld")
                p.wait(eng, "ld")
                for i in range(nt):
                    call = eng.activation(sq[i][:], xbf[i][:, off(i):off(i) + TC], AF.Square)
                p.inc("act", call)
            @blk.tensor
            def _(eng):
                p.wait(eng, "ld")
                p.wait(eng, "gld")
                for i in range(nt):
                    eng.matmul(pmu[:], onesd[:], xbf[i][:, off(i):off(i) + TC], start=(i == 0), stop=(i == nt - 1))
                p.wait(eng, "act")
                for i in range(nt):
                    call = eng.matmul(pex[:], onesd[:], sq[i][:], start=(i == 0), stop=(i == nt - 1))
                p.inc("mm", call)
            @blk.vector
            def _(eng):
                p.wait(eng, "mm")
                p.seq(eng, "vec", eng.tensor_copy(musb[:], pmu[:]))
                p.seq(eng, "vec", eng.tensor_tensor(va[:], musb[:], musb[:], OP.mult))
                p.seq(eng, "vec", eng.tensor_tensor(va[:], pex[:], va[:], OP.subtract))
                p.seq(eng, "vec", eng.tensor_scalar(va[:], va[:], EPS, None, OP.add))
                call = eng.reciprocal(s_sb[:], va[:])
                p.inc("vec", call)
            @blk.scalar
            def _(eng):
                p.wait(eng, "vec")
                call = eng.activation(s_sb[:], s_sb[:], AF.Sqrt)
                p.inc("act", call)
            @blk.vector
            def _(eng):
                p.wait(eng, "act")
                for i in range(nt):
                    o = off(i)
                    call = eng.tensor_tensor(x32[i][:, o:o + TC], x32[i][:, o:o + TC], musb[:], OP.subtract)
                p.seq(eng, "vec", call)
                for i in range(nt):
                    o = off(i)
                    call = eng.tensor_tensor(x32[i][:, o:o + TC], x32[i][:, o:o + TC], s_sb[:], OP.mult)
                p.inc("vec", call)
            @blk.scalar
            def _(eng):
                p.wait(eng, "vec")
                p.wait(eng, "st")
                for i in range(nt):
                    o = off(i)
                    call = eng.activation(outb[i][:], x32[i][:, o:o + TC], AF.Identity,
                                          scale=gcol[:, i:i + 1], bias=bcol[:, i:i + 1])
                p.inc("act", call)
            @blk.gpsimd
            def _(eng):
                p.wait(eng, "act")
                for i in range(nt):
                    p.dma(eng, "st", dst[frange(i, CIN), ci * TC:(ci + 1) * TC], outb[i][:])
    p.pop()


def glu_pass(p, tag, src, dst, HID):
    nc = p.nc
    p.push()
    a_sb = p.sbuf(f"{tag}_a", (128, T), BF16)
    g_sb = p.sbuf(f"{tag}_g", (128, T), BF16)
    e_sb = p.sbuf(f"{tag}_e", (128, T), BF16)
    m_sb = p.sbuf(f"{tag}_m", (128, T), BF16)
    with nc.Block() as blk:
        @blk.gpsimd
        def _(eng):
            p.wait(eng, "st")
    for j in range(ftiles(HID)):
        rows = fsz(j, HID)
        with nc.Block() as blk:
            @blk.sync
            def _(eng):
                p.dma(eng, "ld", a_sb[0:rows, :], src[j * 128:j * 128 + rows, :])
                p.dma(eng, "ld", g_sb[0:rows, :], src[HID + j * 128:HID + j * 128 + rows, :])
            @blk.scalar
            def _(eng):
                p.wait(eng, "ld")
                call = eng.activation(e_sb[0:rows, :], g_sb[0:rows, :], AF.Gelu)
                p.inc("act", call)
            @blk.vector
            def _(eng):
                p.wait(eng, "act")
                if j > 0:
                    p.wait(eng, "st")
                call = eng.tensor_tensor(m_sb[0:rows, :], a_sb[0:rows, :], e_sb[0:rows, :], OP.mult)
                p.inc("vec", call)
            @blk.gpsimd
            def _(eng):
                p.wait(eng, "vec")
                p.dma(eng, "st", dst[j * 128:j * 128 + rows, :], m_sb[0:rows, :])
    p.pop()


NPAIR = ftiles(L_HID)


def w1glu_pass(p, tag, src, wdram, b1dram, dst):
    nc = p.nc
    p.push()
    nct = ftiles(D)
    FOUT = 2 * L_HID
    wsb = [p.sbuf(f"{tag}_w{i}", (128, FOUT), BF16) for i in range(nct)]
    xsb = [[p.sbuf(f"{tag}_x{b}_{i}", (128, TC), BF16) for i in range(nct)] for b in range(2)]
    pb = [p.psum(f"{tag}_pb{i}", (128, TC), F32) for i in range(4)]
    asb = [p.sbuf(f"{tag}_a{i}", (128, TC), F32) for i in range(2)]
    gsb = [p.sbuf(f"{tag}_g{i}", (128, TC), F32) for i in range(2)]
    msb = [p.sbuf(f"{tag}_m{t}", (128, TC), BF16) for t in range(NPAIR)]
    bsb = p.sbuf(f"{tag}_b", (128, 2 * NPAIR), F32)
    lxmark = [0] * NCHK
    gldmark0 = [0]

    def emit_xload(eng, ci):
        sname = f"lx{ci % 2}"
        for i in range(nct):
            p.dma(eng, sname, xsb[ci % 2][i][:, :], src[frange(i, D), ci * TC:(ci + 1) * TC])
        return p.n(sname)

    def pw(t):
        return 128 if t < NPAIR - 1 else L_HID - 128 * (NPAIR - 1)

    with nc.Block() as blk:
        @blk.gpsimd
        def _(eng):
            p.wait(eng, "st")
            for i in range(nct):
                p.dma(eng, "gld", wsb[i][:, :], wdram[frange(i, D), :])
            for t in range(NPAIR):
                w = pw(t)
                p.dma(eng, "gld", bsb[0:w, 2 * t:2 * t + 1], b1dram[256 * t:256 * t + w, :])
                p.dma(eng, "gld", bsb[0:w, 2 * t + 1:2 * t + 2], b1dram[256 * t + w:256 * t + 2 * w, :])
            gldmark0[0] = p.n("gld")
        @blk.sync
        def _(eng):
            p.wait(eng, "st")
            lxmark[0] = emit_xload(eng, 0)

    mm_mark = [0] * NCHK
    with nc.Block() as blk:
        for ci in range(NCHK):
            @blk.sync
            def _(eng):
                if ci + 1 < NCHK:
                    if ci >= 1:
                        p.wait(eng, "mm", mm_mark[ci - 1])
                    lxmark[ci + 1] = emit_xload(eng, ci + 1)
            @blk.tensor
            def _(eng):
                p.wait(eng, f"lx{ci % 2}", lxmark[ci])
                if ci == 0:
                    p.wait(eng, "gld", gldmark0[0])
                evb = p.n("evac")
                for t in range(NPAIR):
                    w = pw(t)
                    if evb + (t - 1) > 0:
                        eng.wait_ge(p.sem["evac"][0], evb + (t - 1))
                    for c in range(nct):
                        eng.matmul(pb[t % 2][0:w, :], wsb[c][:, 256 * t:256 * t + w],
                                   xsb[ci % 2][c][:, :], start=(c == 0), stop=(c == nct - 1))
                    for c in range(nct):
                        call = eng.matmul(pb[2 + t % 2][0:w, :], wsb[c][:, 256 * t + w:256 * t + 2 * w],
                                          xsb[ci % 2][c][:, :], start=(c == 0), stop=(c == nct - 1))
                    p.inc("mm", call)
                mm_mark[ci] = p.n("mm")
            @blk.scalar
            def _(eng):
                mmb = p.n("mm") - NPAIR
                vcb = p.n("vec")
                for t in range(NPAIR):
                    eng.wait_ge(p.sem["mm"][0], mmb + t + 1)
                    if vcb + (t - 1) > 0:
                        eng.wait_ge(p.sem["vec"][0], vcb + (t - 1))
                    w = pw(t)
                    eng.activation(asb[t % 2][0:w, :], pb[t % 2][0:w, :], AF.Identity,
                                   bias=bsb[0:w, 2 * t:2 * t + 1])
                    call = eng.activation(gsb[t % 2][0:w, :], pb[2 + t % 2][0:w, :], AF.Gelu,
                                          bias=bsb[0:w, 2 * t + 1:2 * t + 2])
                    p.inc("evac", call)
            @blk.vector
            def _(eng):
                p.wait(eng, "st")
                evb = p.n("evac") - NPAIR
                for t in range(NPAIR):
                    eng.wait_ge(p.sem["evac"][0], evb + t + 1)
                    w = pw(t)
                    call = eng.tensor_tensor(msb[t][0:w, :], asb[t % 2][0:w, :],
                                             gsb[t % 2][0:w, :], OP.mult)
                    p.inc("vec", call)
            @blk.gpsimd
            def _(eng):
                vcb = p.n("vec") - NPAIR
                for t in range(NPAIR):
                    eng.wait_ge(p.sem["vec"][0], vcb + t + 1)
                    w = pw(t)
                    p.dma(eng, "st", dst[128 * t:128 * t + w, ci * TC:(ci + 1) * TC], msb[t][0:w, :])
    p.pop()


def lffn_attn_pass(p, s, tag=None):
    tag = tag or f"lat{s}"
    nc = p.nc
    p.push()
    qkvtok = p.dram[f"qkvtok{s}"]
    oT = p.dram[f"oT{s}"]
    qT = [p.sbuf(f"{tag}_qT{i}", (128, T), BF16) for i in range(4)]
    kT = [p.sbuf(f"{tag}_kT{i}", (128, T), BF16) for i in range(4)]
    kTp = [p.sbuf(f"{tag}_kTp{i}", (128, T), BF16) for i in range(4)]
    vaug = p.sbuf(f"{tag}_va", (128, L_HEADS * 32 * 65), BF16)
    rtc = p.sbuf(f"{tag}_rtc", (128, 32 * 128), BF16)
    rts = p.sbuf(f"{tag}_rts", (128, 32 * 128), BF16)
    identb = p.sbuf(f"{tag}_idb", (128, 128), BF16)
    keepj = p.sbuf(f"{tag}_kj", (128, 128), BF16)
    keepi = p.sbuf(f"{tag}_ki", (128, 128), BF16)
    ksp = p.sbuf(f"{tag}_ksp", (128, 1), F32)
    b2c = p.sbuf(f"{tag}_b2c", (128, 1), F32)
    qksbc = p.sbuf(f"{tag}_qksbc", (128, 128), F32)
    sq8c = p.sbuf(f"{tag}_sq8c", (128, 1), F32)
    qsr = p.sbuf(f"{tag}_qsr", (1, 64), F32)
    ksr = p.sbuf(f"{tag}_ksr", (1, 64), F32)
    negm = p.sbuf(f"{tag}_negm", (128, 1), F32)
    ones_row = p.sbuf(f"{tag}_or", (1, 128), F32)
    ones_row64 = p.sbuf(f"{tag}_or64", (1, 64), F32)
    tq = [p.sbuf(f"{tag}_tq{b}", (128, 1024), BF16) for b in range(2)]
    sqf = [p.sbuf(f"{tag}_sqf{b}", (128, 1024), BF16) for b in range(2)]
    qh = [p.sbuf(f"{tag}_qh{b}", (128, 1024), BF16) for b in range(2)]
    t1 = [p.sbuf(f"{tag}_t1{b}", (128, 1024), BF16) for b in range(2)]
    t2 = [p.sbuf(f"{tag}_t2{b}", (128, 1024), BF16) for b in range(2)]
    qr = [p.sbuf(f"{tag}_qr{b}", (128, 1024), BF16) for b in range(2)]
    ssqall = p.sbuf(f"{tag}_ssq", (128, 512), F32)
    invall = p.sbuf(f"{tag}_inv", (128, 512), F32)
    NS = 16
    pslot = p.sbuf(f"{tag}_P", (128, NS * 512), BF16)
    bcsb = [p.sbuf(f"{tag}_bc{b}", (64, 512), F32) for b in range(2)]
    risb = [p.sbuf(f"{tag}_ri{b}", (1, 512), F32) for b in range(2)]
    ob = [p.sbuf(f"{tag}_ob{b}", (64, 512), BF16) for b in range(2)]
    psS = p.psum(f"{tag}_psS", (128, 2048), F32)
    psT = [p.psum(f"{tag}_psT0", (128, 512), BF16)] * 2
    psO = [p.psum(f"{tag}_psO{b}", (65, 512), F32) for b in range(2)]
    psB2 = p.psum(f"{tag}_psB2", (64, 512), F32)

    psS_free = [0] * 4
    psT_free = [0]
    pslot_free = [0] * NS
    psO_free = [0] * 2
    risb_free = [0] * 2
    ob_free = [0] * 2
    psB2_free = [0]

    with nc.Block() as blk:
        st0 = p.n("st")
        @blk.gpsimd
        def _(eng):
            p.wait(eng, "st", st0)
            p.dma(eng, "gld", identb[:], p.dram["ident"][:, :])
            p.dma(eng, "gld", keepj[:], p.dram["keep_jle"][:, :])
            p.dma(eng, "gld", keepi[:], p.dram["keep_ile"][:, :])
            for it in range(32):
                p.dma(eng, "gld", rtc[:, it * 128:it * 128 + 64],
                      p.dram["rot_qc"][it * 128:(it + 1) * 128, :])
                p.dma(eng, "gld", rtc[:, it * 128 + 64:it * 128 + 128],
                      p.dram["rot_kc"][it * 128:(it + 1) * 128, :])
                p.dma(eng, "gld", rts[:, it * 128:it * 128 + 64],
                      p.dram["rot_qs"][it * 128:(it + 1) * 128, :])
                p.dma(eng, "gld", rts[:, it * 128 + 64:it * 128 + 128],
                      p.dram["rot_ks"][it * 128:(it + 1) * 128, :])
        gld0 = p.n("gld")
        @blk.sync
        def _(eng):
            p.wait(eng, "st", st0)
            p.dma(eng, "ld", b2c[:], p.dram["base2_col"][:, :])
            p.dma(eng, "ld", qsr[:], p.dram[f"ffn_qs_row{s}"][:, :])
            p.dma(eng, "ld", ksr[:], p.dram[f"ffn_ks_row{s}"][:, :])
        ld_small = p.n("ld")
        @blk.sync
        def _(eng):
            for h in range(L_HEADS):
                for jt in range(32):
                    p.dma(eng, "ld", vaug[:, h * 2080 + jt * 65:h * 2080 + jt * 65 + 64],
                          qkvtok[jt * 128:(jt + 1) * 128, 1024 + h * 64:1024 + (h + 1) * 64])
        ld_vaug = p.n("ld")
        @blk.vector
        def _(eng):
            eng.memset(ones_row[:], 1.0)
            eng.memset(ones_row64[:], 1.0)
            for h in range(L_HEADS):
                ap1 = bass.AP(vaug[:].tensor, h * 2080 + 64, [[L_HEADS * 2080, 128], [65, 32]])
                eng.memset(ap1, 1.0)
            eng.memset(sq8c[:], SQ8)
            p.wait(eng, "ld", ld_small)
            p.seq(eng, "vec", eng.tensor_scalar(ksp[:], b2c[:], SQ8, None, OP.mult))
            p.seq(eng, "vec", eng.tensor_tensor(qh[0][0:1, 0:64], qsr[:], ksr[:], OP.mult))
            p.seq(eng, "vec", eng.reduce_max(risb[0][0:1, 0:1], qh[0][0:1, 0:64], axis=AXX))
            call = eng.tensor_scalar(risb[0][0:1, 0:1], risb[0][0:1, 0:1], -8.0, None, OP.mult)
            p.inc("vec", call)
        vsetup = p.n("vec")
        @blk.tensor
        def _(eng):
            p.wait(eng, "vec", vsetup)
            eng.matmul(psS[:, 0:1], ones_row[:], risb[0][0:1, 0:1], start=True, stop=True)
            eng.matmul(psS[:, 512:576], ones_row[:], qsr[:], start=True, stop=True)
            call = eng.matmul(psS[:, 1024:1088], ones_row[:], ksr[:], start=True, stop=True)
            p.inc("mm", call)
        mmsetup = p.n("mm")
        @blk.vector
        def _(eng):
            p.wait(eng, "mm", mmsetup)
            eng.tensor_copy(negm[:], psS[:, 0:1])
            eng.tensor_copy(qksbc[:, 0:64], psS[:, 512:576])
            call = eng.tensor_copy(qksbc[:, 64:128], psS[:, 1024:1088])
            p.inc("vec", call)
        vsetup2 = p.n("vec")

        phases = getattr(p, 'lffn_phases', ("p1", "p2", "p3", "attn"))
        tq_ld = [0, 0]
        tq_free = [0, 0]
        for it in range(32 if "p1" in phases else 0):
            itb = it % 2
            @blk.sync
            def _(eng):
                if tq_free[itb]:
                    p.wait(eng, "vec", tq_free[itb])
                p.dma(eng, "ld", tq[itb][:], qkvtok[it * 128:(it + 1) * 128, 0:1024])
            tq_ld[itb] = p.n("ld")
            @blk.vector
            def _(eng):
                p.wait(eng, "ld", tq_ld[itb])
                p.seq(eng, "vec", eng.tensor_tensor(sqf[itb][:], tq[itb][:], tq[itb][:], OP.mult))
                call = eng.tensor_reduce(ssqall[:, it * 16:(it + 1) * 16],
                                         sqf[itb][:].rearrange("p (hs q) -> p hs q", hs=16),
                                         axis=AXX, op=OP.add)
                p.inc("vec", call)
            tq_free[itb] = p.n("vec")
        p1_vec = p.n("vec")

        if "p2" in phases:
            @blk.scalar
            def _(eng):
                p.wait(eng, "vec", p1_vec)
                call = eng.activation(ssqall[:], ssqall[:], AF.Sqrt)
                p.inc("act", call)
            p2_act = p.n("act")
            @blk.vector
            def _(eng):
                p.wait(eng, "act", p2_act)
                p.seq(eng, "vec", eng.tensor_scalar(ssqall[:], ssqall[:], 1e-12, None, OP.max))
                call = eng.reciprocal(invall[:], ssqall[:])
                p.inc("vec", call)
        p2_vec = p.n("vec")

        qr_free = [0, 0]
        for it in range(32 if "p3" in phases else 0):
            itb = it % 2
            @blk.sync
            def _(eng):
                if tq_free[itb]:
                    p.wait(eng, "vec", tq_free[itb])
                p.dma(eng, "ld", tq[itb][:], qkvtok[it * 128:(it + 1) * 128, 0:1024])
            tq_ld[itb] = p.n("ld")
            if "p3t" in phases:
                pass
            else:
              @blk.vector
              def _(eng):
                p.wait(eng, "ld", tq_ld[itb])
                if it == 0:
                    p.wait(eng, "vec", p2_vec)
                    p.wait(eng, "gld", gld0)
                p.seq(eng, "vec", eng.tensor_tensor(
                    qh[itb][:].rearrange("p (hs q) -> p hs q", hs=16),
                    tq[itb][:].rearrange("p (hs q) -> p hs q", hs=16),
                    invall[:, it * 16:(it + 1) * 16].unsqueeze(-1).broadcast_to((128, 16, 64)),
                    OP.mult))
                tq_free[itb] = p.n("vec")
                p.seq(eng, "vec", eng.tensor_tensor(
                    qh[itb][:].rearrange("p (s h q) -> p s h q", s=2, h=8),
                    qh[itb][:].rearrange("p (s h q) -> p s h q", s=2, h=8),
                    qksbc[:].rearrange("p (s q) -> p s q", s=2).unsqueeze(2).broadcast_to((128, 2, 8, 64)),
                    OP.mult))
                p.seq(eng, "vec", eng.tensor_tensor(
                    t1[itb][:].rearrange("p (s h q) -> p s h q", s=2, h=8),
                    qh[itb][:].rearrange("p (s h q) -> p s h q", s=2, h=8),
                    rtc[:, it * 128:(it + 1) * 128].rearrange("p (s q) -> p s q", s=2)
                       .unsqueeze(2).broadcast_to((128, 2, 8, 64)),
                    OP.mult))
                for side in range(2):
                    sl = slice(side * 512, (side + 1) * 512)
                    qh4 = qh[itb][:, sl].rearrange("p (h two q) -> p h two q", h=8, two=2)
                    t24 = t2[itb][:, sl].rearrange("p (h two q) -> p h two q", h=8, two=2)
                    ss_t = rts[:, it * 128 + side * 64:it * 128 + (side + 1) * 64]
                    p.seq(eng, "vec", eng.tensor_tensor(
                        t24[:, :, 0, :], qh4[:, :, 1, :],
                        ss_t[:, 0:32].unsqueeze(1).broadcast_to((128, 8, 32)), OP.mult))
                    p.seq(eng, "vec", eng.tensor_tensor(
                        t24[:, :, 1, :], qh4[:, :, 0, :],
                        ss_t[:, 32:64].unsqueeze(1).broadcast_to((128, 8, 32)), OP.mult))
                if qr_free[itb]:
                    p.wait(eng, "mm", qr_free[itb])
                call = eng.tensor_tensor(qr[itb][:], t1[itb][:], t2[itb][:], OP.add)
                p.inc("vec", call)
            qr_mark = p.n("vec")
            tr_mark = 0
            for side in range(2 if "p3v" not in phases else 0):
                slot = side
                @blk.tensor
                def _(eng):
                    p.wait(eng, "vec", qr_mark)
                    if psT_free[0]:
                        p.wait(eng, "act", psT_free[0])
                    for tt in range(4):
                        call = eng.transpose(
                            psT[slot][:, tt * 128:(tt + 1) * 128],
                            qr[itb][:, side * 512 + tt * 128:side * 512 + (tt + 1) * 128],
                            identb[:])
                    p.inc("mm", call)
                tr_mark = p.n("mm")
                @blk.scalar
                def _(eng):
                    if "p3ne" in phases:
                        return
                    p.wait(eng, "mm", tr_mark)
                    dstt = qT if side == 0 else kT
                    for tt in range(4):
                        call = eng.activation(
                            dstt[tt][:, it * 128:(it + 1) * 128],
                            psT[slot][:, tt * 128:(tt + 1) * 128],
                            AF.Identity, scale=sq8c[:, 0:1])
                        p.inc("act", call)
                    if side == 1:
                        for tt in range(4):
                            call = eng.activation(
                                kTp[tt][:, it * 128:(it + 1) * 128],
                                psT[slot][:, tt * 128:(tt + 1) * 128],
                                AF.Identity, scale=ksp[:, 0:1])
                            p.inc("act", call)
                psT_free[0] = p.n("act")
            qr_free[itb] = tr_mark
        prep_vec = p.n("vec")
        prep_act2 = p.n("act")

        pend = []

        def emit_recip(eng, pd):
            rb = pd["u"] % 2
            p.wait(eng, "mm", max(pd["pv"], risb_free[rb]))
            call = eng.reciprocal(risb[rb][:], psO[pd["o"]][64:65, :])
            p.inc("vec", call)
            pd["rb"] = rb
            pd["recip"] = p.n("vec")

        def emit_bcast(eng, pd):
            p.wait(eng, "vec", max(pd["recip"], psB2_free[0]))
            call = eng.matmul(psB2[:], ones_row64[0:1, 0:64], risb[pd["rb"]][:],
                              start=True, stop=True)
            p.inc("mm", call)
            pd["bcast"] = p.n("mm")
            risb_free[pd["rb"]] = p.n("mm")

        def emit_tail_vec(eng, pd):
            pb = pd["u"] % 2
            p.wait(eng, "mm", pd["bcast"])
            call = eng.tensor_copy(bcsb[pb][:], psB2[:])
            p.inc("vec", call)
            psB2_free[0] = p.n("vec")
            if ob_free[pb]:
                p.wait(eng, "st", ob_free[pb])
            call = eng.tensor_tensor(ob[pb][:], psO[pd["o"]][0:64, :], bcsb[pb][:], OP.mult)
            p.inc("vec", call)
            psO_free[pd["o"]] = p.n("vec")
            pd["obm"] = p.n("vec")

        def emit_tail_store(eng, pd):
            pb = pd["u"] % 2
            p.wait(eng, "vec", pd["obm"])
            p.dma(eng, "st", oT[pd["h"] * 64:(pd["h"] + 1) * 64,
                                pd["w"] * 512:(pd["w"] + 1) * 512], ob[pb][:])
            ob_free[pb] = p.n("st")

        def emit_av(eng, av):
            groups, u, h, o = av["groups"], av["u"], av["h"], av["o"]
            gbase = (u % 2) * 8
            ngr = len(groups)
            for gi, (kind, jt, jc) in enumerate(groups):
                g = gbase + gi
                ilo = jt * 128 if kind == "own" else 0
                ihi = 512 if kind == "own" else (jt + 1) * 128
                vslot = vaug[:, h * 2080 + (jc // 128) * 65:h * 2080 + (jc // 128) * 65 + 65]
                vw = av["mask_mark"][gi]
                if gi == 0 and psO_free[o]:
                    vw = max(vw, psO_free[o])
                p.wait(eng, "vec", vw)
                call = eng.matmul(psO[o][:, ilo:ihi], vslot,
                                  pslot[:, g * 512 + ilo:g * 512 + ihi],
                                  start=(gi == 0), stop=(gi == ngr - 1))
                p.inc("mm", call)
                pslot_free[g] = p.n("mm")
            av["pv"] = p.n("mm")
            pend.append(dict(h=av["h"], w=av["w"], u=u, o=o, pv=av["pv"]))

        avq = []
        for h in range(L_HEADS if "attn" in phases else 0):
            pt = h // 2
            pr = 64 * (h % 2)
            for w in range(8):
                u = h * 8 + w
                groups = []
                for jt in range(4):
                    groups.append(("own", jt, w * 512 + jt * 128))
                if w > 0:
                    for jt in range(4):
                        groups.append(("prev", jt, (w - 1) * 512 + jt * 128))
                groups.sort(key=lambda g: 0 if (g[0] == "own" and g[1] == 0) else 1)
                ngr = len(groups)
                gbase = (u % 2) * 8
                qk_mark = [0] * ngr
                exp_mark = [0] * ngr
                mask_mark = [0] * ngr
                o = u % 2
                @blk.tensor
                def _(eng):
                    if pend and "recip" in pend[0] and "bcast" not in pend[0]:
                        emit_bcast(eng, pend[0])
                    a0 = p.n("act")
                    for gi, (kind, jt, jc) in enumerate(groups):
                        if gi == 4 and avq:
                            emit_av(eng, avq.pop(0))
                        src_t = kT if kind == "own" else kTp
                        ilo = jt * 128 if kind == "own" else 0
                        ihi = 512 if kind == "own" else (jt + 1) * 128
                        slot = gi % 4
                        if u == 0 and gi == 0:
                            p.wait(eng, "ld", ld_vaug)
                            p.wait(eng, "vec", prep_vec)
                            p.wait(eng, "act", prep_act2)
                        if gi >= 4:
                            p.wait(eng, "act", a0 + gi - 3)
                        elif psS_free[slot]:
                            p.wait(eng, "act", psS_free[slot])
                        call = eng.matmul(psS[:, slot * 512 + ilo:slot * 512 + ihi],
                                          src_t[pt][pr:pr + 64, jc:jc + 128],
                                          qT[pt][pr:pr + 64, w * 512 + ilo:w * 512 + ihi],
                                          start=True, stop=True)
                        p.inc("mm", call)
                        qk_mark[gi] = p.n("mm")
                    if avq:
                        emit_av(eng, avq.pop(0))
                @blk.scalar
                def _(eng):
                    for gi, (kind, jt, jc) in enumerate(groups):
                        g = gbase + gi
                        ilo = jt * 128 if kind == "own" else 0
                        ihi = 512 if kind == "own" else (jt + 1) * 128
                        p.wait(eng, "mm", max(qk_mark[gi], pslot_free[g]))
                        call = eng.activation(pslot[:, g * 512 + ilo:g * 512 + ihi],
                                              psS[:, (gi % 4) * 512 + ilo:(gi % 4) * 512 + ihi],
                                              AF.Exp, bias=negm[:, 0:1])
                        p.inc("act", call)
                        exp_mark[gi] = p.n("act")
                        psS_free[gi % 4] = p.n("act")
                @blk.vector
                def _(eng):
                    for gi, (kind, jt, jc) in enumerate(groups):
                        g = gbase + gi
                        blo = jt * 128
                        msk = keepj if kind == "own" else keepi
                        p.wait(eng, "act", exp_mark[gi])
                        call = eng.tensor_tensor(pslot[:, g * 512 + blo:g * 512 + blo + 128],
                                                 pslot[:, g * 512 + blo:g * 512 + blo + 128],
                                                 msk[:], OP.mult)
                        p.inc("vec", call)
                        mask_mark[gi] = p.n("vec")
                    if pend and "recip" not in pend[-1]:
                        emit_recip(eng, pend[-1])
                    if pend and "bcast" in pend[0]:
                        emit_tail_vec(eng, pend[0])
                avq.append(dict(h=h, w=w, u=u, o=o, groups=groups, mask_mark=mask_mark))
                @blk.gpsimd
                def _(eng):
                    if pend and "obm" in pend[0]:
                        emit_tail_store(eng, pend.pop(0))
        if avq:
            @blk.tensor
            def _(eng):
                emit_av(eng, avq.pop(0))
        while pend:
            @blk.vector
            def _(eng):
                if "recip" not in pend[0]:
                    emit_recip(eng, pend[0])
            @blk.tensor
            def _(eng):
                if "bcast" not in pend[0]:
                    emit_bcast(eng, pend[0])
            @blk.vector
            def _(eng):
                emit_tail_vec(eng, pend[0])
            @blk.gpsimd
            def _(eng):
                emit_tail_store(eng, pend.pop(0))
    p.pop()
    return




def lffn_attn_pass_old(p, s, tag=None):
    tag = tag or f"lat{s}"
    nc = p.nc
    p.push()
    qkvtok = p.dram[f"qkvtok{s}"]
    oT = p.dram[f"oT{s}"]
    qT = [p.sbuf(f"{tag}_qT{i}", (128, T), BF16) for i in range(4)]
    kT = [p.sbuf(f"{tag}_kT{i}", (128, T), BF16) for i in range(4)]
    kTp = [p.sbuf(f"{tag}_kTp{i}", (128, T), BF16) for i in range(4)]
    vaug = p.sbuf(f"{tag}_va", (128, L_HEADS * 32 * 65), BF16)
    rt = {nm: p.sbuf(f"{tag}_rt{nm}", (128, 32 * 64), BF16) for nm in ["qc", "qs", "kc", "ks"]}
    identb = p.sbuf(f"{tag}_idb", (128, 128), BF16)
    keepj = p.sbuf(f"{tag}_kj", (128, 128), BF16)
    keepi = p.sbuf(f"{tag}_ki", (128, 128), BF16)
    qs8 = p.sbuf(f"{tag}_qs8", (128, 1), F32)
    ks8 = p.sbuf(f"{tag}_ks8", (128, 1), F32)
    ksp = p.sbuf(f"{tag}_ksp", (128, 1), F32)
    b2c = p.sbuf(f"{tag}_b2c", (128, 1), F32)
    qsbc = p.sbuf(f"{tag}_qsbc", (128, 64), F32)
    ksbc = p.sbuf(f"{tag}_ksbc", (128, 64), F32)
    sq8c = p.sbuf(f"{tag}_sq8c", (128, 1), F32)
    qsr = p.sbuf(f"{tag}_qsr", (1, 64), F32)
    ksr = p.sbuf(f"{tag}_ksr", (1, 64), F32)
    negm = p.sbuf(f"{tag}_negm", (128, 1), F32)
    ones_row = p.sbuf(f"{tag}_or", (1, 128), F32)
    ones_row64 = p.sbuf(f"{tag}_or64", (1, 64), F32)
    tq = p.sbuf(f"{tag}_tq", (128, 1024), BF16)
    sqt = p.sbuf(f"{tag}_sq", (128, 512), BF16)
    ssq = p.sbuf(f"{tag}_ssq", (128, 8), F32)
    inv = p.sbuf(f"{tag}_inv", (128, 8), F32)
    qh = p.sbuf(f"{tag}_qh", (128, 512), BF16)
    t1 = p.sbuf(f"{tag}_t1", (128, 512), BF16)
    t2 = p.sbuf(f"{tag}_t2", (128, 512), BF16)
    qr = p.sbuf(f"{tag}_qr", (128, 512), BF16)
    pslot = p.sbuf(f"{tag}_P", (128, 8 * 512), BF16)
    bcsb = p.sbuf(f"{tag}_bc", (64, 512), F32)
    risb = p.sbuf(f"{tag}_ri", (1, 512), F32)
    ob = p.sbuf(f"{tag}_ob", (64, 512), BF16)
    psS = p.psum(f"{tag}_psS", (128, 2048), F32)
    psO = p.psum(f"{tag}_psO", (65, 512), F32)
    psB2 = p.psum(f"{tag}_psB2", (64, 512), F32)
    psT = p.psum(f"{tag}_psT", (128, 512), BF16)

    with nc.Block() as blk:
        @blk.gpsimd
        def _(eng):
            p.wait(eng, "st")
            p.dma(eng, "gld", identb[:], p.dram["ident"][:, :])
            p.dma(eng, "gld", keepj[:], p.dram["keep_jle"][:, :])
            p.dma(eng, "gld", keepi[:], p.dram["keep_ile"][:, :])
            for nm, key in [("qc", "rot_qc"), ("qs", "rot_qs"), ("kc", "rot_kc"), ("ks", "rot_ks")]:
                for it in range(32):
                    p.dma(eng, "gld", rt[nm][:, it * 64:(it + 1) * 64],
                          p.dram[key][it * 128:(it + 1) * 128, :])
            p.wait(eng, "gld")
        @blk.sync
        def _(eng):
            p.dma(eng, "ld", qs8[:], p.dram[f"ffn_qs_col{s}"][:, :])
            p.dma(eng, "ld", ks8[:], p.dram[f"ffn_ks_col{s}"][:, :])
            p.dma(eng, "ld", b2c[:], p.dram["base2_col"][:, :])
            p.dma(eng, "ld", qsr[:], p.dram[f"ffn_qs_row{s}"][:, :])
            p.dma(eng, "ld", ksr[:], p.dram[f"ffn_ks_row{s}"][:, :])
            for h in range(L_HEADS):
                for jt in range(32):
                    p.dma(eng, "ld", vaug[:, h * 2080 + jt * 65:h * 2080 + jt * 65 + 64],
                          qkvtok[jt * 128:(jt + 1) * 128, 1024 + h * 64:1024 + (h + 1) * 64])
            p.wait(eng, "ld")
        @blk.vector
        def _(eng):
            eng.memset(ones_row[:], 1.0)
            call = eng.memset(ones_row64[:], 1.0)
            p.inc("vec", call)
    with nc.Block() as blk:
        @blk.vector
        def _(eng):
            for h in range(L_HEADS):
                ap1 = bass.AP(vaug[:].tensor, h * 2080 + 64, [[L_HEADS * 2080, 128], [65, 32]])
                call = eng.memset(ap1, 1.0)
            p.inc("vec", call)
            p.seq(eng, "vec", eng.memset(sq8c[:], SQ8))
            p.seq(eng, "vec", eng.tensor_scalar(ksp[:], b2c[:], SQ8, None, OP.mult))
            p.seq(eng, "vec", eng.tensor_tensor(qh[0:1, 0:64], qsr[:], ksr[:], OP.mult))
            p.seq(eng, "vec", eng.reduce_max(risb[0:1, 0:1], qh[0:1, 0:64], axis=AXX))
            call = eng.tensor_scalar(risb[0:1, 0:1], risb[0:1, 0:1], -8.0, None, OP.mult)
            p.inc("vec", call)
        @blk.tensor
        def _(eng):
            p.wait(eng, "vec")
            eng.matmul(psS[:, 0:1], ones_row[:], risb[0:1, 0:1], start=True, stop=True)
            eng.matmul(psS[:, 512:576], ones_row[:], qsr[:], start=True, stop=True)
            call = eng.matmul(psS[:, 1024:1088], ones_row[:], ksr[:], start=True, stop=True)
            p.inc("mm", call)
        @blk.vector
        def _(eng):
            p.wait(eng, "mm")
            eng.tensor_copy(negm[:], psS[:, 0:1])
            eng.tensor_copy(qsbc[:], psS[:, 512:576])
            call = eng.tensor_copy(ksbc[:], psS[:, 1024:1088])
            p.inc("vec", call)

    for it in range(32):
        with nc.Block() as blk:
            @blk.sync
            def _(eng):
                p.dma(eng, "ld", tq[:], qkvtok[it * 128:(it + 1) * 128, 0:1024])
            for side in range(2):
                sl = slice(side * 512, (side + 1) * 512)
                @blk.scalar
                def _(eng, sl=sl):
                    p.wait(eng, "ld")
                    p.wait(eng, "vec")
                    call = eng.activation(sqt[:], tq[:, sl], AF.Square)
                    p.inc("act", call)
                @blk.vector
                def _(eng, sl=sl):
                    p.wait(eng, "act")
                    call = eng.tensor_reduce(ssq[:], sqt[:].rearrange("p (h q) -> p h q", h=8),
                                             axis=AXX, op=OP.add)
                    p.inc("vec", call)
                @blk.scalar
                def _(eng):
                    p.wait(eng, "vec")
                    call = eng.activation(ssq[:], ssq[:], AF.Sqrt)
                    p.inc("act", call)
                @blk.vector
                def _(eng, sl=sl, side=side, it=it):
                    p.wait(eng, "act")
                    p.seq(eng, "vec", eng.tensor_scalar(ssq[:], ssq[:], 1e-12, None, OP.max))
                    p.seq(eng, "vec", eng.reciprocal(inv[:], ssq[:]))
                    p.seq(eng, "vec", eng.tensor_tensor(
                        qh[:].rearrange("p (h q) -> p h q", h=8),
                        tq[:, sl].rearrange("p (h q) -> p h q", h=8),
                        inv[:].unsqueeze(-1).broadcast_to((128, 8, 64)), OP.mult))
                    p.seq(eng, "vec", eng.tensor_tensor(
                        qh[:].rearrange("p (h q) -> p h q", h=8),
                        qh[:].rearrange("p (h q) -> p h q", h=8),
                        (qsbc if side == 0 else ksbc)[:].unsqueeze(1).broadcast_to((128, 8, 64)), OP.mult))
                    cs_t = rt["qc" if side == 0 else "kc"][:, it * 64:(it + 1) * 64]
                    ss_t = rt["qs" if side == 0 else "ks"][:, it * 64:(it + 1) * 64]
                    p.seq(eng, "vec", eng.tensor_tensor(
                        t1[:].rearrange("p (h q) -> p h q", h=8),
                        qh[:].rearrange("p (h q) -> p h q", h=8),
                        cs_t.unsqueeze(1).broadcast_to((128, 8, 64)), OP.mult))
                    qh4 = qh[:].rearrange("p (h two q) -> p h two q", h=8, two=2)
                    t24 = t2[:].rearrange("p (h two q) -> p h two q", h=8, two=2)
                    p.seq(eng, "vec", eng.tensor_tensor(
                        t24[:, :, 0, :], qh4[:, :, 1, :],
                        ss_t[:, 0:32].unsqueeze(1).broadcast_to((128, 8, 32)), OP.mult))
                    p.seq(eng, "vec", eng.tensor_tensor(
                        t24[:, :, 1, :], qh4[:, :, 0, :],
                        ss_t[:, 32:64].unsqueeze(1).broadcast_to((128, 8, 32)), OP.mult))
                    call = eng.tensor_tensor(qr[:], t1[:], t2[:], OP.add)
                    p.inc("vec", call)
                mmq = p.n("mm")
                @blk.tensor
                def _(eng):
                    p.wait(eng, "vec")
                    p.wait(eng, "act")
                    for tt in range(4):
                        call = eng.transpose(psT[:, tt * 128:(tt + 1) * 128], qr[:, tt * 128:(tt + 1) * 128], identb[:])
                    p.inc("mm", call)
                @blk.scalar
                def _(eng, side=side, it=it, mmq=mmq):
                    p.wait(eng, "mm", mmq + 1)
                    dstt = qT if side == 0 else kT
                    scl = sq8c
                    for tt in range(4):
                        call = eng.activation(dstt[tt][:, it * 128:(it + 1) * 128],
                                              psT[:, tt * 128:(tt + 1) * 128], AF.Identity,
                                              scale=scl[:, 0:1])
                        p.inc("act", call)
                    if side == 1:
                        for tt in range(4):
                            call = eng.activation(kTp[tt][:, it * 128:(it + 1) * 128],
                                                  psT[:, tt * 128:(tt + 1) * 128], AF.Identity,
                                                  scale=ksp[:, 0:1])
                            p.inc("act", call)

    if f"dbg_rt{s}" in p.dram:
        with nc.Block() as blk:
            @blk.gpsimd
            def _(eng):
                p.wait(eng, "gld")
                p.dma(eng, "st", p.dram[f"dbg_rt{s}"][0:128, :], rt["qs"][:])
                p.dma(eng, "st", p.dram[f"dbg_rt{s}"][128:256, :], rt["qc"][:])
    if f"dbg_qT{s}" in p.dram:
        with nc.Block() as blk:
            @blk.gpsimd
            def _(eng):
                p.wait(eng, "act")
                for tt in range(4):
                    p.dma(eng, "st", p.dram[f"dbg_qT{s}"][tt * 128:(tt + 1) * 128, :], qT[tt][:])
                    p.dma(eng, "st", p.dram[f"dbg_kT{s}"][tt * 128:(tt + 1) * 128, :], kT[tt][:])

    for h in range(L_HEADS):
        pt = h // 2
        pr = 64 * (h % 2)
        for w in range(8):
            groups = []
            for jt in range(4):
                groups.append(("own", jt, w * 512 + jt * 128, jt * 128, 512))
            if w > 0:
                for jt in range(4):
                    groups.append(("prev", jt, (w - 1) * 512 + jt * 128, 0, (jt + 1) * 128))
            groups.sort(key=lambda g: 0 if (g[0] == "own" and g[1] == 0) else 1)
            with nc.Block() as blk:
                mmS = p.n("mm")
                @blk.tensor
                def _(eng):
                    p.wait(eng, "act")
                    p.wait(eng, "vec")
                    for gi, (kind, jt, jc, ilo, ihi) in enumerate(groups):
                        src_t = kT if kind == "own" else kTp
                        if gi >= 4:
                            eng.wait_ge(p.sem["act"][0], p.n("act") + gi - 4 + 1)
                        call = eng.matmul(psS[:, (gi % 4) * 512 + ilo:(gi % 4) * 512 + ihi],
                                          src_t[pt][pr:pr + 64, jc:jc + 128],
                                          qT[pt][pr:pr + 64, w * 512 + ilo:w * 512 + ihi],
                                          start=True, stop=True)
                        p.inc("mm", call)
                aS = p.n("act")
                @blk.scalar
                def _(eng):
                    for gi, (kind, jt, jc, ilo, ihi) in enumerate(groups):
                        eng.wait_ge(p.sem["mm"][0], mmS + gi + 1)
                        call = eng.activation(pslot[:, gi * 512 + ilo:gi * 512 + ihi],
                                              psS[:, (gi % 4) * 512 + ilo:(gi % 4) * 512 + ihi],
                                              AF.Exp, bias=negm[:, 0:1])
                        p.inc("act", call)
                vS = p.n("vec")
                @blk.vector
                def _(eng):
                    for gi, (kind, jt, jc, ilo, ihi) in enumerate(groups):
                        eng.wait_ge(p.sem["act"][0], aS + gi + 1)
                        if kind == "own":
                            msk = keepj
                            blo = jt * 128
                        else:
                            msk = keepi
                            blo = jt * 128
                        call = eng.tensor_tensor(pslot[:, gi * 512 + blo:gi * 512 + blo + 128],
                                                 pslot[:, gi * 512 + blo:gi * 512 + blo + 128],
                                                 msk[:], OP.mult)
                        p.inc("vec", call)
                mmPV = p.n("mm")
                @blk.tensor
                def _(eng):
                    for gi, (kind, jt, jc, ilo, ihi) in enumerate(groups):
                        eng.wait_ge(p.sem["vec"][0], vS + gi + 1)
                        vslot = vaug[:, h * 2080 + (jc // 128) * 65:h * 2080 + (jc // 128) * 65 + 65]
                        call = eng.matmul(psO[:, ilo:ihi], vslot,
                                          pslot[:, gi * 512 + ilo:gi * 512 + ihi],
                                          start=(gi == 0), stop=(gi == len(groups) - 1))
                        p.inc("mm", call)
                vN = p.n("vec")
                @blk.vector
                def _(eng):
                    p.wait(eng, "mm")
                    call = eng.reciprocal(risb[:], psO[64:65, :])
                    p.inc("vec", call)
                mmB = p.n("mm")
                @blk.tensor
                def _(eng):
                    p.wait(eng, "vec", vN + 1)
                    call = eng.matmul(psB2[:], ones_row[0:1, 0:64], risb[:], start=True, stop=True)
                    p.inc("mm", call)
                @blk.vector
                def _(eng):
                    p.wait(eng, "mm", mmB + 1)
                    p.seq(eng, "vec", eng.tensor_copy(bcsb[:], psB2[:]))
                    p.wait(eng, "st")
                    call = eng.tensor_tensor(ob[:], psO[0:64, :], bcsb[:], OP.mult)
                    p.inc("vec", call)
                @blk.gpsimd
                def _(eng):
                    p.wait(eng, "vec")
                    p.dma(eng, "st", oT[h * 64:(h + 1) * 64, w * 512:(w + 1) * 512], ob[:])
    p.pop()


def l1_bounds_pass(p, tag="bnd"):
    nc = p.nc
    p.push()
    qT3, kT3, qnm = p.dram["qT3"], p.dram["kT3"], p.dram["qnm"]
    xb = [[p.sbuf(f"{tag}_x{b}_{i}", (128, TC), BF16) for i in range(8)] for b in range(2)]
    sq = [[p.sbuf(f"{tag}_sq{b}_{i}", (128, TC), BF16) for i in range(8)] for b in range(2)]
    ind = p.sbuf(f"{tag}_ind", (128, 8 * 16), BF16)
    rown = p.sbuf(f"{tag}_rown", (16, 1), F32)
    kmax = p.sbuf(f"{tag}_kmax", (16, 1), F32)
    stg = [p.sbuf(f"{tag}_stg{b}", (16, TC), F32) for b in range(2)]
    psR = [p.psum(f"{tag}_psR{b}", (16, TC), F32) for b in range(2)]

    xb_free = [0, 0]
    psR_free = [None, None]
    st0 = p.n("st")
    with nc.Block() as blk:
        @blk.gpsimd
        def _(eng):
            p.wait(eng, "st", st0)
            for i in range(8):
                p.dma(eng, "gld", ind[:, i * 16:(i + 1) * 16],
                      p.dram["head_ind"][i * 128:(i + 1) * 128, :])
        gld0 = p.n("gld")
        @blk.vector
        def _(eng):
            call = eng.memset(kmax[:], 0.0)
            p.inc("vec", call)
        v0 = p.n("vec")

        for phase in range(2):
            src = kT3 if phase == 0 else qT3
            for ci in range(NCHK):
                u = phase * NCHK + ci
                b = u % 2
                cs = slice(ci * TC, (ci + 1) * TC)
                @blk.sync
                def _(eng):
                    if u == 0:
                        p.wait(eng, "st", st0)
                    if xb_free[b]:
                        p.wait(eng, "mm", xb_free[b])
                    for i in range(8):
                        p.dma(eng, "ld", xb[b][i][:], src[frange(i, D), cs])
                ld_mark = p.n("ld")
                @blk.vector
                def _(eng):
                    p.wait(eng, "ld", ld_mark)
                    for i in range(8):
                        call = eng.tensor_tensor(sq[b][i][:], xb[b][i][:], xb[b][i][:], OP.mult)
                    p.inc("vec", call)
                sq_mark = p.n("vec")
                @blk.tensor
                def _(eng):
                    if u == 0:
                        p.wait(eng, "gld", gld0)
                    p.wait(eng, "vec", sq_mark)
                    if psR_free[b] is not None:
                        p.wait(eng, psR_free[b][0], psR_free[b][1])
                    for i in range(8):
                        call = eng.matmul(psR[b][:], ind[:, i * 16:(i + 1) * 16], sq[b][i][:],
                                          start=(i == 0), stop=(i == 7))
                    p.inc("mm", call)
                mm_mark = p.n("mm")
                xb_free[b] = mm_mark
                if phase == 0:
                    @blk.vector
                    def _(eng):
                        p.wait(eng, "mm", mm_mark)
                        if u == 0:
                            p.wait(eng, "vec", v0)
                        p.seq(eng, "vec", eng.reduce_max(rown[:], psR[b][:], axis=AXX))
                        call = eng.tensor_tensor(kmax[:], kmax[:], rown[:], OP.max)
                        p.inc("vec", call)
                    psR_free[b] = ("vec", p.n("vec"))
                else:
                    @blk.scalar
                    def _(eng):
                        p.wait(eng, "mm", mm_mark)
                        if u == NCHK:
                            p.wait(eng, "vec", kscale_mark[0])
                        if u >= NCHK + 2:
                            p.wait(eng, "st", stg_free[b])
                        call = eng.activation(stg[b][:], psR[b][:], AF.Identity,
                                              scale=-0.5, bias=kmax[:, 0:1])
                        p.inc("act", call)
                    psR_free[b] = ("act", p.n("act"))
                    act_mark = p.n("act")
                    @blk.gpsimd
                    def _(eng):
                        p.wait(eng, "act", act_mark)
                        p.dma(eng, "st", qnm[:, cs], stg[b][:])
                    stg_free[b] = p.n("st")
            if phase == 0:
                kscale_mark = [0]
                stg_free = [0, 0]
                @blk.vector
                def _(eng):
                    p.wait(eng, "vec")
                    call = eng.tensor_scalar(kmax[:], kmax[:], -0.5, None, OP.mult)
                    p.inc("vec", call)
                kscale_mark[0] = p.n("vec")
    p.pop()

def l1_attn_pass(p, tag="l1a"):
    nc = p.nc
    p.push()
    qT3, kT3, vtok3, qnm, oT3 = (p.dram[k] for k in ["qT3", "kT3", "vtok3", "qnm", "oT3"])
    kaug = [p.sbuf(f"{tag}_ka{b}", (65, T), BF16) for b in range(2)]
    qaug = [p.sbuf(f"{tag}_qa{b}", (65, T), BF16) for b in range(2)]
    vaug = [p.sbuf(f"{tag}_va{b}", (128, 32 * 65), BF16) for b in range(2)]
    keepj = p.sbuf(f"{tag}_kj", (128, 128), BF16)
    ones_row64 = p.sbuf(f"{tag}_or64", (1, 64), F32)
    NS = 64
    pslot = p.sbuf(f"{tag}_P", (128, NS * 512), BF16)
    gring = [0]
    bcsb = [p.sbuf(f"{tag}_bc{b}", (64, 512), F32) for b in range(2)]
    risb = [p.sbuf(f"{tag}_ri{b}", (1, 512), F32) for b in range(2)]
    ob = [p.sbuf(f"{tag}_ob{b}", (64, 512), BF16) for b in range(2)]
    psS = p.psum(f"{tag}_psS", (128, 2048), F32)
    psO = [p.psum(f"{tag}_psO{b}", (65, 512), F32) for b in range(3)]
    psB2 = p.psum(f"{tag}_psB2", (64, 512), F32)

    psS_free = [0] * 4
    pslot_free = [0] * NS
    psO_free = [0] * 3
    risb_free = [0] * 2
    ob_free = [0] * 2
    psB2_free = [0]
    head_mm_done = [0] * A_HEADS
    ld_mark = [0] * A_HEADS
    gld_mark = [0] * A_HEADS

    with nc.Block() as blk:
        @blk.gpsimd
        def _(eng):
            p.wait(eng, "st")
            p.dma(eng, "gld", keepj[:], p.dram["keep_jle"][:, :])
        @blk.vector
        def _(eng):
            p.wait(eng, "gld")
            eng.memset(ones_row64[:], 1.0)
            for b in range(2):
                eng.memset(kaug[b][64:65, :], 1.0)
                ap1 = bass.AP(vaug[b][:].tensor, 64, [[32 * 65, 128], [65, 32]])
                call = eng.memset(ap1, 1.0)
            p.inc("vec", call)

        setup_vec = p.n("vec")
        st0 = p.n("st")
        pend = []
        sring = [0]

        def emit_recip(eng, pd):
            rb = pd["u"] % 2
            p.wait(eng, "mm", max(pd["pv"], risb_free[rb]))
            call = eng.reciprocal(risb[rb][:], psO[pd["o"]][64:65, :])
            p.inc("vec", call)
            pd["rb"] = rb
            pd["recip"] = p.n("vec")

        def emit_bcast(eng, pd):
            p.wait(eng, "vec", max(pd["recip"], psB2_free[0]))
            call = eng.matmul(psB2[:], ones_row64[0:1, 0:64], risb[pd["rb"]][:],
                              start=True, stop=True)
            p.inc("mm", call)
            pd["bcast"] = p.n("mm")
            risb_free[pd["rb"]] = p.n("mm")

        def emit_tail_vec(eng, pd):
            pb = pd["u"] % 2
            p.wait(eng, "mm", pd["bcast"])
            call = eng.tensor_copy(bcsb[pb][:], psB2[:])
            p.inc("vec", call)
            psB2_free[0] = p.n("vec")
            if ob_free[pb]:
                p.wait(eng, "st", ob_free[pb])
            call = eng.tensor_tensor(ob[pb][:], psO[pd["o"]][0:64, :], bcsb[pb][:], OP.mult)
            p.inc("vec", call)
            psO_free[pd["o"]] = p.n("vec")
            pd["obm"] = p.n("vec")

        def emit_tail_store(eng, pd):
            pb = pd["u"] % 2
            p.wait(eng, "vec", pd["obm"])
            p.dma(eng, "st", oT3[pd["h"] * 64:(pd["h"] + 1) * 64,
                                 pd["ic"] * 512:(pd["ic"] + 1) * 512], ob[pb][:])
            ob_free[pb] = p.n("st")

        def emit_av(eng, av):
            h, ic, t, o = av["h"], av["ic"], av["u"], av["o"]
            hb = h % 2
            ilo0 = ic * 512
            jts, gslots = av["jts"], av["gslots"]
            ngr = len(jts)
            for gi, jt in enumerate(jts):
                g = gslots[gi]
                ilo = max(0, jt * 128 - ilo0)
                p.wait(eng, "act", av["exp_mark"][gi])
                vw = av["mask_mark"].get(gi, 0)
                if gi == 0:
                    vw = max(vw, psO_free[o])
                if vw:
                    p.wait(eng, "vec", vw)
                call = eng.matmul(psO[o][:, ilo:512], vaug[hb][:, jt * 65:(jt + 1) * 65],
                                  pslot[:, g * 512 + ilo:g * 512 + 512],
                                  start=(gi == 0), stop=(gi == ngr - 1))
                p.inc("mm", call)
                pslot_free[g] = p.n("mm")
            head_mm_done[h] = p.n("mm")
            pend.append(dict(h=h, ic=ic, u=t, o=o, pv=p.n("mm")))

        avq = []
        for h in range(A_HEADS):
            hb = h % 2
            @blk.sync
            def _(eng):
                if h == 0:
                    p.wait(eng, "st", st0)
                if h >= 2:
                    p.wait(eng, "mm", head_mm_done[h - 2])
                p.dma(eng, "ld", kaug[hb][0:64, :], kT3[h * 64:(h + 1) * 64, :])
                p.dma(eng, "ld", qaug[hb][0:64, :], qT3[h * 64:(h + 1) * 64, :])
                for jt in range(32):
                    p.dma(eng, "ld", vaug[hb][:, jt * 65:jt * 65 + 64],
                          vtok3[jt * 128:(jt + 1) * 128, h * 64:(h + 1) * 64])
            ld_mark[h] = p.n("ld")
            @blk.gpsimd
            def _(eng):
                if h >= 2:
                    p.wait(eng, "mm", head_mm_done[h - 2])
                p.dma(eng, "gld", qaug[hb][64:65, :], qnm[h:h + 1, :])
            gld_mark[h] = p.n("gld")

            for ic in range(8):
                t = h * 8 + ic
                ilo0 = ic * 512
                jts = list(range(4 * ic + 4))
                ngr = len(jts)
                gslots = [(gring[0] + gi) % NS for gi in range(ngr)]
                gring[0] = (gring[0] + ngr) % NS
                sslots = [(sring[0] + gi) % 4 for gi in range(ngr)]
                sring[0] = (sring[0] + ngr) % 4
                qk_mark = [0] * ngr
                exp_mark = [0] * ngr
                mask_mark = {}
                o = t % 3
                @blk.tensor
                def _(eng):
                    if pend and "recip" in pend[0] and "bcast" not in pend[0]:
                        emit_bcast(eng, pend[0])
                    if ic == 0:
                        p.wait(eng, "ld", ld_mark[h])
                        p.wait(eng, "gld", gld_mark[h])
                    if t == 0:
                        p.wait(eng, "vec", setup_vec)
                    a0 = p.n("act")
                    for gi, jt in enumerate(jts):
                        if gi == 4 and avq:
                            emit_av(eng, avq.pop(0))
                        s = sslots[gi]
                        if gi >= 4:
                            p.wait(eng, "act", a0 + gi - 3)
                        elif psS_free[s]:
                            p.wait(eng, "act", psS_free[s])
                        ilo = max(0, jt * 128 - ilo0)
                        call = eng.matmul(psS[:, s * 512 + ilo:s * 512 + 512],
                                          kaug[hb][:, jt * 128:(jt + 1) * 128],
                                          qaug[hb][:, ilo0 + ilo:ilo0 + 512],
                                          start=True, stop=True)
                        p.inc("mm", call)
                        qk_mark[gi] = p.n("mm")
                    if avq:
                        emit_av(eng, avq.pop(0))
                @blk.scalar
                def _(eng):
                    for gi, jt in enumerate(jts):
                        g = gslots[gi]
                        s = sslots[gi]
                        ilo = max(0, jt * 128 - ilo0)
                        p.wait(eng, "mm", max(qk_mark[gi], pslot_free[g]))
                        call = eng.activation(pslot[:, g * 512 + ilo:g * 512 + 512],
                                              psS[:, s * 512 + ilo:s * 512 + 512],
                                              AF.Exp)
                        p.inc("act", call)
                        exp_mark[gi] = p.n("act")
                        psS_free[s] = p.n("act")
                @blk.vector
                def _(eng):
                    for gi, jt in enumerate(jts):
                        if jt * 128 < ilo0:
                            continue
                        g = gslots[gi]
                        ilo = jt * 128 - ilo0
                        p.wait(eng, "act", exp_mark[gi])
                        call = eng.tensor_tensor(pslot[:, g * 512 + ilo:g * 512 + ilo + 128],
                                                 pslot[:, g * 512 + ilo:g * 512 + ilo + 128],
                                                 keepj[:], OP.mult)
                        p.inc("vec", call)
                        mask_mark[gi] = p.n("vec")
                    if pend and "recip" not in pend[-1]:
                        emit_recip(eng, pend[-1])
                    if pend and "bcast" in pend[0]:
                        emit_tail_vec(eng, pend[0])
                avq.append(dict(h=h, ic=ic, u=t, o=o, jts=jts, gslots=gslots,
                                exp_mark=exp_mark, mask_mark=mask_mark))
                @blk.gpsimd
                def _(eng):
                    if pend and "obm" in pend[0]:
                        emit_tail_store(eng, pend.pop(0))
        if avq:
            @blk.tensor
            def _(eng):
                emit_av(eng, avq.pop(0))
        while pend:
            @blk.vector
            def _(eng):
                if "recip" not in pend[0]:
                    emit_recip(eng, pend[0])
            @blk.tensor
            def _(eng):
                if "bcast" not in pend[0]:
                    emit_bcast(eng, pend[0])
            @blk.vector
            def _(eng):
                emit_tail_vec(eng, pend[0])
            @blk.gpsimd
            def _(eng):
                emit_tail_store(eng, pend.pop(0))
    p.pop()


def dequant_pass(p, tag, qdram, sdram, dst, CIN, FOUT):
    nc = p.nc
    p.push()
    qi = p.sbuf(f"{tag}_qi", (128, TC), I32)
    tf = p.sbuf(f"{tag}_tf", (128, TC), F32)
    srow = p.sbuf(f"{tag}_sr", (1, TC), F32)
    wout = p.sbuf(f"{tag}_wo", (128, TC), BF16)
    n128 = p.sbuf(f"{tag}_n128", (128, 1), F32)
    ones_row = p.sbuf(f"{tag}_or", (1, 128), F32)
    psB = p.psum(f"{tag}_psB", (128, TC), F32)
    with nc.Block() as blk:
        @blk.gpsimd
        def _(eng):
            p.wait(eng, "st")
        @blk.vector
        def _(eng):
            eng.memset(n128[:], -128.0)
            call = eng.memset(ones_row[:], 1.0)
            p.inc("vec", call)
    for g in range(ftiles(CIN)):
        for fc in range(FOUT // TC):
            with nc.Block() as blk:
                @blk.sync
                def _(eng):
                    p.dma(eng, "ld", qi[:], qdram[g * 128:(g + 1) * 128, fc * TC:(fc + 1) * TC])
                    p.dma(eng, "ld", srow[:], sdram[g:g + 1, fc * TC:(fc + 1) * TC])
                @blk.tensor
                def _(eng):
                    p.wait(eng, "ld")
                    p.wait(eng, "vec")
                    call = eng.matmul(psB[:], ones_row[:], srow[:], start=True, stop=True)
                    p.inc("mm", call)
                @blk.scalar
                def _(eng):
                    p.wait(eng, "ld")
                    call = eng.activation(tf[:], qi[:], AF.Identity, bias=n128[:, 0:1])
                    p.inc("act", call)
                @blk.vector
                def _(eng):
                    p.wait(eng, "mm")
                    p.wait(eng, "act")
                    p.wait(eng, "st")
                    call = eng.tensor_tensor(wout[:], tf[:], psB[:], OP.mult)
                    p.inc("vec", call)
                @blk.gpsimd
                def _(eng):
                    p.wait(eng, "vec")
                    p.dma(eng, "st", dst[g * 128:(g + 1) * 128, fc * TC:(fc + 1) * TC], wout[:])
    p.pop()


def l1_rvec_pass(p, tag="rv"):
    nc = p.nc
    p.push()
    bv = p.sbuf(f"{tag}_bv", (128, 8), BF16)
    wrow = p.sbuf(f"{tag}_w", (128, D), BF16)
    ob_row = p.sbuf(f"{tag}_obr", (1, D), F32)
    rrow = p.sbuf(f"{tag}_r", (1, D), F32)
    psR = p.psum(f"{tag}_psR", (1, D), F32)
    with nc.Block() as blk:
        @blk.gpsimd
        def _(eng):
            p.wait(eng, "st")
            for c in range(8):
                p.dma(eng, "gld", bv[:, c:c + 1], p.dram["attn_bv_col"][c * 128:(c + 1) * 128, :])
            p.wait(eng, "gld")
        @blk.sync
        def _(eng):
            p.dma(eng, "ld", ob_row[:], p.dram["attn_out_b_row"][:, :])
            p.wait(eng, "ld")
    for c in range(8):
        with nc.Block() as blk:
            @blk.gpsimd
            def _(eng):
                p.dma(eng, "gld", wrow[:], p.dram["attn_out_w"][c * 128:(c + 1) * 128, :])
                p.wait(eng, "gld")
            @blk.tensor
            def _(eng):
                p.wait(eng, "gld")
                for half in range(2):
                    call = eng.matmul(psR[:, half * 512:(half + 1) * 512], bv[:, c:c + 1],
                                      wrow[:, half * 512:(half + 1) * 512],
                                      start=(c == 0), stop=(c == 7))
                p.inc("mm", call)
    with nc.Block() as blk:
        @blk.vector
        def _(eng):
            p.wait(eng, "mm")
            p.seq(eng, "vec", eng.tensor_copy(rrow[:], psR[:]))
            call = eng.tensor_tensor(rrow[:], rrow[:], ob_row[:], OP.add)
            p.inc("vec", call)
        @blk.gpsimd
        def _(eng):
            p.wait(eng, "vec")
            p.dma(eng, "st", p.dram["rrow3"][:, :], rrow[:])
    p.pop()


# version 17 (final)
# speedup vs baseline: 1.2531x; 1.0028x over previous
import numpy as np
import concourse.bass as bass
import concourse.mybir as mybir
from concourse.bass_utils import run_bass_kernel_spmd

F32 = mybir.dt.float32
BF16 = mybir.dt.bfloat16
I32 = mybir.dt.int32
AF = mybir.ActivationFunctionType
OP = mybir.AluOpType
AXX = mybir.AxisListType.X

T = 4096
D = 1024
TC = 512
NCHK = T // TC
EPS = 1e-5

M_INTER, M_NST, M_HEADS, M_P = 2048, 128, 32, 64
M_CONVD = M_INTER + 2 * M_NST
M_PROJ = M_INTER + M_CONVD + M_HEADS
CH = 128
NSC = T // CH

L_HEADS, L_DH, L_INNER, L_WIN, L_HID = 8, 64, 512, 512, 2730
SQ8 = float(np.sqrt(8.0))
A_HEADS, A_DH = 16, 64


def host_constants():
    c = {}
    c["ident"] = np.eye(128, dtype=np.float32)
    c["ones2d"] = np.ones((128, 128), dtype=np.float32)
    c["slt"] = np.tril(np.ones((128, 128), dtype=np.float32), -1)
    c["utri"] = np.triu(np.ones((128, 128), dtype=np.float32))
    c["keep_jle"] = np.triu(np.ones((128, 128), dtype=np.float32))
    c["keep_ile"] = np.tril(np.ones((128, 128), dtype=np.float32))
    c["odiv2048"] = np.full((128, 128), 1.0 / 2048.0, dtype=np.float32)
    c["odiv1024"] = np.full((128, 128), 1.0 / 1024.0, dtype=np.float32)
    half = np.arange(0, L_DH, 2, dtype=np.float64)
    inv_freq = 1.0 / 10000.0 ** (half / L_DH)
    t = np.arange(T, dtype=np.float64)
    fr = t[:, None] * inv_freq[None, :]
    fr = np.concatenate([fr, fr], axis=1)
    base = (half + 0.4 * L_DH) / (1.4 * L_DH)
    base2 = np.concatenate([base, base])
    il = (t % L_WIN) / (L_WIN // 2)
    qsc = base2[None, :] ** il[:, None]
    ksc = base2[None, :] ** (-il[:, None])
    sign = np.where(np.arange(L_DH) < 32, -1.0, 1.0)[None, :]
    c["rot_qc"] = (np.cos(fr) * qsc).astype(np.float32)
    c["rot_qs"] = (np.sin(fr) * qsc * sign).astype(np.float32)
    c["rot_kc"] = (np.cos(fr) * ksc).astype(np.float32)
    c["rot_ks"] = (np.sin(fr) * ksc * sign).astype(np.float32)
    c["base2_col"] = np.tile(base2 ** 2.0, 2).reshape(128, 1).astype(np.float32)
    ind = np.zeros((1024, 16), dtype=np.float32)
    for d in range(1024):
        ind[d, d // 64] = 1.0
    c["head_ind"] = ind
    return c


CONSTS = host_constants()


class P:

    def __init__(self, nc, dbg):
        self.nc = nc
        self.ctx = []
        self.scopes = []
        self.sem = {}
        self.dram = {}
        self.dbg = set(dbg)
        for name in ["ld", "gld", "st", "mm", "act", "vec", "evac", "aux", "lx0", "lx1"]:
            h = nc.semaphore(name).__enter__()
            self.sem[name] = [h, 0]

    def sbuf(self, name, shape, dt):
        cm = self.nc.sbuf_tensor(name, list(shape), dt)
        v = cm.__enter__()
        self.ctx.append(cm)
        return v

    def psum(self, name, shape, dt):
        cm = self.nc.psum_tensor(name, list(shape), dt)
        v = cm.__enter__()
        self.ctx.append(cm)
        return v

    def push(self):
        self.scopes.append(len(self.ctx))

    def pop(self):
        base = self.scopes.pop()
        for cm in reversed(self.ctx[base:]):
            cm.__exit__(None, None, None)
        del self.ctx[base:]

    def scratch(self, name, shape, dt):
        if name in getattr(self, "inov", ()):
            kind = "ExternalInput"
        else:
            kind = "ExternalOutput" if name in self.dbg else "Internal"
        h = self.nc.dram_tensor(name, list(shape), dt, kind=kind).ap()
        self.dram[name] = h
        return h

    def inc(self, name, call, n=1):
        s = self.sem[name]
        call.then_inc(s[0], n)
        s[1] += n

    def dma(self, eng, name, out, in_):
        s = self.sem[name]
        eng.dma_start(out=out, in_=in_).then_inc(s[0], 16)
        s[1] += 16

    def n(self, name):
        return self.sem[name][1]

    def wait(self, eng, name, val=None):
        s = self.sem[name]
        eng.wait_ge(s[0], s[1] if val is None else val)

    def seq(self, eng, name, call):
        self.inc(name, call)
        self.wait(eng, name)


def ftiles(C):
    return (C + 127) // 128


def frange(i, C):
    return slice(i * 128, min((i + 1) * 128, C))


def fsz(i, C):
    return min(128, C - i * 128)


def ln_pass(p, tag, src, dst, CIN, dst32=None, src_is_bf=False):
    nc = p.nc
    p.push()
    nt = ftiles(CIN)
    x32 = [p.sbuf(f"{tag}_x32_{i}", (128, TC), F32) for i in range(nt)]
    xbf = [p.sbuf(f"{tag}_xbf_{i}", (128, TC), BF16) for i in range(nt)]
    sq = [p.sbuf(f"{tag}_sq_{i}", (128, TC), BF16) for i in range(nt)]
    onesd = p.sbuf(f"{tag}_onesd", (128, 128), BF16)
    musb = p.sbuf(f"{tag}_mu", (128, TC), F32)
    va = p.sbuf(f"{tag}_va", (128, TC), F32)
    s_sb = p.sbuf(f"{tag}_s", (128, TC), F32)
    outb = [p.sbuf(f"{tag}_o_{i}", (128, TC), BF16) for i in range(nt)]
    o32 = [p.sbuf(f"{tag}_o32_{i}", (128, TC), F32) for i in range(nt)] if dst32 is not None else None
    pmu = p.psum(f"{tag}_pmu", (128, TC), F32)
    pex = p.psum(f"{tag}_pex", (128, TC), F32)

    oc = p.dram["odiv2048"] if CIN == 2048 else p.dram["odiv1024"]
    with nc.Block() as blk:
        @blk.gpsimd
        def _(eng):
            p.wait(eng, "st")
            p.dma(eng, "gld", onesd[:], oc[:, :])

    for ci in range(NCHK):
        cs = slice(ci * TC, (ci + 1) * TC)
        with nc.Block() as blk:
            @blk.sync
            def _(eng):
                for i in range(nt):
                    p.dma(eng, "ld", (xbf if src_is_bf else x32)[i][:], src[frange(i, CIN), cs])
            @blk.gpsimd
            def _(eng):
                for i in range(nt):
                    p.dma(eng, "gld", (x32 if src_is_bf else xbf)[i][:], src[frange(i, CIN), cs])
            @blk.scalar
            def _(eng):
                p.wait(eng, "gld")
                p.wait(eng, "ld")
                for i in range(nt):
                    call = eng.activation(sq[i][:], xbf[i][:], AF.Square)
                p.inc("act", call)
            @blk.tensor
            def _(eng):
                p.wait(eng, "ld")
                p.wait(eng, "gld")
                for i in range(nt):
                    eng.matmul(pmu[:], onesd[:], xbf[i][:], start=(i == 0), stop=(i == nt - 1))
                p.wait(eng, "act")
                for i in range(nt):
                    call = eng.matmul(pex[:], onesd[:], sq[i][:], start=(i == 0), stop=(i == nt - 1))
                p.inc("mm", call)
            @blk.vector
            def _(eng):
                p.wait(eng, "mm")
                p.seq(eng, "vec", eng.tensor_copy(musb[:], pmu[:]))
                p.seq(eng, "vec", eng.tensor_tensor(va[:], musb[:], musb[:], OP.mult))
                p.seq(eng, "vec", eng.tensor_tensor(va[:], pex[:], va[:], OP.subtract))
                p.seq(eng, "vec", eng.tensor_scalar(va[:], va[:], EPS, None, OP.add))
                call = eng.reciprocal(s_sb[:], va[:])
                p.inc("vec", call)
            @blk.scalar
            def _(eng):
                p.wait(eng, "vec")
                call = eng.activation(s_sb[:], s_sb[:], AF.Sqrt)
                p.inc("act", call)
            @blk.vector
            def _(eng):
                p.wait(eng, "act")
                p.wait(eng, "st")
                for i in range(nt):
                    call = eng.tensor_tensor(x32[i][:], x32[i][:], musb[:], OP.subtract)
                p.seq(eng, "vec", call)
                for i in range(nt):
                    if o32 is not None:
                        call = eng.tensor_tensor(o32[i][:], x32[i][:], s_sb[:], OP.mult)
                    else:
                        call = eng.tensor_tensor(outb[i][:], x32[i][:], s_sb[:], OP.mult)
                p.seq(eng, "vec", call)
                if o32 is not None:
                    for i in range(nt):
                        call = eng.tensor_copy(outb[i][:], o32[i][:])
                    p.inc("vec", call)
            @blk.gpsimd
            def _(eng):
                p.wait(eng, "vec")
                for i in range(nt):
                    p.dma(eng, "st", dst[frange(i, CIN), cs], outb[i][:])
                    if dst32 is not None:
                        p.dma(eng, "st", dst32[frange(i, CIN), cs], o32[i][:])
    p.pop()


def dense_pass(p, tag, src, CIN, wdram, FOUT, sinks, src_shift=None, src_f32=False):
    nc = p.nc
    assert not src_f32
    p.push()
    nct = ftiles(CIN)
    nft = ftiles(FOUT)
    has_res = any(spec.get("res") is not None for (_, _, spec) in sinks)
    osz = sum((4 if spec.get("dt", BF16) == F32 else 2) * 128 * TC
              for (flo, fhi, spec) in sinks for _ in range(flo // 128, (fhi + 127) // 128))
    rqsz = nft * 128 * TC * 4 if has_res else 0
    xb1 = nct * 128 * (TC + 4) * 2

    def plan(dbo, dbx):
        return (nct * 128 * FOUT * 2 + (2 if dbx else 1) * xb1
                + (2 if dbo else 1) * (osz + rqsz) + rqsz)
    LIM = 23 * 1024 * 1024
    dbo = False
    dbx = plan(dbo, True) < LIM
    wsb = [p.sbuf(f"{tag}_w{i}", (128, FOUT), BF16) for i in range(nct)]
    xsb = [[p.sbuf(f"{tag}_x{b}_{i}", (128, TC + 4), BF16) for i in range(nct)]
           for b in range(2 if dbx else 1)]
    if not dbx:
        xsb.append(xsb[0])
    NB = 6
    pb = [p.psum(f"{tag}_pb{i}", (128, TC), F32) for i in range(NB)]
    osb = [[], []]
    bias_cols = {}
    for (flo, fhi, spec) in sinks:
        for f in range(flo // 128, (fhi + 127) // 128):
            osb[0].append(p.sbuf(f"{tag}_o0_{f}", (128, TC), spec.get("dt", BF16)))
            osb[1].append(p.sbuf(f"{tag}_o1_{f}", (128, TC), spec.get("dt", BF16)) if dbo
                          else osb[0][-1])
        if spec.get("bias") is not None:
            bias_cols[(flo, fhi)] = p.sbuf(f"{tag}_b{flo}", (128, ftiles(fhi - flo)), F32)
        if spec.get("scale_col") is not None:
            spec["_scol"] = p.sbuf(f"{tag}_s{flo}", (128, 1), F32)
    if has_res:
        r32 = [p.sbuf(f"{tag}_r{f}", (128, TC), F32) for f in range(nft)]
        q32 = [[p.sbuf(f"{tag}_q0_{f}", (128, TC), F32) for f in range(nft)]]
        q32.append([p.sbuf(f"{tag}_q1_{f}", (128, TC), F32) for f in range(nft)] if dbo
                   else q32[0])

    def spec_of(f):
        for (flo, fhi, spec) in sinks:
            if flo <= f * 128 < fhi:
                return flo, fhi, spec
        raise AssertionError(f)

    shifts = src_shift or [0] * nct

    def emit_xload(eng, ci):
        sname = f"lx{ci % 2}"
        buf = xsb[ci % 2]
        for i in range(nct):
            rows = fsz(i, CIN)
            s = shifts[i]
            lo = ci * TC - s
            if lo < 0:
                p.dma(eng, sname, buf[i][0:rows, 4:4 + TC - s], src[frange(i, CIN), 0:TC - s])
            else:
                p.dma(eng, sname, buf[i][0:rows, 4:4 + TC], src[frange(i, CIN), lo:lo + TC])
        return p.n(sname)

    lxmark = [0] * NCHK
    gldmark0 = [0]

    with nc.Block() as blk:
        @blk.gpsimd
        def _(eng):
            p.wait(eng, "st")
            for i in range(nct):
                rows = fsz(i, CIN)
                p.dma(eng, "gld", wsb[i][0:rows, :], wdram[i * 128:i * 128 + rows, :])
            for (flo, fhi, spec) in sinks:
                if spec.get("bias") is not None:
                    bc = bias_cols[(flo, fhi)]
                    for j in range(ftiles(fhi - flo)):
                        rows = fsz(j, fhi - flo)
                        p.dma(eng, "gld", bc[0:rows, j:j + 1], spec["bias"][j * 128:j * 128 + rows, :])
            gldmark0[0] = p.n("gld")
        @blk.sync
        def _(eng):
            p.wait(eng, "st")
            lxmark[0] = emit_xload(eng, 0)
        @blk.vector
        def _(eng):
            p.wait(eng, "gld")
            for b in range(2 if dbx else 1):
                for i in range(nct):
                    eng.memset(xsb[b][i][:, 0:4], 0.0)
            for (flo, fhi, spec) in sinks:
                if spec.get("scale_col") is not None:
                    call = eng.memset(spec["_scol"][:], float(spec["scale_col"]))
                    p.inc("vec", call)
                if spec.get("bias_scale") is not None:
                    bc = bias_cols[(flo, fhi)]
                    call = eng.tensor_scalar(bc[:], bc[:], float(spec["bias_scale"]), None, OP.mult)
                    p.inc("vec", call)

    mm_mark = [0] * NCHK
    vres_mark = [0] * NCHK
    evb0 = [0]
    with nc.Block() as blk:
        for ci in range(NCHK):
            @blk.sync
            def _(eng):
                if not dbx and ci > 0:
                    p.wait(eng, "mm", mm_mark[ci - 1])
                    lxmark[ci] = emit_xload(eng, ci)
                if has_res:
                    if ci >= 1:
                        p.wait(eng, "vec", vres_mark[ci - 1])
                    for f in range(nft):
                        flo2, _, sp = spec_of(f)
                        if sp.get("res") is not None:
                            w = fsz(f, FOUT)
                            r0 = sp.get("dst_f0", 0) + f * 128 - flo2
                            p.dma(eng, "ld", r32[f][0:w, :], sp["res"][r0:r0 + w, ci * TC:(ci + 1) * TC])
                if dbx and ci + 1 < NCHK:
                    if ci >= 1:
                        p.wait(eng, "mm", mm_mark[ci - 1])
                    lxmark[ci + 1] = emit_xload(eng, ci + 1)
            @blk.tensor
            def _(eng):
                p.wait(eng, f"lx{ci % 2}", lxmark[ci])
                if ci == 0:
                    p.wait(eng, "gld", gldmark0[0])
                    evb0[0] = p.n("evac")
                for f in range(nft):
                    gf = ci * nft + f
                    w = fsz(f, FOUT)
                    bank = pb[gf % NB]
                    if gf >= NB:
                        eng.wait_ge(p.sem["evac"][0], evb0[0] + (gf - NB + 1))
                    for c in range(nct):
                        rows = fsz(c, CIN)
                        s = shifts[c]
                        off = 4 if (ci > 0 or s == 0) else 4 - s
                        call = eng.matmul(bank[0:w, :], wsb[c][0:rows, f * 128:f * 128 + w],
                                          xsb[ci % 2][c][0:rows, off:off + TC], start=(c == 0), stop=(c == nct - 1))
                    p.inc("mm", call)
                mm_mark[ci] = p.n("mm")
            @blk.scalar
            def _(eng):
                p.wait(eng, "st")
                mmb = p.n("mm") - nft
                for f in range(nft):
                    gf = ci * nft + f
                    eng.wait_ge(p.sem["mm"][0], mmb + f + 1)
                    w = fsz(f, FOUT)
                    flo, fhi, spec = spec_of(f)
                    j = f - flo // 128
                    kw = {}
                    if spec.get("bias") is not None:
                        kw["bias"] = bias_cols[(flo, fhi)][0:w, j:j + 1]
                    if spec.get("scale_col") is not None:
                        kw["scale"] = spec["_scol"][0:w, 0:1]
                    call = eng.activation(osb[ci % 2][f][0:w, :], pb[gf % NB][0:w, :],
                                          spec.get("func", AF.Identity), **kw)
                    p.inc("evac", call)
            if has_res:
                @blk.vector
                def _(eng):
                    p.wait(eng, "evac")
                    p.wait(eng, "ld")
                    p.wait(eng, "st")
                    for f in range(nft):
                        _, _, sp = spec_of(f)
                        if sp.get("res") is not None:
                            call = eng.tensor_tensor(q32[ci % 2][f][0:fsz(f, FOUT), :], osb[ci % 2][f][0:fsz(f, FOUT), :],
                                                     r32[f][0:fsz(f, FOUT), :], OP.add)
                    p.inc("vec", call)
            vres_mark[ci] = p.n("vec")
            @blk.gpsimd
            def _(eng):
                if has_res:
                    p.wait(eng, "vec")
                else:
                    p.wait(eng, "evac")
                for (flo, fhi, spec) in sinks:
                    d0 = spec.get("dst_f0", 0)
                    for f in range(flo // 128, (fhi + 127) // 128):
                        w = fsz(f, FOUT)
                        r0 = d0 + f * 128 - flo
                        if spec.get("res") is not None:
                            p.dma(eng, "st", spec["dst"][r0:r0 + w, ci * TC:(ci + 1) * TC], q32[ci % 2][f][0:w, :])
                        else:
                            p.dma(eng, "st", spec["dst"][r0:r0 + w, ci * TC:(ci + 1) * TC], osb[ci % 2][f][0:w, :])
    p.pop()


def conv_dt_pass(p, tag="cdt"):
    nc = p.nc
    p.push()
    stage = p.sbuf(f"{tag}_stage", (128, T + 4), BF16)
    acc_a = p.sbuf(f"{tag}_acca", (128, T), BF16)
    acc_b = p.sbuf(f"{tag}_accb", (128, T), BF16)
    outt = p.sbuf(f"{tag}_out", (128, T), BF16)
    wcols = p.sbuf(f"{tag}_w", (128, 4), F32)
    bcol = p.sbuf(f"{tag}_b", (128, 1), F32)
    xbcr = p.dram["xbcr"]
    cw = p.dram["conv_w4"]
    cb = p.dram["conv_b_col"]
    nt = ftiles(M_CONVD)
    with nc.Block() as blk:
        @blk.vector
        def _(eng):
            eng.memset(stage[:, 0:4], 0.0)
        @blk.gpsimd
        def _(eng):
            p.wait(eng, "st")
    for i in range(nt):
        dst = (p.dram["xc"], i) if i < 16 else ((p.dram["bT"], 0) if i == 16 else (p.dram["cT"], 0))
        with nc.Block() as blk:
            @blk.sync
            def _(eng):
                p.dma(eng, "ld", stage[:, 4:4 + T], xbcr[frange(i, M_CONVD), :])
                p.dma(eng, "ld", wcols[:], cw[i * 128:(i + 1) * 128, :])
                p.dma(eng, "ld", bcol[:], cb[i * 128:(i + 1) * 128, :])
            @blk.vector
            def _(eng):
                p.wait(eng, "ld")
                if i > 0:
                    p.wait(eng, "st")
                p.seq(eng, "vec", eng.tensor_scalar(acc_a[:], stage[:, 1:1 + T], wcols[:, 0:1], None, OP.mult))
                p.seq(eng, "vec", eng.scalar_tensor_tensor(acc_b[:], stage[:, 2:2 + T], wcols[:, 1:2], acc_a[:], OP.mult, OP.add))
                p.seq(eng, "vec", eng.scalar_tensor_tensor(acc_a[:], stage[:, 3:3 + T], wcols[:, 2:3], acc_b[:], OP.mult, OP.add))
                call = eng.scalar_tensor_tensor(acc_b[:], stage[:, 4:4 + T], wcols[:, 3:4], acc_a[:], OP.mult, OP.add)
                p.inc("vec", call)
            @blk.scalar
            def _(eng):
                p.wait(eng, "vec")
                p.wait(eng, "st")
                call = eng.activation(outt[:], acc_b[:], AF.Silu, bias=bcol[:, 0:1])
                p.inc("act", call)
            @blk.gpsimd
            def _(eng):
                p.wait(eng, "act")
                d, j = dst
                p.dma(eng, "st", d[j * 128:(j + 1) * 128, :] if i < 16 else d[:, :], outt[:])
    dtr, dts = p.dram["dtr"], p.dram["dts"]
    t0 = p.sbuf(f"{tag}_dt0", (32, T), F32)
    t1 = p.sbuf(f"{tag}_dt1", (32, T), F32)
    t2 = p.sbuf(f"{tag}_dt2", (32, T), F32)
    neg1 = p.sbuf(f"{tag}_neg1", (32, 1), F32)
    one1 = p.sbuf(f"{tag}_one1", (32, 1), F32)
    with nc.Block() as blk:
        @blk.sync
        def _(eng):
            p.dma(eng, "ld", t0[:], dtr[:, :])
        @blk.vector
        def _(eng):
            eng.memset(neg1[:], -1.0)
            call = eng.memset(one1[:], 1.0)
            p.inc("vec", call)
        @blk.scalar
        def _(eng):
            p.wait(eng, "ld")
            p.wait(eng, "vec")
            p.seq(eng, "act", eng.activation(t1[:], t0[:], AF.Abs))
            p.seq(eng, "act", eng.activation(t1[:], t1[:], AF.Exp, scale=neg1[:, 0:1]))
            call = eng.activation(t1[:], t1[:], AF.Ln, bias=one1[:, 0:1])
            p.inc("act", call)
        @blk.vector
        def _(eng):
            p.wait(eng, "act")
            p.seq(eng, "vec", eng.tensor_scalar(t2[:], t0[:], 0.0, None, OP.max))
            call = eng.tensor_tensor(t2[:], t2[:], t1[:], OP.add)
            p.inc("vec", call)
        @blk.gpsimd
        def _(eng):
            p.wait(eng, "vec")
            p.dma(eng, "st", dts[:, :], t2[:])
    p.pop()


def ssd_pass(p, tag="ssd"):
    nc = p.nc
    p.push()
    identf = p.sbuf(f"{tag}_idf", (128, 128), F32)
    slt_bf = p.sbuf(f"{tag}_slt", (128, 128), BF16)
    slt_f = p.sbuf(f"{tag}_sltf", (128, 128), F32)
    ones_bf = p.sbuf(f"{tag}_onesb", (128, 128), BF16)
    utri_f = p.sbuf(f"{tag}_utri", (128, 128), F32)
    utri_b = p.sbuf(f"{tag}_utrib", (128, 128), BF16)
    ones_row = p.sbuf(f"{tag}_or", (1, 128), F32)
    ones_col = p.sbuf(f"{tag}_ocol", (128, 1), F32)
    arow = p.sbuf(f"{tag}_arow", (1, 32), F32)
    drow = p.sbuf(f"{tag}_drow", (1, 32), F32)
    abc = p.sbuf(f"{tag}_abc", (128, 32), F32)
    dbc = p.sbuf(f"{tag}_dbc", (128, 32), F32)
    carry32a = p.sbuf(f"{tag}_ca", (128, 2048), F32)
    carry32b = p.sbuf(f"{tag}_cb", (128, 2048), F32)
    carryb = p.sbuf(f"{tag}_cbf", (128, 2048), BF16)
    x_tok = p.sbuf(f"{tag}_xtok", (128, 2048), BF16)
    b_tok = p.sbuf(f"{tag}_btok", (128, 128), BF16)
    bT_sb = p.sbuf(f"{tag}_bT", (128, 128), BF16)
    cT_sb = p.sbuf(f"{tag}_cT", (128, 128), BF16)
    dt_sb = p.sbuf(f"{tag}_dt", (32, 128), F32)
    dtok = p.sbuf(f"{tag}_dtok", (128, 32), BF16)
    da_tok = p.sbuf(f"{tag}_da", (128, 32), BF16)
    b_all = p.sbuf(f"{tag}_ball", (128, 4096), BF16)
    expm = p.sbuf(f"{tag}_expm", (128, 4096), BF16)
    ea = p.sbuf(f"{tag}_ea", (128, 4096), BF16)
    gm_sb = p.sbuf(f"{tag}_gm", (128, 128), BF16)
    gm_all = p.sbuf(f"{tag}_gma", (128, 4096), BF16)
    dxt = p.sbuf(f"{tag}_dxt", (128, 2048), BF16)
    ch_all = p.sbuf(f"{tag}_cha", (128, 4096), BF16)
    dd_sb = p.sbuf(f"{tag}_dd", (128, 32), BF16)
    cd_sb = p.sbuf(f"{tag}_cd", (1, 32), F32)
    cdb_sb = p.sbuf(f"{tag}_cdb", (128, 32), F32)
    xs_sb = p.sbuf(f"{tag}_xs", (128, 2048), BF16)
    xd_sb = p.sbuf(f"{tag}_xd", (128, 2048), BF16)
    ystage = p.sbuf(f"{tag}_yst", (128, 2048), BF16)
    psA = p.psum(f"{tag}_psA", (128, 2048), F32)
    psB = p.psum(f"{tag}_psB", (128, 2048), F32)

    xc, bT, cT, dts, ytok = (p.dram[k] for k in ["xc", "bT", "cT", "dts", "ytok"])

    with nc.Block() as blk:
        @blk.gpsimd
        def _(eng):
            p.wait(eng, "st")
            p.dma(eng, "gld", slt_bf[:], p.dram["slt"][:, :])
            p.dma(eng, "gld", ones_bf[:], p.dram["ones2d"][:, :])
            p.dma(eng, "gld", utri_b[:], p.dram["utri"][:, :])
            p.wait(eng, "gld")
        @blk.sync
        def _(eng):
            p.dma(eng, "ld", identf[:], p.dram["ident"][:, :])
            p.dma(eng, "ld", slt_f[:], p.dram["slt"][:, :])
            p.dma(eng, "ld", utri_f[:], p.dram["utri"][:, :])
            p.dma(eng, "ld", arow[:], p.dram["A_log_row"][:, :])
            p.dma(eng, "ld", drow[:], p.dram["D_row"][:, :])
            p.wait(eng, "ld")
        @blk.vector
        def _(eng):
            eng.memset(ones_row[:], 1.0)
            eng.memset(ones_col[:], 1.0)
            eng.memset(carry32a[:], 0.0)
            call = eng.memset(carryb[:], 0.0)
            p.inc("vec", call)
    with nc.Block() as blk:
        @blk.scalar
        def _(eng):
            p.wait(eng, "ld")
            call = eng.activation(arow[:], arow[:], AF.Exp)
            p.inc("act", call)
        @blk.vector
        def _(eng):
            p.wait(eng, "act")
            call = eng.tensor_scalar(arow[:], arow[:], -1.0, None, OP.mult)
            p.inc("vec", call)
            p.wait(eng, "vec")
    with nc.Block() as blk:
        @blk.tensor
        def _(eng):
            p.wait(eng, "vec")
            eng.matmul(psA[:, 0:32], ones_row[:], arow[:], start=True, stop=True)
            call = eng.matmul(psA[:, 32:64], ones_row[:], drow[:], start=True, stop=True)
            p.inc("mm", call)
        @blk.vector
        def _(eng):
            p.wait(eng, "mm")
            eng.tensor_copy(abc[:], psA[:, 0:32])
            call = eng.tensor_copy(dbc[:], psA[:, 32:64])
            p.inc("vec", call)

    for c in range(getattr(p, 'ssd_chunks', NSC)):
        cs = slice(c * CH, (c + 1) * CH)
        with nc.Block() as blk:
            @blk.sync
            def _(eng):
                p.dma(eng, "ld", dt_sb[:], dts[:, cs])
                p.dma(eng, "ld", bT_sb[:], bT[:, cs])
                p.dma(eng, "ld", cT_sb[:], cT[:, cs])
                eng.dma_start(out=x_tok[:], in_=xc[:, cs], transpose=True).then_inc(p.sem["ld"][0], 16)
                p.sem["ld"][1] += 16
                eng.dma_start(out=b_tok[:], in_=bT[:, cs], transpose=True).then_inc(p.sem["ld"][0], 16)
                p.sem["ld"][1] += 16
            @blk.tensor
            def _(eng):
                p.wait(eng, "ld")
                call = eng.transpose(psB[:, 0:32], dt_sb[:], identf[0:32, 0:32])
                p.inc("mm", call)
            @blk.vector
            def _(eng):
                p.wait(eng, "mm")
                eng.tensor_copy(dtok[:], psB[:, 0:32])
                eng.tensor_tensor(
                    dxt[:].rearrange("p (h q) -> p h q", h=32),
                    x_tok[:].rearrange("p (h q) -> p h q", h=32),
                    dbc[:].unsqueeze(-1).broadcast_to((128, 32, 64)), OP.mult)
                call = eng.tensor_tensor(da_tok[:], psB[:, 0:32], abc[:], OP.mult)
                p.inc("vec", call)
            @blk.vector
            def _(eng):
                p.wait(eng, "vec")
                call = eng.tensor_tensor(
                    b_all[:].rearrange("p (h l) -> p h l", h=32),
                    da_tok[:].unsqueeze(-1).broadcast_to((128, 32, 128)),
                    utri_b[:].unsqueeze(1).broadcast_to((128, 32, 128)),
                    OP.mult)
                p.inc("vec", call)
            mm0 = p.n("mm")
            a0 = p.n("act")
            @blk.tensor
            def _(eng):
                p.wait(eng, "vec")
                for j in range(4):
                    call = eng.matmul(psA[:, j * 512:(j + 1) * 512], slt_bf[:],
                                      b_all[:, j * 512:(j + 1) * 512], start=True, stop=True)
                p.inc("mm", call)
                for j in range(4):
                    call = eng.matmul(psB[:, j * 512:(j + 1) * 512], ones_bf[:],
                                      b_all[:, j * 512:(j + 1) * 512], start=True, stop=True)
                p.inc("mm", call)
                eng.wait_ge(p.sem["act"][0], a0 + 1)
                for j in range(4):
                    call = eng.matmul(psA[:, j * 512:(j + 1) * 512], slt_bf[:],
                                      b_all[:, 2048 + j * 512:2048 + (j + 1) * 512], start=True, stop=True)
                p.inc("mm", call)
                eng.wait_ge(p.sem["act"][0], a0 + 2)
                for j in range(4):
                    call = eng.matmul(psB[:, j * 512:(j + 1) * 512], ones_bf[:],
                                      b_all[:, 2048 + j * 512:2048 + (j + 1) * 512], start=True, stop=True)
                p.inc("mm", call)
            @blk.scalar
            def _(eng):
                eng.wait_ge(p.sem["mm"][0], mm0 + 1)
                call = eng.activation(expm[:, 0:2048], psA[:], AF.Exp)
                p.inc("act", call)
                eng.wait_ge(p.sem["mm"][0], mm0 + 2)
                call = eng.activation(ea[:, 0:2048], psB[:], AF.Exp)
                p.inc("act", call)
                eng.wait_ge(p.sem["mm"][0], mm0 + 3)
                call = eng.activation(expm[:, 2048:4096], psA[:], AF.Exp)
                p.inc("act", call)
                eng.wait_ge(p.sem["mm"][0], mm0 + 4)
                call = eng.activation(ea[:, 2048:4096], psB[:], AF.Exp)
                p.inc("act", call)
            mm1 = p.n("mm")
            a1 = p.n("act")
            @blk.tensor
            def _(eng):
                p.wait(eng, "act")
                eng.matmul(psB[:, 0:128], bT_sb[:], cT_sb[:], start=True, stop=True)
                call = eng.matmul(psB[:, 512:544], slt_bf[:], da_tok[:], start=True, stop=True)
                p.inc("mm", call)
            v1 = p.n("vec")
            @blk.vector
            def _(eng):
                eng.wait_ge(p.sem["mm"][0], mm1 + 1)
                call = eng.tensor_tensor(gm_sb[:], psB[:, 0:128], utri_f[:], OP.mult)
                p.inc("vec", call)
            @blk.scalar
            def _(eng):
                eng.wait_ge(p.sem["mm"][0], mm1 + 1)
                call = eng.activation(dd_sb[:], psB[:, 512:544], AF.Exp)
                p.inc("act", call)
            v2 = p.n("vec")
            @blk.vector
            def _(eng):
                p.wait(eng, "vec", v1 + 1)
                p.wait(eng, "act")
                eng.tensor_tensor(
                    gm_all[:].rearrange("p (h l) -> p h l", h=32),
                    gm_sb[:].unsqueeze(1).broadcast_to((128, 32, 128)),
                    expm[:].rearrange("p (h l) -> p h l", h=32), OP.mult)
                call = eng.tensor_tensor(ch_all[:].rearrange("p (h l) -> p h l", h=32),
                                         cT_sb[:].unsqueeze(1).broadcast_to((128, 32, 128)),
                                         ea[:].rearrange("p (h l) -> p h l", h=32), OP.mult)
                p.inc("vec", call)
            @blk.vector
            def _(eng):
                p.wait(eng, "vec")
                p.wait(eng, "act")
                p.seq(eng, "vec", eng.tensor_tensor(
                    xs_sb[:].rearrange("p (h q) -> p h q", h=32),
                    x_tok[:].rearrange("p (h q) -> p h q", h=32),
                    dtok[:].unsqueeze(-1).broadcast_to((128, 32, 64)), OP.mult))
                call = eng.tensor_tensor(xd_sb[:].rearrange("p (h q) -> p h q", h=32),
                                         xs_sb[:].rearrange("p (h q) -> p h q", h=32),
                                         dd_sb[:].unsqueeze(-1).broadcast_to((128, 32, 64)), OP.mult)
                p.inc("vec", call)
            mm2 = p.n("mm")
            @blk.tensor
            def _(eng):
                p.wait(eng, "vec")
                call = eng.matmul(psB[0:1, 1024:1056], ones_bf[:, 0:1], da_tok[:], start=True, stop=True)
                p.inc("mm", call)
            a2 = p.n("act")
            @blk.scalar
            def _(eng):
                p.wait(eng, "mm", mm2 + 1)
                call = eng.activation(cd_sb[:], psB[0:1, 1024:1056], AF.Exp)
                p.inc("act", call)
            mm3 = p.n("mm")
            @blk.tensor
            def _(eng):
                p.wait(eng, "act", a2 + 1)
                call = eng.matmul(psB[:, 1536:1568], ones_row[:], cd_sb[:], start=True, stop=True)
                p.inc("mm", call)
            v3 = p.n("vec")
            @blk.vector
            def _(eng):
                p.wait(eng, "mm", mm3 + 1)
                call = eng.tensor_copy(cdb_sb[:], psB[:, 1536:1568])
                p.inc("vec", call)
            mm4 = p.n("mm")
            @blk.tensor
            def _(eng):
                p.wait(eng, "vec")
                p.wait(eng, "act")
                for h in range(32):
                    hs = slice(h * 64, (h + 1) * 64)
                    hl = slice(h * 128, (h + 1) * 128)
                    eng.matmul(psA[:, hs], gm_all[:, hl], xs_sb[:, hs], start=True, stop=False)
                    call = eng.matmul(psA[:, hs], ch_all[:, hl], carryb[:, hs], start=False, stop=True)
                p.inc("mm", call)
                for h in range(32):
                    call = eng.matmul(psB[:, h * 64:(h + 1) * 64],
                                      b_tok[:], xd_sb[:, h * 64:(h + 1) * 64], start=True, stop=True)
                p.inc("mm", call)
            @blk.vector
            def _(eng):
                p.wait(eng, "mm", mm4 + 1)
                p.wait(eng, "st")
                call = eng.tensor_tensor(ystage[:], psA[:], dxt[:], OP.add)
                p.inc("vec", call)
            v9 = p.n("vec")
            @blk.gpsimd
            def _(eng):
                p.wait(eng, "vec", v9)
                p.dma(eng, "st", ytok[cs, :], ystage[:])
            ca, cb_ = (carry32a, carry32b) if c % 2 == 0 else (carry32b, carry32a)
            @blk.vector
            def _(eng):
                p.wait(eng, "mm")
                p.wait(eng, "vec")
                p.seq(eng, "vec", eng.tensor_tensor(
                    cb_[:].rearrange("p (h q) -> p h q", h=32),
                    ca[:].rearrange("p (h q) -> p h q", h=32),
                    cdb_sb[:].unsqueeze(-1).broadcast_to((128, 32, 64)), OP.mult))
                p.seq(eng, "vec", eng.tensor_tensor(cb_[:], cb_[:], psB[:, 0:2048], OP.add))
                call = eng.tensor_copy(carryb[:], cb_[:])
                p.inc("vec", call)
    p.ssd_tiles = dict(b_all=b_all, expm=expm, ea=ea, gm_sb=gm_sb, gm_all=gm_all, ch_all=ch_all, dd_sb=dd_sb, cdb_sb=cdb_sb, xs_sb=xs_sb,
                       xd_sb=xd_sb, da_tok=da_tok, dtok=dtok, x_tok=x_tok,
                       abc=abc, arow=arow, utri_f=utri_f, utri_b=utri_b, dbc=dbc,
                       carryb=carryb, b_tok=b_tok, bT_sb=bT_sb, cT_sb=cT_sb, ystage=ystage)
    p.pop()


def groupln_gate_pass(p, tag="gln"):
    nc = p.nc
    p.push()
    nt = 16
    y32 = [p.sbuf(f"{tag}_y32_{i}", (128, TC), F32) for i in range(nt)]
    ybf = [p.sbuf(f"{tag}_ybf_{i}", (128, TC), BF16) for i in range(nt)]
    sq = [p.sbuf(f"{tag}_sq_{i}", (128, TC), BF16) for i in range(nt)]
    gsb = [p.sbuf(f"{tag}_g_{i}", (128, TC), BF16) for i in range(nt)]
    onesd = p.sbuf(f"{tag}_onesd", (128, 128), BF16)
    musb = p.sbuf(f"{tag}_mu", (128, TC), F32)
    va = p.sbuf(f"{tag}_va", (128, TC), F32)
    s_sb = p.sbuf(f"{tag}_s", (128, TC), F32)
    wcol = p.sbuf(f"{tag}_w", (128, nt), F32)
    bcol = p.sbuf(f"{tag}_bc", (128, nt), F32)
    outb = [p.sbuf(f"{tag}_o_{i}", (128, TC), BF16) for i in range(nt)]
    pmu = p.psum(f"{tag}_pmu", (128, TC), F32)
    pex = p.psum(f"{tag}_pex", (128, TC), F32)
    ytok, gate_s, mzin = p.dram["ytok"], p.dram["gate_s"], p.dram["mzin"]
    with nc.Block() as blk:
        @blk.gpsimd
        def _(eng):
            p.wait(eng, "st")
            p.dma(eng, "gld", onesd[:], p.dram["odiv2048"][:, :])
            p.wait(eng, "gld")
        @blk.sync
        def _(eng):
            for i in range(nt):
                p.dma(eng, "ld", wcol[:, i:i + 1], p.dram["mln_w_col"][i * 128:(i + 1) * 128, :])
                p.dma(eng, "ld", bcol[:, i:i + 1], p.dram["mln_b_col"][i * 128:(i + 1) * 128, :])
            p.wait(eng, "ld")
    for ci in range(NCHK):
        cs = slice(ci * TC, (ci + 1) * TC)
        with nc.Block() as blk:
            @blk.sync
            def _(eng):
                for i in range(nt):
                    eng.dma_start(out=ybf[i][:], in_=ytok[cs, i * 128:(i + 1) * 128],
                                  transpose=True).then_inc(p.sem["ld"][0], 16)
                    p.sem["ld"][1] += 16
                    p.dma(eng, "ld", gsb[i][:], gate_s[frange(i, 2048), cs])
            @blk.scalar
            def _(eng):
                p.wait(eng, "ld")
                for i in range(nt):
                    call = eng.activation(sq[i][:], ybf[i][:], AF.Square)
                p.inc("act", call)
            @blk.tensor
            def _(eng):
                p.wait(eng, "ld")
                p.wait(eng, "gld")
                for i in range(nt):
                    eng.matmul(pmu[:], onesd[:], ybf[i][:], start=(i == 0), stop=(i == nt - 1))
                p.wait(eng, "act")
                for i in range(nt):
                    call = eng.matmul(pex[:], onesd[:], sq[i][:], start=(i == 0), stop=(i == nt - 1))
                p.inc("mm", call)
            @blk.vector
            def _(eng):
                p.wait(eng, "mm")
                p.seq(eng, "vec", eng.tensor_copy(musb[:], pmu[:]))
                p.seq(eng, "vec", eng.tensor_tensor(va[:], musb[:], musb[:], OP.mult))
                p.seq(eng, "vec", eng.tensor_tensor(va[:], pex[:], va[:], OP.subtract))
                p.seq(eng, "vec", eng.tensor_scalar(va[:], va[:], EPS, None, OP.add))
                call = eng.reciprocal(s_sb[:], va[:])
                p.inc("vec", call)
            @blk.scalar
            def _(eng):
                p.wait(eng, "vec")
                call = eng.activation(s_sb[:], s_sb[:], AF.Sqrt)
                p.inc("act", call)
            @blk.vector
            def _(eng):
                p.wait(eng, "act")
                for i in range(nt):
                    call = eng.tensor_tensor(y32[i][:], ybf[i][:], musb[:], OP.subtract)
                p.seq(eng, "vec", call)
                for i in range(nt):
                    call = eng.tensor_tensor(y32[i][:], y32[i][:], s_sb[:], OP.mult)
                p.inc("vec", call)
            @blk.scalar
            def _(eng):
                p.wait(eng, "vec")
                for i in range(nt):
                    call = eng.activation(ybf[i][:], y32[i][:], AF.Identity,
                                          scale=wcol[:, i:i + 1], bias=bcol[:, i:i + 1])
                p.inc("act", call)
            @blk.vector
            def _(eng):
                p.wait(eng, "act")
                p.wait(eng, "st")
                for i in range(nt):
                    call = eng.tensor_tensor(outb[i][:], ybf[i][:], gsb[i][:], OP.mult)
                p.inc("vec", call)
            @blk.gpsimd
            def _(eng):
                p.wait(eng, "vec")
                for i in range(nt):
                    p.dma(eng, "st", mzin[frange(i, 2048), cs], outb[i][:])
    p.pop()


def res_ln_pass(p, tag, o_src, xres_src, xres_dst, y_dst=None, y_dst32=None):
    nc = p.nc
    p.push()
    nt = 8
    o32 = [p.sbuf(f"{tag}_o32_{i}", (128, TC), F32) for i in range(nt)]
    obf = [p.sbuf(f"{tag}_obf_{i}", (128, TC), BF16) for i in range(nt)]
    x32 = [p.sbuf(f"{tag}_x32_{i}", (128, TC), F32) for i in range(nt)]
    sq = [p.sbuf(f"{tag}_sq_{i}", (128, TC), BF16) for i in range(nt)]
    xnb = [p.sbuf(f"{tag}_xnb_{i}", (128, TC), BF16) for i in range(nt)]
    yb = [p.sbuf(f"{tag}_yb_{i}", (128, TC), BF16) for i in range(nt)]
    y32 = [p.sbuf(f"{tag}_y32_{i}", (128, TC), F32) for i in range(nt)] if y_dst32 is not None else None
    onesd = p.sbuf(f"{tag}_onesd", (128, 128), BF16)
    musb = p.sbuf(f"{tag}_mu", (128, TC), F32)
    va = p.sbuf(f"{tag}_va", (128, TC), F32)
    s_sb = p.sbuf(f"{tag}_s", (128, TC), F32)
    pmu = p.psum(f"{tag}_pmu", (128, TC), F32)
    pex = p.psum(f"{tag}_pex", (128, TC), F32)
    with nc.Block() as blk:
        @blk.gpsimd
        def _(eng):
            p.wait(eng, "st")
            p.dma(eng, "gld", onesd[:], p.dram["odiv1024"][:, :])
    for ci in range(NCHK):
        cs = slice(ci * TC, (ci + 1) * TC)
        with nc.Block() as blk:
            @blk.sync
            def _(eng):
                for i in range(nt):
                    p.dma(eng, "ld", o32[i][:], o_src[frange(i, D), cs])
                    p.dma(eng, "ld", x32[i][:], xres_src[frange(i, D), cs])
            @blk.gpsimd
            def _(eng):
                for i in range(nt):
                    p.dma(eng, "gld", obf[i][:], o_src[frange(i, D), cs])

            def stats(src_tiles, sq_dst):
                @blk.scalar
                def _(eng):
                    p.wait(eng, "vec")
                    p.wait(eng, "aux")
                    for i in range(nt):
                        call = eng.activation(sq_dst[i][:], src_tiles[i][:], AF.Square)
                    p.inc("act", call)
                @blk.tensor
                def _(eng):
                    p.wait(eng, "vec")
                    p.wait(eng, "aux")
                    p.wait(eng, "gld")
                    p.wait(eng, "ld")
                    for i in range(nt):
                        eng.matmul(pmu[:], onesd[:], src_tiles[i][:], start=(i == 0), stop=(i == nt - 1))
                    p.wait(eng, "act")
                    for i in range(nt):
                        call = eng.matmul(pex[:], onesd[:], sq_dst[i][:], start=(i == 0), stop=(i == nt - 1))
                    p.inc("mm", call)
                @blk.vector
                def _(eng):
                    p.wait(eng, "mm")
                    p.seq(eng, "vec", eng.tensor_copy(musb[:], pmu[:]))
                    p.seq(eng, "vec", eng.tensor_tensor(va[:], musb[:], musb[:], OP.mult))
                    p.seq(eng, "vec", eng.tensor_tensor(va[:], pex[:], va[:], OP.subtract))
                    p.seq(eng, "vec", eng.tensor_scalar(va[:], va[:], EPS, None, OP.add))
                    call = eng.reciprocal(s_sb[:], va[:])
                    p.inc("vec", call)
                @blk.scalar
                def _(eng):
                    p.wait(eng, "vec")
                    call = eng.activation(s_sb[:], s_sb[:], AF.Sqrt)
                    p.inc("act", call)
                return p.n("act")

            def norm_tiles(eng, sem, tiles, smark, stm, first):
                p.wait(eng, "act", smark)
                p.wait(eng, "st", stm)
                for i in tiles:
                    eng.tensor_tensor(first[i][:], first[i][:], musb[:], OP.subtract)
                for i in tiles:
                    eng.tensor_tensor(first[i][:], first[i][:], s_sb[:], OP.mult)

            @blk.scalar
            def _(eng):
                p.wait(eng, "gld")
                p.wait(eng, "ld")
            smark1 = stats(obf, sq)
            stm = p.n("st")
            VT, GT = list(range(4)), list(range(4, nt))
            @blk.vector
            def _(eng):
                norm_tiles(eng, "vec", VT, smark1, stm, o32)
                for i in VT:
                    eng.tensor_tensor(x32[i][:], x32[i][:], o32[i][:], OP.add)
                for i in VT:
                    call = eng.tensor_copy(xnb[i][:], x32[i][:])
                p.inc("vec", call)
            @blk.gpsimd
            def _(eng):
                norm_tiles(eng, "aux", GT, smark1, stm, o32)
                for i in GT:
                    eng.tensor_tensor(x32[i][:], x32[i][:], o32[i][:], OP.add)
                for i in GT:
                    call = eng.tensor_copy(xnb[i][:], x32[i][:])
                p.inc("aux", call)
            @blk.gpsimd
            def _(eng):
                p.wait(eng, "vec")
                for i in range(nt):
                    p.dma(eng, "st", xres_dst[frange(i, D), cs], x32[i][:])
            if y_dst is None:
                continue
            smark2 = stats(xnb, sq)
            stm2 = p.n("st")
            @blk.vector
            def _(eng):
                norm_tiles(eng, "vec", VT, smark2, stm2, x32)
                for i in VT:
                    call = eng.tensor_copy(yb[i][:], x32[i][:])
                p.inc("vec", call)
            @blk.gpsimd
            def _(eng):
                norm_tiles(eng, "aux", GT, smark2, stm2, x32)
                for i in GT:
                    call = eng.tensor_copy(yb[i][:], x32[i][:])
                p.inc("aux", call)
            @blk.gpsimd
            def _(eng):
                p.wait(eng, "vec")
                for i in range(nt):
                    p.dma(eng, "st", y_dst[frange(i, D), cs], yb[i][:])
                    if y_dst32 is not None:
                        p.dma(eng, "st", y_dst32[frange(i, D), cs], x32[i][:])
    p.pop()


INPUT_DEFS = [
    ("xT", (D, T), F32),
    ("W_inproj", (D, M_PROJ), F32),
    ("conv_w4", (M_CONVD, 4), F32),
    ("conv_b_col", (M_CONVD, 1), F32),
    ("dt_bias_col", (32, 1), F32),
    ("A_log_row", (1, 32), F32),
    ("D_row", (1, 32), F32),
    ("mln_w_col", (M_INTER, 1), F32),
    ("mln_b_col", (M_INTER, 1), F32),
    ("W_outproj", (M_INTER, D), F32),
] + [(k, v.shape, F32) for k, v in CONSTS.items()] + [
    (f"ffn_wqkv{_s}", (D, 3 * L_INNER), F32) for _s in range(2)] + [
    (f"ffn_qs_col{_s}", (128, 1), F32) for _s in range(2)] + [
    (f"ffn_ks_col{_s}", (128, 1), F32) for _s in range(2)] + [
    (f"ffn_qs_row{_s}", (1, 64), F32) for _s in range(2)] + [
    (f"ffn_ks_row{_s}", (1, 64), F32) for _s in range(2)] + [
    (f"ffn_wo{_s}", (L_INNER, D), F32) for _s in range(2)] + [
    (f"ffn_lng_col{_s}", (D, 1), F32) for _s in range(2)] + [
    (f"ffn_lnb_col{_s}", (D, 1), F32) for _s in range(2)] + [
    (f"ffn_w1_{_s}", (D, 2 * L_HID), F32) for _s in range(2)] + [
    (f"ffn_b1_col{_s}", (2 * L_HID, 1), F32) for _s in range(2)] + [
    (f"ffn_w2_{_s}", (L_HID, D), F32) for _s in range(2)] + [
    (f"ffn_b2_col{_s}", (D, 1), F32) for _s in range(2)] + [
    ("attn_qkv_w", (D, 3 * D), F32),
    ("attn_qkv_w_v", (D, D), F32),
    ("attn_qkv_bq_col", (D, 1), F32),
    ("attn_qkv_bk_col", (D, 1), F32),
    ("attn_bv_col", (D, 1), F32),
    ("attn_out_w", (D, D), F32),
    ("attn_out_b_row", (1, D), F32),
    ("l1_lng_col", (D, 1), F32),
    ("l1_lnb_col", (D, 1), F32),
    ("fc1_q", (D, 4096), I32),
    ("fc1_scales", (8, 4096), F32),
    ("fc1_b_col", (4096, 1), F32),
    ("fc2_q", (4096, D), I32),
    ("fc2_scales", (32, D), F32),
    ("fc2_b_col", (D, 1), F32),
]

SCRATCH_DEFS_B2 = []
for _s in range(2):
    SCRATCH_DEFS_B2 += [
        (f"qkvtok{_s}", (T, 1536), BF16),
        (f"oT{_s}", (L_INNER, T), BF16),
        (f"zmid{_s}", (D, T), F32),
        (f"ln2o{_s}", (D, T), BF16),
        (f"h1_{_s}", (2 * L_HID, T), BF16),
        (f"m1_{_s}", (L_HID, T), BF16),
        (f"zout{_s}", (D, T), F32),
    ]
SCRATCH_DEFS_B2 += [("x2", (D, T), F32), ("y3", (D, T), BF16)]

SCRATCH_DEFS_B34 = [
    ("qT3", (D, T), BF16),
    ("kT3", (D, T), BF16),
    ("vtok3", (T, D), BF16),
    ("qnm", (A_HEADS, T), F32),
    ("oT3", (D, T), BF16),
    ("rrow3", (1, D), F32),
    ("mo3", (D, T), F32),
    ("x3", (D, T), F32),
    ("y4", (D, T), BF16),
    ("y4f", (D, T), F32),
    ("w1d", (D, 4096), BF16),
    ("w2d", (4096, D), BF16),
    ("h4", (D, T), BF16),
    ("y4h", (4096, T), BF16),
    ("ffo", (D, T), F32),
]

SCRATCH_DEFS = [
    ("y1", (D, T), BF16),
    ("gate_s", (M_INTER, T), BF16),
    ("xbcr", (M_CONVD, T), BF16),
    ("dtr", (32, T), F32),
    ("xc", (M_INTER, T), BF16),
    ("bT", (128, T), BF16),
    ("cT", (128, T), BF16),
    ("dts", (32, T), F32),
    ("ytok", (T, M_INTER), BF16),
    ("mzin", (M_INTER, T), BF16),
    ("mo", (D, T), F32),
    ("x1", (D, T), F32),
    ("y2", (D, T), BF16),
    ("y2f", (D, T), F32),
]


def emit_b1(p, upto=99):
    d = p.dram
    ln_pass(p, "ln0", d["xT"], d["y1"], D)
    if upto < 1:
        return
    dense_pass(p, "inproj", d["y1"], D, d["W_inproj"], M_PROJ, sinks=[
        (0, 2048, dict(func=AF.Silu, dst=d["gate_s"])),
        (2048, 4352, dict(func=AF.Identity, dst=d["xbcr"])),
        (4352, 4384, dict(func=AF.Identity, bias=d["dt_bias_col"], dt=F32, dst=d["dtr"])),
    ])
    if upto < 2:
        return
    conv_dt_pass(p)
    if upto < 3:
        return
    ssd_pass(p)
    if upto < 4:
        return
    groupln_gate_pass(p)
    if upto < 5:
        return
    dense_pass(p, "outproj", d["mzin"], M_INTER, d["W_outproj"], D, sinks=[
        (0, D, dict(func=AF.Identity, dt=F32, dst=d["mo"])),
    ])
    if upto < 6:
        return
    res_ln_pass(p, "rl1", d["mo"], d["xT"], d["x1"], d["y2"], d["y2f"])


def emit_b2(p, upto=99):
    d = p.dram
    zin = d["y2f"]
    step = 0
    for s in range(2):
        shifts = [0, 0, 0, 0, 1, 1, 1, 1]
        dense_tok_pass(p, f"qkv{s}", zin, D, d[f"ffn_wqkv{s}"], 1536, d[f"qkvtok{s}"],
                       shifts=shifts, src_f32=True)
        step += 1
        if upto < step:
            return
        lffn_attn_pass(p, s)
        step += 1
        if upto < step:
            return
        dense_pass(p, f"wo{s}", d[f"oT{s}"], L_INNER, d[f"ffn_wo{s}"], D, sinks=[
            (0, D, dict(func=AF.Identity, res=zin, dt=BF16, dst=d[f"zmid{s}"], res_dt=F32)),
        ])
        step += 1
        if upto < step:
            return
        ln_affine_pass(p, f"ln2{s}", d[f"zmid{s}"], d[f"ln2o{s}"],
                       d[f"ffn_lng_col{s}"], d[f"ffn_lnb_col{s}"], shifts=shifts)
        step += 1
        if upto < step:
            return
        w1glu_pass(p, f"w1g{s}", d[f"ln2o{s}"], d[f"ffn_w1_{s}"], d[f"ffn_b1_col{s}"],
                   d[f"m1_{s}"])
        step += 2
        if upto < step:
            return
        dense_pass(p, f"w2{s}", d[f"m1_{s}"], L_HID, d[f"ffn_w2_{s}"], D, sinks=[
            (0, D, dict(func=AF.Identity, bias=d[f"ffn_b2_col{s}"], res=d[f"zmid{s}"], dst=d[f"zout{s}"])),
        ])
        step += 1
        if upto < step:
            return
        zin = d[f"zout{s}"]
    res_ln_pass(p, "rl2", d["zout1"], d["x1"], d["x2"], d["y3"])


def emit_b3(p, upto=99):
    d = p.dram
    s8 = float(1.0 / np.sqrt(8.0))
    dense_pass(p, "qk3", d["y3"], D, d["attn_qkv_w"][:, 0:2048], 2048, sinks=[
        (0, D, dict(func=AF.Identity, bias=d["attn_qkv_bq_col"], bias_scale=s8, scale_col=s8, dst=d["qT3"])),
        (D, 2 * D, dict(func=AF.Identity, bias=d["attn_qkv_bk_col"], bias_scale=s8, scale_col=s8, dst=d["kT3"])),
    ])
    if upto < 2:
        return
    dense_tok_pass(p, "v3", d["y3"], D, d["attn_qkv_w_v"], D, d["vtok3"], src_f32=False)
    if upto < 3:
        return
    l1_bounds_pass(p)
    if upto < 4:
        return
    l1_attn_pass(p)
    if upto < 5:
        return
    l1_rvec_pass(p)
    if upto < 6:
        return
    rcol = d["rrow3"].rearrange("one (f c) -> (one f) c", c=1)
    dense_pass(p, "ao3", d["oT3"], D, d["attn_out_w"], D, sinks=[
        (0, D, dict(func=AF.Identity, bias=rcol, dt=F32, dst=d["mo3"])),
    ])
    if upto < 7:
        return
    res_ln_pass(p, "rl3", d["mo3"], d["x2"], d["x3"], d["y4"], d["y4f"])


def emit_b4(p, upto=99):
    d = p.dram
    dequant_pass(p, "dq1", d["fc1_q"], d["fc1_scales"], d["w1d"], D, 4096)
    dequant_pass(p, "dq2", d["fc2_q"], d["fc2_scales"], d["w2d"], 4096, D)
    if upto < 2:
        return
    ln_affine_pass(p, "ln4", d["y4f"], d["h4"], d["l1_lng_col"], d["l1_lnb_col"])
    if upto < 3:
        return
    dense_pass(p, "fc1", d["h4"], D, d["w1d"], 4096, sinks=[
        (0, 4096, dict(func=AF.Silu, bias=d["fc1_b_col"], dst=d["y4h"])),
    ])
    if upto < 4:
        return
    dense_pass(p, "fc2", d["y4h"], 4096, d["w2d"], D, sinks=[
        (0, D, dict(func=AF.Identity, bias=d["fc2_b_col"], bias_scale=0.5, scale_col=0.5,
                    res=d["y4f"], dst=d["ffo"])),
    ])
    if upto < 5:
        return
    res_ln_pass(p, "rl4", d["ffo"], d["x3"], d["xoutT"])


def build(dbg=(), blocks=(1,), upto=99, in_override=(), passes=None, ssd_chunks=NSC):
    nc = bass.Bass("TRN2", target_bir_lowering=False)
    p = P(nc, dbg)
    p.inov = set(in_override)
    p.ssd_chunks = ssd_chunks
    for name, shape, dt in INPUT_DEFS:
        p.dram[name] = nc.dram_tensor(name, list(shape), dt, kind="ExternalInput").ap()
    for name, shape, dt in SCRATCH_DEFS + SCRATCH_DEFS_B2 + SCRATCH_DEFS_B34:
        p.scratch(name, shape, dt)
    p.dram["xoutT"] = nc.dram_tensor("xoutT", [D, T], F32, kind="ExternalOutput").ap()
    for _s in range(2):
        if f"dbg_qT{_s}" in p.dbg:
            p.scratch(f"dbg_qT{_s}", (512, T), BF16)
            p.scratch(f"dbg_kT{_s}", (512, T), BF16)
        if f"dbg_rt{_s}" in p.dbg:
            p.scratch(f"dbg_rt{_s}", (256, 2048), BF16)
    if passes is not None:
        for fn in passes:
            fn(p)
    else:
        if 1 in blocks:
            emit_b1(p, upto if 2 not in blocks else 99)
        if 2 in blocks:
            emit_b2(p, upto if max(blocks) > 2 else upto)
        if 3 in blocks:
            emit_b3(p, upto if 4 in blocks else upto)
        if 4 in blocks:
            emit_b4(p, upto)
    with nc.Block() as blk:
        @blk.sync
        def _(eng):
            p.wait(eng, "st")
            p.wait(eng, "gld")
            p.wait(eng, "ld")
    while p.scopes:
        p.pop()
    return p


def host_inputs(inputs):
    g = {}
    g["W_inproj"] = np.ascontiguousarray(inputs["L0_attn_in_proj_w"].T)
    g["conv_w4"] = np.ascontiguousarray(inputs["L0_attn_conv_w"].reshape(M_CONVD, 4))
    g["conv_b_col"] = np.ascontiguousarray(inputs["L0_attn_conv_b"].reshape(M_CONVD, 1))
    g["dt_bias_col"] = np.ascontiguousarray(inputs["L0_attn_dt_bias"].reshape(32, 1))
    g["A_log_row"] = np.ascontiguousarray(inputs["L0_attn_A_log"].reshape(1, 32))
    g["D_row"] = np.ascontiguousarray(inputs["L0_attn_D"].reshape(1, 32))
    g["mln_w_col"] = np.ascontiguousarray(inputs["L0_attn_ln_w"].reshape(M_INTER, 1))
    g["mln_b_col"] = np.ascontiguousarray(inputs["L0_attn_ln_b"].reshape(M_INTER, 1))
    g["W_outproj"] = np.ascontiguousarray(inputs["L0_attn_out_proj_w"].T)
    for s in range(2):
        g[f"ffn_wqkv{s}"] = np.ascontiguousarray(inputs["L0_ffn_wqkv"][s])
        g[f"ffn_qs_col{s}"] = np.ascontiguousarray(np.tile(inputs["L0_ffn_q_scale"][s], 2).reshape(128, 1))
        g[f"ffn_ks_col{s}"] = np.ascontiguousarray(np.tile(inputs["L0_ffn_k_scale"][s], 2).reshape(128, 1))
        g[f"ffn_qs_row{s}"] = np.ascontiguousarray(inputs["L0_ffn_q_scale"][s].reshape(1, 64))
        g[f"ffn_ks_row{s}"] = np.ascontiguousarray(inputs["L0_ffn_k_scale"][s].reshape(1, 64))
        g[f"ffn_wo{s}"] = np.ascontiguousarray(inputs["L0_ffn_wo"][s])
        g[f"ffn_lng_col{s}"] = np.ascontiguousarray(inputs["L0_ffn_ln_g"][s].reshape(D, 1))
        g[f"ffn_lnb_col{s}"] = np.ascontiguousarray(inputs["L0_ffn_ln_b"][s].reshape(D, 1))
        perm = []
        for t in range(NPAIR):
            w = 128 if t < NPAIR - 1 else L_HID - 128 * (NPAIR - 1)
            perm += list(range(128 * t, 128 * t + w))
            perm += list(range(L_HID + 128 * t, L_HID + 128 * t + w))
        g[f"ffn_w1_{s}"] = np.ascontiguousarray(inputs["L0_ffn_w1"][s][:, perm])
        g[f"ffn_b1_col{s}"] = np.ascontiguousarray(inputs["L0_ffn_b1"][s][perm].reshape(2 * L_HID, 1))
        g[f"ffn_w2_{s}"] = np.ascontiguousarray(inputs["L0_ffn_w2"][s])
        g[f"ffn_b2_col{s}"] = np.ascontiguousarray(inputs["L0_ffn_b2"][s].reshape(D, 1))
    g["attn_qkv_w"] = np.ascontiguousarray(inputs["L1_attn_qkv_w"])
    g["attn_qkv_w_v"] = np.ascontiguousarray(inputs["L1_attn_qkv_w"][:, 2048:3072])
    g["attn_qkv_bq_col"] = np.ascontiguousarray(inputs["L1_attn_qkv_b"][0:1024].reshape(D, 1))
    g["attn_qkv_bk_col"] = np.ascontiguousarray(inputs["L1_attn_qkv_b"][1024:2048].reshape(D, 1))
    g["attn_bv_col"] = np.ascontiguousarray(inputs["L1_attn_qkv_b"][2048:3072].reshape(D, 1))
    g["attn_out_w"] = np.ascontiguousarray(inputs["L1_attn_out_w"])
    g["attn_out_b_row"] = np.ascontiguousarray(inputs["L1_attn_out_b"].reshape(1, D))
    g["l1_lng_col"] = np.ascontiguousarray(inputs["L1_ffn_ln_gamma"].reshape(D, 1))
    g["l1_lnb_col"] = np.ascontiguousarray(inputs["L1_ffn_ln_beta"].reshape(D, 1))
    g["fc1_q"] = np.ascontiguousarray(inputs["L1_ffn_fc1_q"])
    g["fc1_scales"] = np.ascontiguousarray(inputs["L1_ffn_fc1_scales"])
    g["fc1_b_col"] = np.ascontiguousarray(inputs["L1_ffn_fc1_bias"].reshape(4096, 1))
    g["fc2_q"] = np.ascontiguousarray(inputs["L1_ffn_fc2_q"])
    g["fc2_scales"] = np.ascontiguousarray(inputs["L1_ffn_fc2_scales"])
    g["fc2_b_col"] = np.ascontiguousarray(inputs["L1_ffn_fc2_bias"].reshape(D, 1))
    for k, v in CONSTS.items():
        g[k] = v
    return g


def kernel(**inputs):
    p = build(dbg=(), blocks=(1, 2, 3, 4))
    g = host_inputs(inputs)
    x = inputs["L0_attn_hidden_states"]
    in_maps = []
    for b in range(8):
        m = dict(g)
        m["xT"] = np.ascontiguousarray(x[b].T)
        in_maps.append(m)
    res = run_bass_kernel_spmd(p.nc, in_maps, core_ids=list(range(8)))
    out = np.stack([np.ascontiguousarray(r["xoutT"].T) for r in res.results])
    return out


def dense_tok_pass(p, tag, src, CIN, wdram, FOUT, dst, shifts=None, src_f32=True):
    nc = p.nc
    p.push()
    nct = ftiles(CIN)
    nfc = (FOUT + TC - 1) // TC
    ngrp = (TC // 128) * nfc
    wsb = [p.sbuf(f"{tag}_w{i}", (128, FOUT), BF16) for i in range(nct)]
    xsb = [[p.sbuf(f"{tag}_x{b}_{i}", (128, TC + 4), BF16) for i in range(nct)] for b in range(2)]
    NB = 4
    pb = [p.psum(f"{tag}_pb{i}", (128, TC), F32) for i in range(NB)]
    osb = [p.sbuf(f"{tag}_o{i}", (128, TC), BF16) for i in range(NB)]
    shifts = shifts or [0] * nct
    lxmark = [0] * NCHK
    gldmark0 = [0]
    mm_mark = [0] * NCHK
    evb0 = [0]

    def emit_xload(eng, ci):
        sname = f"lx{ci % 2}"
        for i in range(nct):
            rows = fsz(i, CIN)
            s = shifts[i]
            lo = ci * TC - s
            if lo < 0:
                p.dma(eng, sname, xsb[ci % 2][i][0:rows, 4:4 + TC - s], src[frange(i, CIN), 0:TC - s])
            else:
                p.dma(eng, sname, xsb[ci % 2][i][0:rows, 4:4 + TC], src[frange(i, CIN), lo:lo + TC])
        return p.n(sname)

    with nc.Block() as blk:
        @blk.gpsimd
        def _(eng):
            p.wait(eng, "st")
            for i in range(nct):
                rows = fsz(i, CIN)
                p.dma(eng, "gld", wsb[i][0:rows, :], wdram[i * 128:i * 128 + rows, :])
            gldmark0[0] = p.n("gld")
            if src_f32:
                lxmark[0] = emit_xload(eng, 0)
        @blk.sync
        def _(eng):
            if not src_f32:
                p.wait(eng, "st")
                lxmark[0] = emit_xload(eng, 0)
        @blk.vector
        def _(eng):
            p.wait(eng, "gld")
            for b in range(2):
                for i in range(nct):
                    eng.memset(xsb[b][i][:, 0:4], 0.0)
    with nc.Block() as blk:
        for ci in range(NCHK):
            @blk.sync
            def _(eng):
                if not src_f32 and ci + 1 < NCHK:
                    if ci >= 1:
                        p.wait(eng, "mm", mm_mark[ci - 1])
                    lxmark[ci + 1] = emit_xload(eng, ci + 1)
            @blk.gpsimd
            def _(eng):
                if src_f32 and ci + 1 < NCHK:
                    if ci >= 1:
                        p.wait(eng, "mm", mm_mark[ci - 1])
                    lxmark[ci + 1] = emit_xload(eng, ci + 1)
            @blk.tensor
            def _(eng):
                p.wait(eng, f"lx{ci % 2}", lxmark[ci])
                if ci == 0:
                    p.wait(eng, "gld", gldmark0[0])
                    evb0[0] = p.n("evac")
                g = 0
                for it in range(TC // 128):
                    for fc in range(nfc):
                        gg = ci * ngrp + g
                        w = min(TC, FOUT - fc * TC)
                        bank = pb[g % NB]
                        if gg >= NB:
                            eng.wait_ge(p.sem["evac"][0], evb0[0] + (gg - NB + 1))
                        for c in range(nct):
                            rows = fsz(c, CIN)
                            s = shifts[c]
                            off = 4 if (ci > 0 or s == 0) else 4 - s
                            call = eng.matmul(bank[:, 0:w],
                                              xsb[ci % 2][c][0:rows, off + it * 128:off + (it + 1) * 128],
                                              wsb[c][0:rows, fc * TC:fc * TC + w],
                                              start=(c == 0), stop=(c == nct - 1))
                        p.inc("mm", call)
                        g += 1
                mm_mark[ci] = p.n("mm")
            @blk.scalar
            def _(eng):
                p.wait(eng, "st")
                mmb = p.n("mm") - ngrp
                g = 0
                for it in range(TC // 128):
                    for fc in range(nfc):
                        w = min(TC, FOUT - fc * TC)
                        eng.wait_ge(p.sem["mm"][0], mmb + g + 1)
                        call = eng.activation(osb[g % NB][:, 0:w], pb[g % NB][:, 0:w], AF.Identity)
                        p.inc("evac", call)
                        g += 1
            @blk.gpsimd
            def _(eng):
                evb2 = p.n("evac")
                g = 0
                for it in range(TC // 128):
                    for fc in range(nfc):
                        w = min(TC, FOUT - fc * TC)
                        p.wait(eng, "evac", evb2 - ngrp + g + 1)
                        p.dma(eng, "st", dst[ci * TC + it * 128:ci * TC + (it + 1) * 128, fc * TC:fc * TC + w],
                              osb[g % NB][:, 0:w])
                        g += 1
    p.pop()


def ln_affine_pass(p, tag, src, dst, gamma, beta, shifts=None, CIN=D):
    nc = p.nc
    p.push()
    nt = ftiles(CIN)
    x32 = [p.sbuf(f"{tag}_x32_{i}", (128, TC + 4), F32) for i in range(nt)]
    xbf = [p.sbuf(f"{tag}_xbf_{i}", (128, TC + 4), BF16) for i in range(nt)]
    sq = [p.sbuf(f"{tag}_sq_{i}", (128, TC), BF16) for i in range(nt)]
    onesd = p.sbuf(f"{tag}_onesd", (128, 128), BF16)
    musb = p.sbuf(f"{tag}_mu", (128, TC), F32)
    va = p.sbuf(f"{tag}_va", (128, TC), F32)
    s_sb = p.sbuf(f"{tag}_s", (128, TC), F32)
    gcol = p.sbuf(f"{tag}_g", (128, nt), F32)
    bcol = p.sbuf(f"{tag}_bc", (128, nt), F32)
    outb = [p.sbuf(f"{tag}_o_{i}", (128, TC), BF16) for i in range(nt)]
    pmu = p.psum(f"{tag}_pmu", (128, TC), F32)
    pex = p.psum(f"{tag}_pex", (128, TC), F32)
    shifts = shifts or [0] * nt
    oc = p.dram["odiv2048"] if CIN == 2048 else p.dram["odiv1024"]
    with nc.Block() as blk:
        @blk.gpsimd
        def _(eng):
            p.wait(eng, "st")
            p.dma(eng, "gld", onesd[:], oc[:, :])
            p.wait(eng, "gld")
        @blk.sync
        def _(eng):
            for i in range(nt):
                p.dma(eng, "ld", gcol[:, i:i + 1], gamma[i * 128:(i + 1) * 128, :])
                p.dma(eng, "ld", bcol[:, i:i + 1], beta[i * 128:(i + 1) * 128, :])
            p.wait(eng, "ld")
        @blk.vector
        def _(eng):
            for i in range(nt):
                eng.memset(x32[i][:, 0:4], 0.0)
                eng.memset(xbf[i][:, 0:4], 0.0)
    for ci in range(NCHK):
        with nc.Block() as blk:
            @blk.sync
            def _(eng):
                for i in range(nt):
                    s = shifts[i]
                    lo = ci * TC - s
                    if lo < 0:
                        p.dma(eng, "ld", x32[i][:, 4:4 + TC - s], src[frange(i, CIN), 0:TC - s])
                    else:
                        p.dma(eng, "ld", x32[i][:, 4:4 + TC], src[frange(i, CIN), lo:lo + TC])
            @blk.gpsimd
            def _(eng):
                for i in range(nt):
                    s = shifts[i]
                    lo = ci * TC - s
                    if lo < 0:
                        p.dma(eng, "gld", xbf[i][:, 4:4 + TC - s], src[frange(i, CIN), 0:TC - s])
                    else:
                        p.dma(eng, "gld", xbf[i][:, 4:4 + TC], src[frange(i, CIN), lo:lo + TC])

            def off(i):
                return 4 if (ci > 0 or shifts[i] == 0) else 4 - shifts[i]
            @blk.scalar
            def _(eng):
                p.wait(eng, "gld")
                p.wait(eng, "ld")
                for i in range(nt):
                    call = eng.activation(sq[i][:], xbf[i][:, off(i):off(i) + TC], AF.Square)
                p.inc("act", call)
            @blk.tensor
            def _(eng):
                p.wait(eng, "ld")
                p.wait(eng, "gld")
                for i in range(nt):
                    eng.matmul(pmu[:], onesd[:], xbf[i][:, off(i):off(i) + TC], start=(i == 0), stop=(i == nt - 1))
                p.wait(eng, "act")
                for i in range(nt):
                    call = eng.matmul(pex[:], onesd[:], sq[i][:], start=(i == 0), stop=(i == nt - 1))
                p.inc("mm", call)
            @blk.vector
            def _(eng):
                p.wait(eng, "mm")
                p.seq(eng, "vec", eng.tensor_copy(musb[:], pmu[:]))
                p.seq(eng, "vec", eng.tensor_tensor(va[:], musb[:], musb[:], OP.mult))
                p.seq(eng, "vec", eng.tensor_tensor(va[:], pex[:], va[:], OP.subtract))
                p.seq(eng, "vec", eng.tensor_scalar(va[:], va[:], EPS, None, OP.add))
                call = eng.reciprocal(s_sb[:], va[:])
                p.inc("vec", call)
            @blk.scalar
            def _(eng):
                p.wait(eng, "vec")
                call = eng.activation(s_sb[:], s_sb[:], AF.Sqrt)
                p.inc("act", call)
            @blk.vector
            def _(eng):
                p.wait(eng, "act")
                for i in range(nt):
                    o = off(i)
                    call = eng.tensor_tensor(x32[i][:, o:o + TC], x32[i][:, o:o + TC], musb[:], OP.subtract)
                p.seq(eng, "vec", call)
                for i in range(nt):
                    o = off(i)
                    call = eng.tensor_tensor(x32[i][:, o:o + TC], x32[i][:, o:o + TC], s_sb[:], OP.mult)
                p.inc("vec", call)
            @blk.scalar
            def _(eng):
                p.wait(eng, "vec")
                p.wait(eng, "st")
                for i in range(nt):
                    o = off(i)
                    call = eng.activation(outb[i][:], x32[i][:, o:o + TC], AF.Identity,
                                          scale=gcol[:, i:i + 1], bias=bcol[:, i:i + 1])
                p.inc("act", call)
            @blk.gpsimd
            def _(eng):
                p.wait(eng, "act")
                for i in range(nt):
                    p.dma(eng, "st", dst[frange(i, CIN), ci * TC:(ci + 1) * TC], outb[i][:])
    p.pop()


def glu_pass(p, tag, src, dst, HID):
    nc = p.nc
    p.push()
    a_sb = p.sbuf(f"{tag}_a", (128, T), BF16)
    g_sb = p.sbuf(f"{tag}_g", (128, T), BF16)
    e_sb = p.sbuf(f"{tag}_e", (128, T), BF16)
    m_sb = p.sbuf(f"{tag}_m", (128, T), BF16)
    with nc.Block() as blk:
        @blk.gpsimd
        def _(eng):
            p.wait(eng, "st")
    for j in range(ftiles(HID)):
        rows = fsz(j, HID)
        with nc.Block() as blk:
            @blk.sync
            def _(eng):
                p.dma(eng, "ld", a_sb[0:rows, :], src[j * 128:j * 128 + rows, :])
                p.dma(eng, "ld", g_sb[0:rows, :], src[HID + j * 128:HID + j * 128 + rows, :])
            @blk.scalar
            def _(eng):
                p.wait(eng, "ld")
                call = eng.activation(e_sb[0:rows, :], g_sb[0:rows, :], AF.Gelu)
                p.inc("act", call)
            @blk.vector
            def _(eng):
                p.wait(eng, "act")
                if j > 0:
                    p.wait(eng, "st")
                call = eng.tensor_tensor(m_sb[0:rows, :], a_sb[0:rows, :], e_sb[0:rows, :], OP.mult)
                p.inc("vec", call)
            @blk.gpsimd
            def _(eng):
                p.wait(eng, "vec")
                p.dma(eng, "st", dst[j * 128:j * 128 + rows, :], m_sb[0:rows, :])
    p.pop()


NPAIR = ftiles(L_HID)


def w1glu_pass(p, tag, src, wdram, b1dram, dst):
    nc = p.nc
    p.push()
    nct = ftiles(D)
    FOUT = 2 * L_HID
    wsb = [p.sbuf(f"{tag}_w{i}", (128, FOUT), BF16) for i in range(nct)]
    xsb = [[p.sbuf(f"{tag}_x{b}_{i}", (128, TC), BF16) for i in range(nct)] for b in range(2)]
    pb = [p.psum(f"{tag}_pb{i}", (128, TC), F32) for i in range(4)]
    asb = [p.sbuf(f"{tag}_a{i}", (128, TC), F32) for i in range(2)]
    gsb = [p.sbuf(f"{tag}_g{i}", (128, TC), F32) for i in range(2)]
    msb = [p.sbuf(f"{tag}_m{t}", (128, TC), BF16) for t in range(NPAIR)]
    bsb = p.sbuf(f"{tag}_b", (128, 2 * NPAIR), F32)
    lxmark = [0] * NCHK
    gldmark0 = [0]

    def emit_xload(eng, ci):
        sname = f"lx{ci % 2}"
        for i in range(nct):
            p.dma(eng, sname, xsb[ci % 2][i][:, :], src[frange(i, D), ci * TC:(ci + 1) * TC])
        return p.n(sname)

    def pw(t):
        return 128 if t < NPAIR - 1 else L_HID - 128 * (NPAIR - 1)

    with nc.Block() as blk:
        @blk.gpsimd
        def _(eng):
            p.wait(eng, "st")
            for i in range(nct):
                p.dma(eng, "gld", wsb[i][:, :], wdram[frange(i, D), :])
            for t in range(NPAIR):
                w = pw(t)
                p.dma(eng, "gld", bsb[0:w, 2 * t:2 * t + 1], b1dram[256 * t:256 * t + w, :])
                p.dma(eng, "gld", bsb[0:w, 2 * t + 1:2 * t + 2], b1dram[256 * t + w:256 * t + 2 * w, :])
            gldmark0[0] = p.n("gld")
        @blk.sync
        def _(eng):
            p.wait(eng, "st")
            lxmark[0] = emit_xload(eng, 0)

    mm_mark = [0] * NCHK
    with nc.Block() as blk:
        for ci in range(NCHK):
            @blk.sync
            def _(eng):
                if ci + 1 < NCHK:
                    if ci >= 1:
                        p.wait(eng, "mm", mm_mark[ci - 1])
                    lxmark[ci + 1] = emit_xload(eng, ci + 1)
            @blk.tensor
            def _(eng):
                p.wait(eng, f"lx{ci % 2}", lxmark[ci])
                if ci == 0:
                    p.wait(eng, "gld", gldmark0[0])
                evb = p.n("evac")
                for t in range(NPAIR):
                    w = pw(t)
                    if evb + (t - 1) > 0:
                        eng.wait_ge(p.sem["evac"][0], evb + (t - 1))
                    for c in range(nct):
                        eng.matmul(pb[t % 2][0:w, :], wsb[c][:, 256 * t:256 * t + w],
                                   xsb[ci % 2][c][:, :], start=(c == 0), stop=(c == nct - 1))
                    for c in range(nct):
                        call = eng.matmul(pb[2 + t % 2][0:w, :], wsb[c][:, 256 * t + w:256 * t + 2 * w],
                                          xsb[ci % 2][c][:, :], start=(c == 0), stop=(c == nct - 1))
                    p.inc("mm", call)
                mm_mark[ci] = p.n("mm")
            @blk.scalar
            def _(eng):
                mmb = p.n("mm") - NPAIR
                vcb = p.n("vec")
                for t in range(NPAIR):
                    eng.wait_ge(p.sem["mm"][0], mmb + t + 1)
                    if vcb + (t - 1) > 0:
                        eng.wait_ge(p.sem["vec"][0], vcb + (t - 1))
                    w = pw(t)
                    eng.activation(asb[t % 2][0:w, :], pb[t % 2][0:w, :], AF.Identity,
                                   bias=bsb[0:w, 2 * t:2 * t + 1])
                    call = eng.activation(gsb[t % 2][0:w, :], pb[2 + t % 2][0:w, :], AF.Gelu,
                                          bias=bsb[0:w, 2 * t + 1:2 * t + 2])
                    p.inc("evac", call)
            @blk.vector
            def _(eng):
                p.wait(eng, "st")
                evb = p.n("evac") - NPAIR
                for t in range(NPAIR):
                    eng.wait_ge(p.sem["evac"][0], evb + t + 1)
                    w = pw(t)
                    call = eng.tensor_tensor(msb[t][0:w, :], asb[t % 2][0:w, :],
                                             gsb[t % 2][0:w, :], OP.mult)
                    p.inc("vec", call)
            @blk.gpsimd
            def _(eng):
                vcb = p.n("vec") - NPAIR
                for t in range(NPAIR):
                    eng.wait_ge(p.sem["vec"][0], vcb + t + 1)
                    w = pw(t)
                    p.dma(eng, "st", dst[128 * t:128 * t + w, ci * TC:(ci + 1) * TC], msb[t][0:w, :])
    p.pop()


def lffn_attn_pass(p, s, tag=None):
    tag = tag or f"lat{s}"
    nc = p.nc
    p.push()
    qkvtok = p.dram[f"qkvtok{s}"]
    oT = p.dram[f"oT{s}"]
    qT = [p.sbuf(f"{tag}_qT{i}", (128, T), BF16) for i in range(4)]
    kT = [p.sbuf(f"{tag}_kT{i}", (128, T), BF16) for i in range(4)]
    kTp = [p.sbuf(f"{tag}_kTp{i}", (128, T), BF16) for i in range(4)]
    vaug = p.sbuf(f"{tag}_va", (128, L_HEADS * 32 * 65), BF16)
    rtc = p.sbuf(f"{tag}_rtc", (128, 32 * 128), BF16)
    rts = p.sbuf(f"{tag}_rts", (128, 32 * 128), BF16)
    identb = p.sbuf(f"{tag}_idb", (128, 128), BF16)
    keepj = p.sbuf(f"{tag}_kj", (128, 128), BF16)
    keepi = p.sbuf(f"{tag}_ki", (128, 128), BF16)
    ksp = p.sbuf(f"{tag}_ksp", (128, 1), F32)
    b2c = p.sbuf(f"{tag}_b2c", (128, 1), F32)
    qksbc = p.sbuf(f"{tag}_qksbc", (128, 128), F32)
    sq8c = p.sbuf(f"{tag}_sq8c", (128, 1), F32)
    qsr = p.sbuf(f"{tag}_qsr", (1, 64), F32)
    ksr = p.sbuf(f"{tag}_ksr", (1, 64), F32)
    negm = p.sbuf(f"{tag}_negm", (128, 1), F32)
    ones_row = p.sbuf(f"{tag}_or", (1, 128), F32)
    ones_row64 = p.sbuf(f"{tag}_or64", (1, 64), F32)
    tq = [p.sbuf(f"{tag}_tq{b}", (128, 1024), BF16) for b in range(2)]
    sqf = [p.sbuf(f"{tag}_sqf{b}", (128, 1024), BF16) for b in range(2)]
    qh = [p.sbuf(f"{tag}_qh{b}", (128, 1024), BF16) for b in range(2)]
    t1 = [p.sbuf(f"{tag}_t1{b}", (128, 1024), BF16) for b in range(2)]
    t2 = [p.sbuf(f"{tag}_t2{b}", (128, 1024), BF16) for b in range(2)]
    qr = [p.sbuf(f"{tag}_qr{b}", (128, 1024), BF16) for b in range(2)]
    ssqall = p.sbuf(f"{tag}_ssq", (128, 512), F32)
    invall = p.sbuf(f"{tag}_inv", (128, 512), F32)
    NS = 16
    pslot = p.sbuf(f"{tag}_P", (128, NS * 512), BF16)
    bcsb = [p.sbuf(f"{tag}_bc{b}", (64, 512), F32) for b in range(2)]
    risb = [p.sbuf(f"{tag}_ri{b}", (1, 512), F32) for b in range(2)]
    ob = [p.sbuf(f"{tag}_ob{b}", (64, 512), BF16) for b in range(2)]
    psS = p.psum(f"{tag}_psS", (128, 2048), F32)
    psT = [p.psum(f"{tag}_psT0", (128, 512), BF16)] * 2
    psO = [p.psum(f"{tag}_psO{b}", (65, 512), F32) for b in range(2)]
    psB2 = p.psum(f"{tag}_psB2", (64, 512), F32)

    psS_free = [0] * 4
    psT_free = [0]
    pslot_free = [0] * NS
    psO_free = [0] * 2
    risb_free = [0] * 2
    ob_free = [0] * 2
    psB2_free = [0]

    with nc.Block() as blk:
        st0 = p.n("st")
        @blk.gpsimd
        def _(eng):
            p.wait(eng, "st", st0)
            p.dma(eng, "gld", identb[:], p.dram["ident"][:, :])
            p.dma(eng, "gld", keepj[:], p.dram["keep_jle"][:, :])
            p.dma(eng, "gld", keepi[:], p.dram["keep_ile"][:, :])
            for it in range(32):
                p.dma(eng, "gld", rtc[:, it * 128:it * 128 + 64],
                      p.dram["rot_qc"][it * 128:(it + 1) * 128, :])
                p.dma(eng, "gld", rtc[:, it * 128 + 64:it * 128 + 128],
                      p.dram["rot_kc"][it * 128:(it + 1) * 128, :])
                p.dma(eng, "gld", rts[:, it * 128:it * 128 + 64],
                      p.dram["rot_qs"][it * 128:(it + 1) * 128, :])
                p.dma(eng, "gld", rts[:, it * 128 + 64:it * 128 + 128],
                      p.dram["rot_ks"][it * 128:(it + 1) * 128, :])
        gld0 = p.n("gld")
        @blk.sync
        def _(eng):
            p.wait(eng, "st", st0)
            p.dma(eng, "ld", b2c[:], p.dram["base2_col"][:, :])
            p.dma(eng, "ld", qsr[:], p.dram[f"ffn_qs_row{s}"][:, :])
            p.dma(eng, "ld", ksr[:], p.dram[f"ffn_ks_row{s}"][:, :])
        ld_small = p.n("ld")
        @blk.sync
        def _(eng):
            for h in range(L_HEADS):
                for jt in range(32):
                    p.dma(eng, "ld", vaug[:, h * 2080 + jt * 65:h * 2080 + jt * 65 + 64],
                          qkvtok[jt * 128:(jt + 1) * 128, 1024 + h * 64:1024 + (h + 1) * 64])
        ld_vaug = p.n("ld")
        @blk.vector
        def _(eng):
            eng.memset(ones_row[:], 1.0)
            eng.memset(ones_row64[:], 1.0)
            for h in range(L_HEADS):
                ap1 = bass.AP(vaug[:].tensor, h * 2080 + 64, [[L_HEADS * 2080, 128], [65, 32]])
                eng.memset(ap1, 1.0)
            eng.memset(sq8c[:], SQ8)
            p.wait(eng, "ld", ld_small)
            p.seq(eng, "vec", eng.tensor_scalar(ksp[:], b2c[:], SQ8, None, OP.mult))
            p.seq(eng, "vec", eng.tensor_tensor(qh[0][0:1, 0:64], qsr[:], ksr[:], OP.mult))
            p.seq(eng, "vec", eng.reduce_max(risb[0][0:1, 0:1], qh[0][0:1, 0:64], axis=AXX))
            call = eng.tensor_scalar(risb[0][0:1, 0:1], risb[0][0:1, 0:1], -8.0, None, OP.mult)
            p.inc("vec", call)
        vsetup = p.n("vec")
        @blk.tensor
        def _(eng):
            p.wait(eng, "vec", vsetup)
            eng.matmul(psS[:, 0:1], ones_row[:], risb[0][0:1, 0:1], start=True, stop=True)
            eng.matmul(psS[:, 512:576], ones_row[:], qsr[:], start=True, stop=True)
            call = eng.matmul(psS[:, 1024:1088], ones_row[:], ksr[:], start=True, stop=True)
            p.inc("mm", call)
        mmsetup = p.n("mm")
        @blk.vector
        def _(eng):
            p.wait(eng, "mm", mmsetup)
            eng.tensor_copy(negm[:], psS[:, 0:1])
            eng.tensor_copy(qksbc[:, 0:64], psS[:, 512:576])
            call = eng.tensor_copy(qksbc[:, 64:128], psS[:, 1024:1088])
            p.inc("vec", call)
        vsetup2 = p.n("vec")

        phases = getattr(p, 'lffn_phases', ("p1", "p2", "p3", "attn"))
        tq_ld = [0, 0]
        tq_free = [0, 0]
        for it in range(32 if "p1" in phases else 0):
            itb = it % 2
            @blk.sync
            def _(eng):
                if tq_free[itb]:
                    p.wait(eng, "vec", tq_free[itb])
                p.dma(eng, "ld", tq[itb][:], qkvtok[it * 128:(it + 1) * 128, 0:1024])
            tq_ld[itb] = p.n("ld")
            @blk.vector
            def _(eng):
                p.wait(eng, "ld", tq_ld[itb])
                p.seq(eng, "vec", eng.tensor_tensor(sqf[itb][:], tq[itb][:], tq[itb][:], OP.mult))
                call = eng.tensor_reduce(ssqall[:, it * 16:(it + 1) * 16],
                                         sqf[itb][:].rearrange("p (hs q) -> p hs q", hs=16),
                                         axis=AXX, op=OP.add)
                p.inc("vec", call)
            tq_free[itb] = p.n("vec")
        p1_vec = p.n("vec")

        if "p2" in phases:
            @blk.scalar
            def _(eng):
                p.wait(eng, "vec", p1_vec)
                call = eng.activation(ssqall[:], ssqall[:], AF.Sqrt)
                p.inc("act", call)
            p2_act = p.n("act")
            @blk.vector
            def _(eng):
                p.wait(eng, "act", p2_act)
                p.seq(eng, "vec", eng.tensor_scalar(ssqall[:], ssqall[:], 1e-12, None, OP.max))
                call = eng.reciprocal(invall[:], ssqall[:])
                p.inc("vec", call)
        p2_vec = p.n("vec")

        qr_free = [0, 0]
        for it in range(32 if "p3" in phases else 0):
            itb = it % 2
            @blk.sync
            def _(eng):
                if tq_free[itb]:
                    p.wait(eng, "vec", tq_free[itb])
                p.dma(eng, "ld", tq[itb][:], qkvtok[it * 128:(it + 1) * 128, 0:1024])
            tq_ld[itb] = p.n("ld")
            if "p3t" in phases:
                pass
            else:
              @blk.vector
              def _(eng):
                p.wait(eng, "ld", tq_ld[itb])
                if it == 0:
                    p.wait(eng, "vec", p2_vec)
                    p.wait(eng, "gld", gld0)
                p.seq(eng, "vec", eng.tensor_tensor(
                    qh[itb][:].rearrange("p (hs q) -> p hs q", hs=16),
                    tq[itb][:].rearrange("p (hs q) -> p hs q", hs=16),
                    invall[:, it * 16:(it + 1) * 16].unsqueeze(-1).broadcast_to((128, 16, 64)),
                    OP.mult))
                tq_free[itb] = p.n("vec")
                p.seq(eng, "vec", eng.tensor_tensor(
                    qh[itb][:].rearrange("p (s h q) -> p s h q", s=2, h=8),
                    qh[itb][:].rearrange("p (s h q) -> p s h q", s=2, h=8),
                    qksbc[:].rearrange("p (s q) -> p s q", s=2).unsqueeze(2).broadcast_to((128, 2, 8, 64)),
                    OP.mult))
                p.seq(eng, "vec", eng.tensor_tensor(
                    t1[itb][:].rearrange("p (s h q) -> p s h q", s=2, h=8),
                    qh[itb][:].rearrange("p (s h q) -> p s h q", s=2, h=8),
                    rtc[:, it * 128:(it + 1) * 128].rearrange("p (s q) -> p s q", s=2)
                       .unsqueeze(2).broadcast_to((128, 2, 8, 64)),
                    OP.mult))
                for side in range(2):
                    sl = slice(side * 512, (side + 1) * 512)
                    qh4 = qh[itb][:, sl].rearrange("p (h two q) -> p h two q", h=8, two=2)
                    t24 = t2[itb][:, sl].rearrange("p (h two q) -> p h two q", h=8, two=2)
                    ss_t = rts[:, it * 128 + side * 64:it * 128 + (side + 1) * 64]
                    p.seq(eng, "vec", eng.tensor_tensor(
                        t24[:, :, 0, :], qh4[:, :, 1, :],
                        ss_t[:, 0:32].unsqueeze(1).broadcast_to((128, 8, 32)), OP.mult))
                    p.seq(eng, "vec", eng.tensor_tensor(
                        t24[:, :, 1, :], qh4[:, :, 0, :],
                        ss_t[:, 32:64].unsqueeze(1).broadcast_to((128, 8, 32)), OP.mult))
                if qr_free[itb]:
                    p.wait(eng, "mm", qr_free[itb])
                call = eng.tensor_tensor(qr[itb][:], t1[itb][:], t2[itb][:], OP.add)
                p.inc("vec", call)
            qr_mark = p.n("vec")
            tr_mark = 0
            for side in range(2 if "p3v" not in phases else 0):
                slot = side
                @blk.tensor
                def _(eng):
                    p.wait(eng, "vec", qr_mark)
                    if psT_free[0]:
                        p.wait(eng, "act", psT_free[0])
                    for tt in range(4):
                        call = eng.transpose(
                            psT[slot][:, tt * 128:(tt + 1) * 128],
                            qr[itb][:, side * 512 + tt * 128:side * 512 + (tt + 1) * 128],
                            identb[:])
                    p.inc("mm", call)
                tr_mark = p.n("mm")
                @blk.scalar
                def _(eng):
                    if "p3ne" in phases:
                        return
                    p.wait(eng, "mm", tr_mark)
                    dstt = qT if side == 0 else kT
                    for tt in range(4):
                        call = eng.activation(
                            dstt[tt][:, it * 128:(it + 1) * 128],
                            psT[slot][:, tt * 128:(tt + 1) * 128],
                            AF.Identity, scale=sq8c[:, 0:1])
                        p.inc("act", call)
                    if side == 1:
                        for tt in range(4):
                            call = eng.activation(
                                kTp[tt][:, it * 128:(it + 1) * 128],
                                psT[slot][:, tt * 128:(tt + 1) * 128],
                                AF.Identity, scale=ksp[:, 0:1])
                            p.inc("act", call)
                psT_free[0] = p.n("act")
            qr_free[itb] = tr_mark
        prep_vec = p.n("vec")
        prep_act2 = p.n("act")

        pend = []

        def emit_recip(eng, pd):
            rb = pd["u"] % 2
            p.wait(eng, "mm", max(pd["pv"], risb_free[rb]))
            call = eng.reciprocal(risb[rb][:], psO[pd["o"]][64:65, :])
            p.inc("vec", call)
            pd["rb"] = rb
            pd["recip"] = p.n("vec")

        def emit_bcast(eng, pd):
            p.wait(eng, "vec", max(pd["recip"], psB2_free[0]))
            call = eng.matmul(psB2[:], ones_row64[0:1, 0:64], risb[pd["rb"]][:],
                              start=True, stop=True)
            p.inc("mm", call)
            pd["bcast"] = p.n("mm")
            risb_free[pd["rb"]] = p.n("mm")

        def emit_tail_vec(eng, pd):
            pb = pd["u"] % 2
            p.wait(eng, "mm", pd["bcast"])
            call = eng.tensor_copy(bcsb[pb][:], psB2[:])
            p.inc("vec", call)
            psB2_free[0] = p.n("vec")
            if ob_free[pb]:
                p.wait(eng, "st", ob_free[pb])
            call = eng.tensor_tensor(ob[pb][:], psO[pd["o"]][0:64, :], bcsb[pb][:], OP.mult)
            p.inc("vec", call)
            psO_free[pd["o"]] = p.n("vec")
            pd["obm"] = p.n("vec")

        def emit_tail_store(eng, pd):
            pb = pd["u"] % 2
            p.wait(eng, "vec", pd["obm"])
            p.dma(eng, "st", oT[pd["h"] * 64:(pd["h"] + 1) * 64,
                                pd["w"] * 512:(pd["w"] + 1) * 512], ob[pb][:])
            ob_free[pb] = p.n("st")

        def emit_av(eng, av):
            groups, u, h, o = av["groups"], av["u"], av["h"], av["o"]
            gbase = (u % 2) * 8
            ngr = len(groups)
            for gi, (kind, jt, jc) in enumerate(groups):
                g = gbase + gi
                ilo = jt * 128 if kind == "own" else 0
                ihi = 512 if kind == "own" else (jt + 1) * 128
                vslot = vaug[:, h * 2080 + (jc // 128) * 65:h * 2080 + (jc // 128) * 65 + 65]
                vw = av["mask_mark"][gi]
                if gi == 0 and psO_free[o]:
                    vw = max(vw, psO_free[o])
                p.wait(eng, "vec", vw)
                call = eng.matmul(psO[o][:, ilo:ihi], vslot,
                                  pslot[:, g * 512 + ilo:g * 512 + ihi],
                                  start=(gi == 0), stop=(gi == ngr - 1))
                p.inc("mm", call)
                pslot_free[g] = p.n("mm")
            av["pv"] = p.n("mm")
            pend.append(dict(h=av["h"], w=av["w"], u=u, o=o, pv=av["pv"]))

        avq = []
        for h in range(L_HEADS if "attn" in phases else 0):
            pt = h // 2
            pr = 64 * (h % 2)
            for w in range(8):
                u = h * 8 + w
                groups = []
                for jt in range(4):
                    groups.append(("own", jt, w * 512 + jt * 128))
                if w > 0:
                    for jt in range(4):
                        groups.append(("prev", jt, (w - 1) * 512 + jt * 128))
                groups.sort(key=lambda g: 0 if (g[0] == "own" and g[1] == 0) else 1)
                ngr = len(groups)
                gbase = (u % 2) * 8
                qk_mark = [0] * ngr
                exp_mark = [0] * ngr
                mask_mark = [0] * ngr
                o = u % 2
                @blk.tensor
                def _(eng):
                    if pend and "recip" in pend[0] and "bcast" not in pend[0]:
                        emit_bcast(eng, pend[0])
                    a0 = p.n("act")
                    for gi, (kind, jt, jc) in enumerate(groups):
                        if gi == 4 and avq:
                            emit_av(eng, avq.pop(0))
                        src_t = kT if kind == "own" else kTp
                        ilo = jt * 128 if kind == "own" else 0
                        ihi = 512 if kind == "own" else (jt + 1) * 128
                        slot = gi % 4
                        if u == 0 and gi == 0:
                            p.wait(eng, "ld", ld_vaug)
                            p.wait(eng, "vec", prep_vec)
                            p.wait(eng, "act", prep_act2)
                        if gi >= 4:
                            p.wait(eng, "act", a0 + gi - 3)
                        elif psS_free[slot]:
                            p.wait(eng, "act", psS_free[slot])
                        call = eng.matmul(psS[:, slot * 512 + ilo:slot * 512 + ihi],
                                          src_t[pt][pr:pr + 64, jc:jc + 128],
                                          qT[pt][pr:pr + 64, w * 512 + ilo:w * 512 + ihi],
                                          start=True, stop=True)
                        p.inc("mm", call)
                        qk_mark[gi] = p.n("mm")
                    if avq:
                        emit_av(eng, avq.pop(0))
                @blk.scalar
                def _(eng):
                    for gi, (kind, jt, jc) in enumerate(groups):
                        g = gbase + gi
                        ilo = jt * 128 if kind == "own" else 0
                        ihi = 512 if kind == "own" else (jt + 1) * 128
                        p.wait(eng, "mm", max(qk_mark[gi], pslot_free[g]))
                        call = eng.activation(pslot[:, g * 512 + ilo:g * 512 + ihi],
                                              psS[:, (gi % 4) * 512 + ilo:(gi % 4) * 512 + ihi],
                                              AF.Exp, bias=negm[:, 0:1])
                        p.inc("act", call)
                        exp_mark[gi] = p.n("act")
                        psS_free[gi % 4] = p.n("act")
                @blk.vector
                def _(eng):
                    for gi, (kind, jt, jc) in enumerate(groups):
                        g = gbase + gi
                        blo = jt * 128
                        msk = keepj if kind == "own" else keepi
                        p.wait(eng, "act", exp_mark[gi])
                        call = eng.tensor_tensor(pslot[:, g * 512 + blo:g * 512 + blo + 128],
                                                 pslot[:, g * 512 + blo:g * 512 + blo + 128],
                                                 msk[:], OP.mult)
                        p.inc("vec", call)
                        mask_mark[gi] = p.n("vec")
                    if pend and "recip" not in pend[-1]:
                        emit_recip(eng, pend[-1])
                    if pend and "bcast" in pend[0]:
                        emit_tail_vec(eng, pend[0])
                avq.append(dict(h=h, w=w, u=u, o=o, groups=groups, mask_mark=mask_mark))
                @blk.gpsimd
                def _(eng):
                    if pend and "obm" in pend[0]:
                        emit_tail_store(eng, pend.pop(0))
        if avq:
            @blk.tensor
            def _(eng):
                emit_av(eng, avq.pop(0))
        while pend:
            @blk.vector
            def _(eng):
                if "recip" not in pend[0]:
                    emit_recip(eng, pend[0])
            @blk.tensor
            def _(eng):
                if "bcast" not in pend[0]:
                    emit_bcast(eng, pend[0])
            @blk.vector
            def _(eng):
                emit_tail_vec(eng, pend[0])
            @blk.gpsimd
            def _(eng):
                emit_tail_store(eng, pend.pop(0))
    p.pop()
    return




def lffn_attn_pass_old(p, s, tag=None):
    tag = tag or f"lat{s}"
    nc = p.nc
    p.push()
    qkvtok = p.dram[f"qkvtok{s}"]
    oT = p.dram[f"oT{s}"]
    qT = [p.sbuf(f"{tag}_qT{i}", (128, T), BF16) for i in range(4)]
    kT = [p.sbuf(f"{tag}_kT{i}", (128, T), BF16) for i in range(4)]
    kTp = [p.sbuf(f"{tag}_kTp{i}", (128, T), BF16) for i in range(4)]
    vaug = p.sbuf(f"{tag}_va", (128, L_HEADS * 32 * 65), BF16)
    rt = {nm: p.sbuf(f"{tag}_rt{nm}", (128, 32 * 64), BF16) for nm in ["qc", "qs", "kc", "ks"]}
    identb = p.sbuf(f"{tag}_idb", (128, 128), BF16)
    keepj = p.sbuf(f"{tag}_kj", (128, 128), BF16)
    keepi = p.sbuf(f"{tag}_ki", (128, 128), BF16)
    qs8 = p.sbuf(f"{tag}_qs8", (128, 1), F32)
    ks8 = p.sbuf(f"{tag}_ks8", (128, 1), F32)
    ksp = p.sbuf(f"{tag}_ksp", (128, 1), F32)
    b2c = p.sbuf(f"{tag}_b2c", (128, 1), F32)
    qsbc = p.sbuf(f"{tag}_qsbc", (128, 64), F32)
    ksbc = p.sbuf(f"{tag}_ksbc", (128, 64), F32)
    sq8c = p.sbuf(f"{tag}_sq8c", (128, 1), F32)
    qsr = p.sbuf(f"{tag}_qsr", (1, 64), F32)
    ksr = p.sbuf(f"{tag}_ksr", (1, 64), F32)
    negm = p.sbuf(f"{tag}_negm", (128, 1), F32)
    ones_row = p.sbuf(f"{tag}_or", (1, 128), F32)
    ones_row64 = p.sbuf(f"{tag}_or64", (1, 64), F32)
    tq = p.sbuf(f"{tag}_tq", (128, 1024), BF16)
    sqt = p.sbuf(f"{tag}_sq", (128, 512), BF16)
    ssq = p.sbuf(f"{tag}_ssq", (128, 8), F32)
    inv = p.sbuf(f"{tag}_inv", (128, 8), F32)
    qh = p.sbuf(f"{tag}_qh", (128, 512), BF16)
    t1 = p.sbuf(f"{tag}_t1", (128, 512), BF16)
    t2 = p.sbuf(f"{tag}_t2", (128, 512), BF16)
    qr = p.sbuf(f"{tag}_qr", (128, 512), BF16)
    pslot = p.sbuf(f"{tag}_P", (128, 8 * 512), BF16)
    bcsb = p.sbuf(f"{tag}_bc", (64, 512), F32)
    risb = p.sbuf(f"{tag}_ri", (1, 512), F32)
    ob = p.sbuf(f"{tag}_ob", (64, 512), BF16)
    psS = p.psum(f"{tag}_psS", (128, 2048), F32)
    psO = p.psum(f"{tag}_psO", (65, 512), F32)
    psB2 = p.psum(f"{tag}_psB2", (64, 512), F32)
    psT = p.psum(f"{tag}_psT", (128, 512), BF16)

    with nc.Block() as blk:
        @blk.gpsimd
        def _(eng):
            p.wait(eng, "st")
            p.dma(eng, "gld", identb[:], p.dram["ident"][:, :])
            p.dma(eng, "gld", keepj[:], p.dram["keep_jle"][:, :])
            p.dma(eng, "gld", keepi[:], p.dram["keep_ile"][:, :])
            for nm, key in [("qc", "rot_qc"), ("qs", "rot_qs"), ("kc", "rot_kc"), ("ks", "rot_ks")]:
                for it in range(32):
                    p.dma(eng, "gld", rt[nm][:, it * 64:(it + 1) * 64],
                          p.dram[key][it * 128:(it + 1) * 128, :])
            p.wait(eng, "gld")
        @blk.sync
        def _(eng):
            p.dma(eng, "ld", qs8[:], p.dram[f"ffn_qs_col{s}"][:, :])
            p.dma(eng, "ld", ks8[:], p.dram[f"ffn_ks_col{s}"][:, :])
            p.dma(eng, "ld", b2c[:], p.dram["base2_col"][:, :])
            p.dma(eng, "ld", qsr[:], p.dram[f"ffn_qs_row{s}"][:, :])
            p.dma(eng, "ld", ksr[:], p.dram[f"ffn_ks_row{s}"][:, :])
            for h in range(L_HEADS):
                for jt in range(32):
                    p.dma(eng, "ld", vaug[:, h * 2080 + jt * 65:h * 2080 + jt * 65 + 64],
                          qkvtok[jt * 128:(jt + 1) * 128, 1024 + h * 64:1024 + (h + 1) * 64])
            p.wait(eng, "ld")
        @blk.vector
        def _(eng):
            eng.memset(ones_row[:], 1.0)
            call = eng.memset(ones_row64[:], 1.0)
            p.inc("vec", call)
    with nc.Block() as blk:
        @blk.vector
        def _(eng):
            for h in range(L_HEADS):
                ap1 = bass.AP(vaug[:].tensor, h * 2080 + 64, [[L_HEADS * 2080, 128], [65, 32]])
                call = eng.memset(ap1, 1.0)
            p.inc("vec", call)
            p.seq(eng, "vec", eng.memset(sq8c[:], SQ8))
            p.seq(eng, "vec", eng.tensor_scalar(ksp[:], b2c[:], SQ8, None, OP.mult))
            p.seq(eng, "vec", eng.tensor_tensor(qh[0:1, 0:64], qsr[:], ksr[:], OP.mult))
            p.seq(eng, "vec", eng.reduce_max(risb[0:1, 0:1], qh[0:1, 0:64], axis=AXX))
            call = eng.tensor_scalar(risb[0:1, 0:1], risb[0:1, 0:1], -8.0, None, OP.mult)
            p.inc("vec", call)
        @blk.tensor
        def _(eng):
            p.wait(eng, "vec")
            eng.matmul(psS[:, 0:1], ones_row[:], risb[0:1, 0:1], start=True, stop=True)
            eng.matmul(psS[:, 512:576], ones_row[:], qsr[:], start=True, stop=True)
            call = eng.matmul(psS[:, 1024:1088], ones_row[:], ksr[:], start=True, stop=True)
            p.inc("mm", call)
        @blk.vector
        def _(eng):
            p.wait(eng, "mm")
            eng.tensor_copy(negm[:], psS[:, 0:1])
            eng.tensor_copy(qsbc[:], psS[:, 512:576])
            call = eng.tensor_copy(ksbc[:], psS[:, 1024:1088])
            p.inc("vec", call)

    for it in range(32):
        with nc.Block() as blk:
            @blk.sync
            def _(eng):
                p.dma(eng, "ld", tq[:], qkvtok[it * 128:(it + 1) * 128, 0:1024])
            for side in range(2):
                sl = slice(side * 512, (side + 1) * 512)
                @blk.scalar
                def _(eng, sl=sl):
                    p.wait(eng, "ld")
                    p.wait(eng, "vec")
                    call = eng.activation(sqt[:], tq[:, sl], AF.Square)
                    p.inc("act", call)
                @blk.vector
                def _(eng, sl=sl):
                    p.wait(eng, "act")
                    call = eng.tensor_reduce(ssq[:], sqt[:].rearrange("p (h q) -> p h q", h=8),
                                             axis=AXX, op=OP.add)
                    p.inc("vec", call)
                @blk.scalar
                def _(eng):
                    p.wait(eng, "vec")
                    call = eng.activation(ssq[:], ssq[:], AF.Sqrt)
                    p.inc("act", call)
                @blk.vector
                def _(eng, sl=sl, side=side, it=it):
                    p.wait(eng, "act")
                    p.seq(eng, "vec", eng.tensor_scalar(ssq[:], ssq[:], 1e-12, None, OP.max))
                    p.seq(eng, "vec", eng.reciprocal(inv[:], ssq[:]))
                    p.seq(eng, "vec", eng.tensor_tensor(
                        qh[:].rearrange("p (h q) -> p h q", h=8),
                        tq[:, sl].rearrange("p (h q) -> p h q", h=8),
                        inv[:].unsqueeze(-1).broadcast_to((128, 8, 64)), OP.mult))
                    p.seq(eng, "vec", eng.tensor_tensor(
                        qh[:].rearrange("p (h q) -> p h q", h=8),
                        qh[:].rearrange("p (h q) -> p h q", h=8),
                        (qsbc if side == 0 else ksbc)[:].unsqueeze(1).broadcast_to((128, 8, 64)), OP.mult))
                    cs_t = rt["qc" if side == 0 else "kc"][:, it * 64:(it + 1) * 64]
                    ss_t = rt["qs" if side == 0 else "ks"][:, it * 64:(it + 1) * 64]
                    p.seq(eng, "vec", eng.tensor_tensor(
                        t1[:].rearrange("p (h q) -> p h q", h=8),
                        qh[:].rearrange("p (h q) -> p h q", h=8),
                        cs_t.unsqueeze(1).broadcast_to((128, 8, 64)), OP.mult))
                    qh4 = qh[:].rearrange("p (h two q) -> p h two q", h=8, two=2)
                    t24 = t2[:].rearrange("p (h two q) -> p h two q", h=8, two=2)
                    p.seq(eng, "vec", eng.tensor_tensor(
                        t24[:, :, 0, :], qh4[:, :, 1, :],
                        ss_t[:, 0:32].unsqueeze(1).broadcast_to((128, 8, 32)), OP.mult))
                    p.seq(eng, "vec", eng.tensor_tensor(
                        t24[:, :, 1, :], qh4[:, :, 0, :],
                        ss_t[:, 32:64].unsqueeze(1).broadcast_to((128, 8, 32)), OP.mult))
                    call = eng.tensor_tensor(qr[:], t1[:], t2[:], OP.add)
                    p.inc("vec", call)
                mmq = p.n("mm")
                @blk.tensor
                def _(eng):
                    p.wait(eng, "vec")
                    p.wait(eng, "act")
                    for tt in range(4):
                        call = eng.transpose(psT[:, tt * 128:(tt + 1) * 128], qr[:, tt * 128:(tt + 1) * 128], identb[:])
                    p.inc("mm", call)
                @blk.scalar
                def _(eng, side=side, it=it, mmq=mmq):
                    p.wait(eng, "mm", mmq + 1)
                    dstt = qT if side == 0 else kT
                    scl = sq8c
                    for tt in range(4):
                        call = eng.activation(dstt[tt][:, it * 128:(it + 1) * 128],
                                              psT[:, tt * 128:(tt + 1) * 128], AF.Identity,
                                              scale=scl[:, 0:1])
                        p.inc("act", call)
                    if side == 1:
                        for tt in range(4):
                            call = eng.activation(kTp[tt][:, it * 128:(it + 1) * 128],
                                                  psT[:, tt * 128:(tt + 1) * 128], AF.Identity,
                                                  scale=ksp[:, 0:1])
                            p.inc("act", call)

    if f"dbg_rt{s}" in p.dram:
        with nc.Block() as blk:
            @blk.gpsimd
            def _(eng):
                p.wait(eng, "gld")
                p.dma(eng, "st", p.dram[f"dbg_rt{s}"][0:128, :], rt["qs"][:])
                p.dma(eng, "st", p.dram[f"dbg_rt{s}"][128:256, :], rt["qc"][:])
    if f"dbg_qT{s}" in p.dram:
        with nc.Block() as blk:
            @blk.gpsimd
            def _(eng):
                p.wait(eng, "act")
                for tt in range(4):
                    p.dma(eng, "st", p.dram[f"dbg_qT{s}"][tt * 128:(tt + 1) * 128, :], qT[tt][:])
                    p.dma(eng, "st", p.dram[f"dbg_kT{s}"][tt * 128:(tt + 1) * 128, :], kT[tt][:])

    for h in range(L_HEADS):
        pt = h // 2
        pr = 64 * (h % 2)
        for w in range(8):
            groups = []
            for jt in range(4):
                groups.append(("own", jt, w * 512 + jt * 128, jt * 128, 512))
            if w > 0:
                for jt in range(4):
                    groups.append(("prev", jt, (w - 1) * 512 + jt * 128, 0, (jt + 1) * 128))
            groups.sort(key=lambda g: 0 if (g[0] == "own" and g[1] == 0) else 1)
            with nc.Block() as blk:
                mmS = p.n("mm")
                @blk.tensor
                def _(eng):
                    p.wait(eng, "act")
                    p.wait(eng, "vec")
                    for gi, (kind, jt, jc, ilo, ihi) in enumerate(groups):
                        src_t = kT if kind == "own" else kTp
                        if gi >= 4:
                            eng.wait_ge(p.sem["act"][0], p.n("act") + gi - 4 + 1)
                        call = eng.matmul(psS[:, (gi % 4) * 512 + ilo:(gi % 4) * 512 + ihi],
                                          src_t[pt][pr:pr + 64, jc:jc + 128],
                                          qT[pt][pr:pr + 64, w * 512 + ilo:w * 512 + ihi],
                                          start=True, stop=True)
                        p.inc("mm", call)
                aS = p.n("act")
                @blk.scalar
                def _(eng):
                    for gi, (kind, jt, jc, ilo, ihi) in enumerate(groups):
                        eng.wait_ge(p.sem["mm"][0], mmS + gi + 1)
                        call = eng.activation(pslot[:, gi * 512 + ilo:gi * 512 + ihi],
                                              psS[:, (gi % 4) * 512 + ilo:(gi % 4) * 512 + ihi],
                                              AF.Exp, bias=negm[:, 0:1])
                        p.inc("act", call)
                vS = p.n("vec")
                @blk.vector
                def _(eng):
                    for gi, (kind, jt, jc, ilo, ihi) in enumerate(groups):
                        eng.wait_ge(p.sem["act"][0], aS + gi + 1)
                        if kind == "own":
                            msk = keepj
                            blo = jt * 128
                        else:
                            msk = keepi
                            blo = jt * 128
                        call = eng.tensor_tensor(pslot[:, gi * 512 + blo:gi * 512 + blo + 128],
                                                 pslot[:, gi * 512 + blo:gi * 512 + blo + 128],
                                                 msk[:], OP.mult)
                        p.inc("vec", call)
                mmPV = p.n("mm")
                @blk.tensor
                def _(eng):
                    for gi, (kind, jt, jc, ilo, ihi) in enumerate(groups):
                        eng.wait_ge(p.sem["vec"][0], vS + gi + 1)
                        vslot = vaug[:, h * 2080 + (jc // 128) * 65:h * 2080 + (jc // 128) * 65 + 65]
                        call = eng.matmul(psO[:, ilo:ihi], vslot,
                                          pslot[:, gi * 512 + ilo:gi * 512 + ihi],
                                          start=(gi == 0), stop=(gi == len(groups) - 1))
                        p.inc("mm", call)
                vN = p.n("vec")
                @blk.vector
                def _(eng):
                    p.wait(eng, "mm")
                    call = eng.reciprocal(risb[:], psO[64:65, :])
                    p.inc("vec", call)
                mmB = p.n("mm")
                @blk.tensor
                def _(eng):
                    p.wait(eng, "vec", vN + 1)
                    call = eng.matmul(psB2[:], ones_row[0:1, 0:64], risb[:], start=True, stop=True)
                    p.inc("mm", call)
                @blk.vector
                def _(eng):
                    p.wait(eng, "mm", mmB + 1)
                    p.seq(eng, "vec", eng.tensor_copy(bcsb[:], psB2[:]))
                    p.wait(eng, "st")
                    call = eng.tensor_tensor(ob[:], psO[0:64, :], bcsb[:], OP.mult)
                    p.inc("vec", call)
                @blk.gpsimd
                def _(eng):
                    p.wait(eng, "vec")
                    p.dma(eng, "st", oT[h * 64:(h + 1) * 64, w * 512:(w + 1) * 512], ob[:])
    p.pop()


def l1_bounds_pass(p, tag="bnd"):
    nc = p.nc
    p.push()
    qT3, kT3, qnm = p.dram["qT3"], p.dram["kT3"], p.dram["qnm"]
    xb = [[p.sbuf(f"{tag}_x{b}_{i}", (128, TC), BF16) for i in range(8)] for b in range(2)]
    sq = [[p.sbuf(f"{tag}_sq{b}_{i}", (128, TC), BF16) for i in range(8)] for b in range(2)]
    ind = p.sbuf(f"{tag}_ind", (128, 8 * 16), BF16)
    rown = p.sbuf(f"{tag}_rown", (16, 1), F32)
    kmax = p.sbuf(f"{tag}_kmax", (16, 1), F32)
    stg = [p.sbuf(f"{tag}_stg{b}", (16, TC), F32) for b in range(2)]
    psR = [p.psum(f"{tag}_psR{b}", (16, TC), F32) for b in range(2)]

    xb_free = [0, 0]
    psR_free = [None, None]
    st0 = p.n("st")
    with nc.Block() as blk:
        @blk.gpsimd
        def _(eng):
            p.wait(eng, "st", st0)
            for i in range(8):
                p.dma(eng, "gld", ind[:, i * 16:(i + 1) * 16],
                      p.dram["head_ind"][i * 128:(i + 1) * 128, :])
        gld0 = p.n("gld")
        @blk.vector
        def _(eng):
            call = eng.memset(kmax[:], 0.0)
            p.inc("vec", call)
        v0 = p.n("vec")

        for phase in range(2):
            src = kT3 if phase == 0 else qT3
            for ci in range(NCHK):
                u = phase * NCHK + ci
                b = u % 2
                cs = slice(ci * TC, (ci + 1) * TC)
                @blk.sync
                def _(eng):
                    if u == 0:
                        p.wait(eng, "st", st0)
                    if xb_free[b]:
                        p.wait(eng, "mm", xb_free[b])
                    for i in range(8):
                        p.dma(eng, "ld", xb[b][i][:], src[frange(i, D), cs])
                ld_mark = p.n("ld")
                @blk.vector
                def _(eng):
                    p.wait(eng, "ld", ld_mark)
                    for i in range(8):
                        call = eng.tensor_tensor(sq[b][i][:], xb[b][i][:], xb[b][i][:], OP.mult)
                    p.inc("vec", call)
                sq_mark = p.n("vec")
                @blk.tensor
                def _(eng):
                    if u == 0:
                        p.wait(eng, "gld", gld0)
                    p.wait(eng, "vec", sq_mark)
                    if psR_free[b] is not None:
                        p.wait(eng, psR_free[b][0], psR_free[b][1])
                    for i in range(8):
                        call = eng.matmul(psR[b][:], ind[:, i * 16:(i + 1) * 16], sq[b][i][:],
                                          start=(i == 0), stop=(i == 7))
                    p.inc("mm", call)
                mm_mark = p.n("mm")
                xb_free[b] = mm_mark
                if phase == 0:
                    @blk.vector
                    def _(eng):
                        p.wait(eng, "mm", mm_mark)
                        if u == 0:
                            p.wait(eng, "vec", v0)
                        p.seq(eng, "vec", eng.reduce_max(rown[:], psR[b][:], axis=AXX))
                        call = eng.tensor_tensor(kmax[:], kmax[:], rown[:], OP.max)
                        p.inc("vec", call)
                    psR_free[b] = ("vec", p.n("vec"))
                else:
                    @blk.scalar
                    def _(eng):
                        p.wait(eng, "mm", mm_mark)
                        if u == NCHK:
                            p.wait(eng, "vec", kscale_mark[0])
                        if u >= NCHK + 2:
                            p.wait(eng, "st", stg_free[b])
                        call = eng.activation(stg[b][:], psR[b][:], AF.Identity,
                                              scale=-0.5, bias=kmax[:, 0:1])
                        p.inc("act", call)
                    psR_free[b] = ("act", p.n("act"))
                    act_mark = p.n("act")
                    @blk.gpsimd
                    def _(eng):
                        p.wait(eng, "act", act_mark)
                        p.dma(eng, "st", qnm[:, cs], stg[b][:])
                    stg_free[b] = p.n("st")
            if phase == 0:
                kscale_mark = [0]
                stg_free = [0, 0]
                @blk.vector
                def _(eng):
                    p.wait(eng, "vec")
                    call = eng.tensor_scalar(kmax[:], kmax[:], -0.5, None, OP.mult)
                    p.inc("vec", call)
                kscale_mark[0] = p.n("vec")
    p.pop()

def l1_attn_pass(p, tag="l1a"):
    nc = p.nc
    p.push()
    qT3, kT3, vtok3, qnm, oT3 = (p.dram[k] for k in ["qT3", "kT3", "vtok3", "qnm", "oT3"])
    kaug = [p.sbuf(f"{tag}_ka{b}", (65, T), BF16) for b in range(2)]
    qaug = [p.sbuf(f"{tag}_qa{b}", (65, T), BF16) for b in range(2)]
    vaug = [p.sbuf(f"{tag}_va{b}", (128, 32 * 65), BF16) for b in range(2)]
    keepj = p.sbuf(f"{tag}_kj", (128, 128), BF16)
    ones_row64 = p.sbuf(f"{tag}_or64", (1, 64), F32)
    NS = 64
    pslot = p.sbuf(f"{tag}_P", (128, NS * 512), BF16)
    gring = [0]
    bcsb = [p.sbuf(f"{tag}_bc{b}", (64, 512), F32) for b in range(2)]
    risb = [p.sbuf(f"{tag}_ri{b}", (1, 512), F32) for b in range(2)]
    ob = [p.sbuf(f"{tag}_ob{b}", (64, 512), BF16) for b in range(2)]
    psS = p.psum(f"{tag}_psS", (128, 2048), F32)
    psO = [p.psum(f"{tag}_psO{b}", (65, 512), F32) for b in range(3)]
    psB2 = p.psum(f"{tag}_psB2", (64, 512), F32)

    psS_free = [0] * 4
    pslot_free = [0] * NS
    psO_free = [0] * 3
    risb_free = [0] * 2
    ob_free = [0] * 2
    psB2_free = [0]
    head_mm_done = [0] * A_HEADS
    ld_mark = [0] * A_HEADS
    gld_mark = [0] * A_HEADS

    with nc.Block() as blk:
        @blk.gpsimd
        def _(eng):
            p.wait(eng, "st")
            p.dma(eng, "gld", keepj[:], p.dram["keep_jle"][:, :])
        @blk.vector
        def _(eng):
            p.wait(eng, "gld")
            eng.memset(ones_row64[:], 1.0)
            for b in range(2):
                eng.memset(kaug[b][64:65, :], 1.0)
                ap1 = bass.AP(vaug[b][:].tensor, 64, [[32 * 65, 128], [65, 32]])
                call = eng.memset(ap1, 1.0)
            p.inc("vec", call)

        setup_vec = p.n("vec")
        st0 = p.n("st")
        pend = []
        sring = [0]

        def emit_recip(eng, pd):
            rb = pd["u"] % 2
            p.wait(eng, "mm", max(pd["pv"], risb_free[rb]))
            call = eng.reciprocal(risb[rb][:], psO[pd["o"]][64:65, :])
            p.inc("vec", call)
            pd["rb"] = rb
            pd["recip"] = p.n("vec")

        def emit_bcast(eng, pd):
            p.wait(eng, "vec", max(pd["recip"], psB2_free[0]))
            call = eng.matmul(psB2[:], ones_row64[0:1, 0:64], risb[pd["rb"]][:],
                              start=True, stop=True)
            p.inc("mm", call)
            pd["bcast"] = p.n("mm")
            risb_free[pd["rb"]] = p.n("mm")

        def emit_tail_vec(eng, pd):
            pb = pd["u"] % 2
            p.wait(eng, "mm", pd["bcast"])
            call = eng.tensor_copy(bcsb[pb][:], psB2[:])
            p.inc("vec", call)
            psB2_free[0] = p.n("vec")
            if ob_free[pb]:
                p.wait(eng, "st", ob_free[pb])
            call = eng.tensor_tensor(ob[pb][:], psO[pd["o"]][0:64, :], bcsb[pb][:], OP.mult)
            p.inc("vec", call)
            psO_free[pd["o"]] = p.n("vec")
            pd["obm"] = p.n("vec")

        def emit_tail_store(eng, pd):
            pb = pd["u"] % 2
            p.wait(eng, "vec", pd["obm"])
            p.dma(eng, "st", oT3[pd["h"] * 64:(pd["h"] + 1) * 64,
                                 pd["ic"] * 512:(pd["ic"] + 1) * 512], ob[pb][:])
            ob_free[pb] = p.n("st")

        def emit_av(eng, av):
            h, ic, t, o = av["h"], av["ic"], av["u"], av["o"]
            hb = h % 2
            ilo0 = ic * 512
            jts, gslots = av["jts"], av["gslots"]
            ngr = len(jts)
            for gi, jt in enumerate(jts):
                g = gslots[gi]
                ilo = max(0, jt * 128 - ilo0)
                p.wait(eng, "act", av["exp_mark"][gi])
                vw = av["mask_mark"].get(gi, 0)
                if gi == 0:
                    vw = max(vw, psO_free[o])
                if vw:
                    p.wait(eng, "vec", vw)
                call = eng.matmul(psO[o][:, ilo:512], vaug[hb][:, jt * 65:(jt + 1) * 65],
                                  pslot[:, g * 512 + ilo:g * 512 + 512],
                                  start=(gi == 0), stop=(gi == ngr - 1))
                p.inc("mm", call)
                pslot_free[g] = p.n("mm")
            head_mm_done[h] = p.n("mm")
            pend.append(dict(h=h, ic=ic, u=t, o=o, pv=p.n("mm")))

        avq = []
        for h in range(A_HEADS):
            hb = h % 2
            @blk.sync
            def _(eng):
                if h == 0:
                    p.wait(eng, "st", st0)
                if h >= 2:
                    p.wait(eng, "mm", head_mm_done[h - 2])
                p.dma(eng, "ld", kaug[hb][0:64, :], kT3[h * 64:(h + 1) * 64, :])
                p.dma(eng, "ld", qaug[hb][0:64, :], qT3[h * 64:(h + 1) * 64, :])
                for jt in range(32):
                    p.dma(eng, "ld", vaug[hb][:, jt * 65:jt * 65 + 64],
                          vtok3[jt * 128:(jt + 1) * 128, h * 64:(h + 1) * 64])
            ld_mark[h] = p.n("ld")
            @blk.gpsimd
            def _(eng):
                if h >= 2:
                    p.wait(eng, "mm", head_mm_done[h - 2])
                p.dma(eng, "gld", qaug[hb][64:65, :], qnm[h:h + 1, :])
            gld_mark[h] = p.n("gld")

            for ic in range(8):
                t = h * 8 + ic
                ilo0 = ic * 512
                jts = list(range(4 * ic + 4))
                ngr = len(jts)
                gslots = [(gring[0] + gi) % NS for gi in range(ngr)]
                gring[0] = (gring[0] + ngr) % NS
                sslots = [(sring[0] + gi) % 4 for gi in range(ngr)]
                sring[0] = (sring[0] + ngr) % 4
                qk_mark = [0] * ngr
                exp_mark = [0] * ngr
                mask_mark = {}
                o = t % 3
                @blk.tensor
                def _(eng):
                    if pend and "recip" in pend[0] and "bcast" not in pend[0]:
                        emit_bcast(eng, pend[0])
                    if ic == 0:
                        p.wait(eng, "ld", ld_mark[h])
                        p.wait(eng, "gld", gld_mark[h])
                    if t == 0:
                        p.wait(eng, "vec", setup_vec)
                    a0 = p.n("act")
                    for gi, jt in enumerate(jts):
                        if gi == 4 and avq:
                            emit_av(eng, avq.pop(0))
                        s = sslots[gi]
                        if gi >= 4:
                            p.wait(eng, "act", a0 + gi - 3)
                        elif psS_free[s]:
                            p.wait(eng, "act", psS_free[s])
                        ilo = max(0, jt * 128 - ilo0)
                        call = eng.matmul(psS[:, s * 512 + ilo:s * 512 + 512],
                                          kaug[hb][:, jt * 128:(jt + 1) * 128],
                                          qaug[hb][:, ilo0 + ilo:ilo0 + 512],
                                          start=True, stop=True)
                        p.inc("mm", call)
                        qk_mark[gi] = p.n("mm")
                    if avq:
                        emit_av(eng, avq.pop(0))
                @blk.scalar
                def _(eng):
                    for gi, jt in enumerate(jts):
                        g = gslots[gi]
                        s = sslots[gi]
                        ilo = max(0, jt * 128 - ilo0)
                        p.wait(eng, "mm", max(qk_mark[gi], pslot_free[g]))
                        call = eng.activation(pslot[:, g * 512 + ilo:g * 512 + 512],
                                              psS[:, s * 512 + ilo:s * 512 + 512],
                                              AF.Exp)
                        p.inc("act", call)
                        exp_mark[gi] = p.n("act")
                        psS_free[s] = p.n("act")
                @blk.vector
                def _(eng):
                    for gi, jt in enumerate(jts):
                        if jt * 128 < ilo0:
                            continue
                        g = gslots[gi]
                        ilo = jt * 128 - ilo0
                        p.wait(eng, "act", exp_mark[gi])
                        call = eng.tensor_tensor(pslot[:, g * 512 + ilo:g * 512 + ilo + 128],
                                                 pslot[:, g * 512 + ilo:g * 512 + ilo + 128],
                                                 keepj[:], OP.mult)
                        p.inc("vec", call)
                        mask_mark[gi] = p.n("vec")
                    if pend and "recip" not in pend[-1]:
                        emit_recip(eng, pend[-1])
                    if pend and "bcast" in pend[0]:
                        emit_tail_vec(eng, pend[0])
                avq.append(dict(h=h, ic=ic, u=t, o=o, jts=jts, gslots=gslots,
                                exp_mark=exp_mark, mask_mark=mask_mark))
                @blk.gpsimd
                def _(eng):
                    if pend and "obm" in pend[0]:
                        emit_tail_store(eng, pend.pop(0))
        if avq:
            @blk.tensor
            def _(eng):
                emit_av(eng, avq.pop(0))
        while pend:
            @blk.vector
            def _(eng):
                if "recip" not in pend[0]:
                    emit_recip(eng, pend[0])
            @blk.tensor
            def _(eng):
                if "bcast" not in pend[0]:
                    emit_bcast(eng, pend[0])
            @blk.vector
            def _(eng):
                emit_tail_vec(eng, pend[0])
            @blk.gpsimd
            def _(eng):
                emit_tail_store(eng, pend.pop(0))
    p.pop()


def dequant_pass(p, tag, qdram, sdram, dst, CIN, FOUT):
    nc = p.nc
    p.push()
    qi = p.sbuf(f"{tag}_qi", (128, TC), I32)
    tf = p.sbuf(f"{tag}_tf", (128, TC), F32)
    srow = p.sbuf(f"{tag}_sr", (1, TC), F32)
    wout = p.sbuf(f"{tag}_wo", (128, TC), BF16)
    n128 = p.sbuf(f"{tag}_n128", (128, 1), F32)
    ones_row = p.sbuf(f"{tag}_or", (1, 128), F32)
    psB = p.psum(f"{tag}_psB", (128, TC), F32)
    with nc.Block() as blk:
        @blk.gpsimd
        def _(eng):
            p.wait(eng, "st")
        @blk.vector
        def _(eng):
            eng.memset(n128[:], -128.0)
            call = eng.memset(ones_row[:], 1.0)
            p.inc("vec", call)
    for g in range(ftiles(CIN)):
        for fc in range(FOUT // TC):
            with nc.Block() as blk:
                @blk.sync
                def _(eng):
                    p.dma(eng, "ld", qi[:], qdram[g * 128:(g + 1) * 128, fc * TC:(fc + 1) * TC])
                    p.dma(eng, "ld", srow[:], sdram[g:g + 1, fc * TC:(fc + 1) * TC])
                @blk.tensor
                def _(eng):
                    p.wait(eng, "ld")
                    p.wait(eng, "vec")
                    call = eng.matmul(psB[:], ones_row[:], srow[:], start=True, stop=True)
                    p.inc("mm", call)
                @blk.scalar
                def _(eng):
                    p.wait(eng, "ld")
                    call = eng.activation(tf[:], qi[:], AF.Identity, bias=n128[:, 0:1])
                    p.inc("act", call)
                @blk.vector
                def _(eng):
                    p.wait(eng, "mm")
                    p.wait(eng, "act")
                    p.wait(eng, "st")
                    call = eng.tensor_tensor(wout[:], tf[:], psB[:], OP.mult)
                    p.inc("vec", call)
                @blk.gpsimd
                def _(eng):
                    p.wait(eng, "vec")
                    p.dma(eng, "st", dst[g * 128:(g + 1) * 128, fc * TC:(fc + 1) * TC], wout[:])
    p.pop()


def l1_rvec_pass(p, tag="rv"):
    nc = p.nc
    p.push()
    bv = p.sbuf(f"{tag}_bv", (128, 8), BF16)
    wrow = p.sbuf(f"{tag}_w", (128, D), BF16)
    ob_row = p.sbuf(f"{tag}_obr", (1, D), F32)
    rrow = p.sbuf(f"{tag}_r", (1, D), F32)
    psR = p.psum(f"{tag}_psR", (1, D), F32)
    with nc.Block() as blk:
        @blk.gpsimd
        def _(eng):
            p.wait(eng, "st")
            for c in range(8):
                p.dma(eng, "gld", bv[:, c:c + 1], p.dram["attn_bv_col"][c * 128:(c + 1) * 128, :])
            p.wait(eng, "gld")
        @blk.sync
        def _(eng):
            p.dma(eng, "ld", ob_row[:], p.dram["attn_out_b_row"][:, :])
            p.wait(eng, "ld")
    for c in range(8):
        with nc.Block() as blk:
            @blk.gpsimd
            def _(eng):
                p.dma(eng, "gld", wrow[:], p.dram["attn_out_w"][c * 128:(c + 1) * 128, :])
                p.wait(eng, "gld")
            @blk.tensor
            def _(eng):
                p.wait(eng, "gld")
                for half in range(2):
                    call = eng.matmul(psR[:, half * 512:(half + 1) * 512], bv[:, c:c + 1],
                                      wrow[:, half * 512:(half + 1) * 512],
                                      start=(c == 0), stop=(c == 7))
                p.inc("mm", call)
    with nc.Block() as blk:
        @blk.vector
        def _(eng):
            p.wait(eng, "mm")
            p.seq(eng, "vec", eng.tensor_copy(rrow[:], psR[:]))
            call = eng.tensor_tensor(rrow[:], rrow[:], ob_row[:], OP.add)
            p.inc("vec", call)
        @blk.gpsimd
        def _(eng):
            p.wait(eng, "vec")
            p.dma(eng, "st", p.dram["rrow3"][:, :], rrow[:])
    p.pop()
